# version 02
# baseline (speedup 1.0000x reference)
import math
import numpy as np
from contextlib import ExitStack

import concourse.bass as bass
import concourse.mybir as mybir
from concourse.bass_utils import run_bass_kernel_spmd

F32 = mybir.dt.float32
BF16 = mybir.dt.bfloat16
I32 = mybir.dt.int32
AF = mybir.ActivationFunctionType
ALU = mybir.AluOpType
AX = mybir.AxisListType.X

NCORE = 8
B = 2
S = 4096
DM = 2048
NT = 1024
H0 = 16
DH = 128
CK = 512
NCK = S // CK
EPS = 1e-5
D_SC = DH ** -0.5


INTER = 5504

class Sem:
    def __init__(self, h):
        self.h = h
        self.n = 0



def build_launch_a(upto='a3'):
    nc = bass.Bass("TRN2", target_bir_lowering=False, debug=False)
    xT = nc.dram_tensor("xT", [DM, S], F32, kind="ExternalInput").ap()
    wqT = nc.dram_tensor("wqT", [DM, DM], F32, kind="ExternalInput").ap()
    wkT = nc.dram_tensor("wkT", [DM, DM], F32, kind="ExternalInput").ap()
    wvT = nc.dram_tensor("wvT", [DM, DM], F32, kind="ExternalInput").ap()
    woT = nc.dram_tensor("woT", [DM, DM], F32, kind="ExternalInput").ap()
    gam = nc.dram_tensor("gam", [DH, 1], F32, kind="ExternalInput").ap()
    bet = nc.dram_tensor("bet", [DH, 1], F32, kind="ExternalInput").ap()
    msk = nc.dram_tensor("msk", [128, NCK * 4 * 2 * 512], F32, kind="ExternalInput").ap()
    rk_in = nc.dram_tensor("rk", [1, 1], I32, kind="ExternalInput").ap()

    xn_out = nc.dram_tensor("xn_out", [DM, S], BF16, kind="ExternalOutput").ap()
    s0_out = nc.dram_tensor("s0_out", [DM, NT], F32, kind="ExternalOutput").ap()
    xn_dram = nc.dram_tensor("xn_dram", [DM, S], BF16)

    xT3 = xT.rearrange("(j p) t -> p j t", p=128)
    wqT3 = wqT.rearrange("(j p) f -> p j f", p=128)
    wkT3 = wkT.rearrange("(j p) f -> p j f", p=128)
    wvT3 = wvT.rearrange("(j p) f -> p j f", p=128)
    woT3 = woT.rearrange("(j p) f -> p j f", p=128)
    xnd3 = xn_dram.ap().rearrange("(j p) t -> p j t", p=128)
    xno3 = xn_out.rearrange("(j p) t -> p j t", p=128)
    s0o3 = s0_out.rearrange("(j p) t -> p j t", p=128)
    msk3 = msk.rearrange("p (k s q t) -> p k s q t", k=NCK, s=4, q=2)

    es = ExitStack()
    sb = lambda n, s, d: es.enter_context(nc.sbuf_tensor(n, s, d))
    pst = lambda n, s, d: es.enter_context(nc.psum_tensor(n, s, d))
    mksem = lambda n: Sem(es.enter_context(nc.semaphore(n)))

    ones_bf = sb("ones_bf", [128, 1], BF16)
    eps_s = sb("eps_s", [128, 1], F32)
    ones_row = sb("ones_row", [1, 128], BF16)
    gam_s = sb("gam_s", [DH, 1], F32)
    bet_s = sb("bet_s", [DH, 1], F32)
    rk_s = sb("rk_s", [1, 1], I32)
    st = [sb(f"st{i}", [1, CK], F32) for i in range(7)]
    st_abf = sb("st_abf", [1, CK], BF16)
    st_bbf = sb("st_bbf", [1, CK], BF16)
    ab_bc = sb("ab_bc", [128, 2 * CK], BF16)
    p_t = [sb(f"p_t{i}", [128, 512], BF16) for i in range(2)]
    qg_t = sb("qg_t", [128, CK], BF16)
    kg_t = sb("kg_t", [128, CK], BF16)
    sq2 = sb("sq2", [128, 2 * CK], BF16)
    rz = sb("rz", [1, NT], F32)
    rz_bf = sb("rz_bf", [1, NT], BF16)
    q_all = sb("q_all", [128, H0 * NT], BF16)
    on_all = q_all
    xck = [sb(f"xck{i}", [128, 16 * CK], BF16) for i in range(2)]

    psA = pst("psA", [128, 512], F32)
    psB = pst("psB", [128, 512], F32)
    psO0 = pst("psO0", [128, 512], F32)
    psO1 = pst("psO1", [128, 512], F32)
    psS1 = pst("psS1", [1, 512], F32)
    psS2 = pst("psS2", [1, 512], F32)
    psBC = pst("psBC", [128, 512], F32)
    psBC2 = pst("psBC2", [128, 512], F32)

    ld = mksem("ld")
    lw = mksem("lw")
    lxp = [mksem("lxa"), mksem("lxb")]
    lm = mksem("lm")
    lst = mksem("lst")
    pe = mksem("pe")
    ve = mksem("ve")
    se = mksem("se")
    fc = mksem("fc")
    so = mksem("so")
    xs = mksem("xs")
    xqd = mksem("xqd")

    def stats_ladder(blk, W, inv_n, scale):
        pew = pe.n

        @blk.vector
        def _(v):
            v.wait_ge(pe.h, pew)
            v.tensor_copy(out=st[0][:, :W], in_=psS1[:, :W])
            v.tensor_copy(out=st[1][:, :W], in_=psS2[:, :W]).then_inc(ve.h, 1)
            ve.n += 1
            v.wait_ge(ve.h, ve.n)
            v.tensor_scalar_mul(out=st[2][:, :W], in0=st[0][:, :W], scalar1=float(inv_n))
            v.tensor_scalar_mul(out=st[3][:, :W], in0=st[1][:, :W], scalar1=float(inv_n)).then_inc(ve.h, 1)
            ve.n += 1
            v.wait_ge(ve.h, ve.n)
            v.tensor_mul(out=st[4][:, :W], in0=st[2][:, :W], in1=st[2][:, :W]).then_inc(ve.h, 1)
            ve.n += 1
            v.wait_ge(ve.h, ve.n)
            v.tensor_sub(out=st[3][:, :W], in0=st[3][:, :W], in1=st[4][:, :W]).then_inc(ve.h, 1)
            ve.n += 1

        sew0 = ve.n

        @blk.scalar
        def _(s):
            s.wait_ge(ve.h, sew0)
            s.activation(out=st[5][:, :W], in_=st[3][:, :W], func=AF.Sqrt,
                         bias=eps_s[0:1, 0:1], scale=1.0).then_inc(se.h, 1)
            se.n += 1

        sew1 = se.n

        @blk.vector
        def _(v):
            v.wait_ge(se.h, sew1)
            v.reciprocal(out=st[5][:, :W], in_=st[5][:, :W]).then_inc(ve.h, 1)
            ve.n += 1
            v.wait_ge(ve.h, ve.n)
            if scale != 1.0:
                v.tensor_scalar_mul(out=st[5][:, :W], in0=st[5][:, :W], scalar1=float(scale)).then_inc(ve.h, 1)
                ve.n += 1
                v.wait_ge(ve.h, ve.n)
            v.tensor_mul(out=st[6][:, :W], in0=st[2][:, :W], in1=st[5][:, :W]).then_inc(ve.h, 1)
            ve.n += 1
            v.wait_ge(ve.h, ve.n)
            v.tensor_scalar_mul(out=st[6][:, :W], in0=st[6][:, :W], scalar1=-1.0)
            v.tensor_copy(out=st_abf[:, :W], in_=st[5][:, :W]).then_inc(ve.h, 1)
            ve.n += 1
            v.wait_ge(ve.h, ve.n)
            v.tensor_copy(out=st_bbf[:, :W], in_=st[6][:, :W]).then_inc(ve.h, 1)
            ve.n += 1

        vew = ve.n

        @blk.tensor
        def _(t):
            t.wait_ge(ve.h, vew)
            t.matmul(psBC[:, :W], ones_row[0:1, :], st_abf[0:1, :W], start=True, stop=True)
            t.matmul(psBC2[:, :W], ones_row[0:1, :], st_bbf[0:1, :W], start=True, stop=True).then_inc(pe.h, 1)
            pe.n += 1

        pew2 = pe.n

        @blk.vector
        def _(v):
            v.wait_ge(pe.h, pew2)
            v.tensor_copy(out=ab_bc[:, 0:W], in_=psBC[:, :W])
            v.tensor_copy(out=ab_bc[:, W:2 * W], in_=psBC2[:, :W]).then_inc(ve.h, 1)
            ve.n += 1

    with es, nc.Block() as blk:
        a0 = ExitStack()
        xn_t = a0.enter_context(nc.sbuf_tensor("xn_t", [128, 16 * CK], BF16))

        @blk.sync
        def _(s):
            s.dma_start(out=gam_s[:, :], in_=gam).then_inc(ld.h, 16)
            s.dma_start(out=bet_s[:, :], in_=bet).then_inc(ld.h, 16)
            s.dma_start(out=rk_s[:, :], in_=rk_in).then_inc(ld.h, 16)
            ld.n += 48

        @blk.vector
        def _(v):
            v.memset(ones_bf[:, :], 1.0)
            v.memset(ones_row[:, :], 1.0)
            v.memset(eps_s[:, :], EPS).then_inc(ve.h, 1)
            ve.n += 1

        pe_stats_prev = 0
        for kt in range(NCK):
            bi = kt % 2

            @blk.gpsimd
            def _(g, kt=kt, bi=bi):
                if kt >= 2:
                    g.wait_ge(fc.h, 2 * (kt - 1))
                g.dma_start(out=xck[bi][:, :].rearrange("p (j t) -> p j t", j=16),
                            in_=xT3[:, :, kt * CK:(kt + 1) * CK]).then_inc(lxp[bi].h, 16)
                lxp[bi].n += 16

            ldx = lxp[bi].n

            for j in range(16):
                sqw = pe.n if j >= 2 else pe_stats_prev

                @blk.vector
                def _(v, bi=bi, j=j, ldx=ldx, sqw=sqw):
                    if j == 0:
                        v.wait_ge(lxp[bi].h, ldx)
                    v.wait_ge(pe.h, sqw)
                    v.tensor_mul(out=sq2[:, (j % 2) * CK:(j % 2 + 1) * CK],
                                 in0=xck[bi][:, j * CK:(j + 1) * CK],
                                 in1=xck[bi][:, j * CK:(j + 1) * CK]).then_inc(ve.h, 1)
                    ve.n += 1

                vw = ve.n

                @blk.tensor
                def _(t, bi=bi, j=j, ldx=ldx, vw=vw):
                    if j == 0:
                        t.wait_ge(lxp[bi].h, ldx)
                    t.wait_ge(ve.h, vw)
                    t.matmul(psS1[:, :], ones_bf[:, :], xck[bi][:, j * CK:(j + 1) * CK],
                             start=(j == 0), stop=(j == 15))
                    t.matmul(psS2[:, :], ones_bf[:, :], sq2[:, (j % 2) * CK:(j % 2 + 1) * CK],
                             start=(j == 0), stop=(j == 15)).then_inc(pe.h, 1)
                    pe.n += 1

            pe_stats_prev = pe.n

            stats_ladder(blk, CK, 1.0 / DM, 1.0)
            abd = ve.n

            @blk.vector
            def _(v, bi=bi, abd=abd, kt=kt):
                v.wait_ge(ve.h, abd)
                if kt >= 1:
                    v.wait_ge(fc.h, 2 * kt)
                last = None
                for j in range(16):
                    last = v.tensor_mul(out=xn_t[:, j * CK:(j + 1) * CK],
                                 in0=xck[bi][:, j * CK:(j + 1) * CK],
                                 in1=ab_bc[:, 0:CK])
                last.then_inc(ve.h, 1)
                ve.n += 1
                v.wait_ge(ve.h, ve.n)
                last = None
                for j in range(16):
                    last = v.tensor_add(out=xn_t[:, j * CK:(j + 1) * CK],
                                 in0=xn_t[:, j * CK:(j + 1) * CK],
                                 in1=ab_bc[:, CK:2 * CK])
                last.then_inc(ve.h, 1)
                ve.n += 1
                v.engine_nop().then_inc(fc.h, 1)
                fc.n += 1

            xnd = ve.n

            @blk.gpsimd
            def _(g, kt=kt, xnd=xnd):
                g.wait_ge(ve.h, xnd)
                g.dma_start(out=xnd3[:, :, kt * CK:(kt + 1) * CK],
                            in_=xn_t[:, :].rearrange("p (j t) -> p j t", j=16)).then_inc(lst.h, 16)
                g.dma_start(out=xno3[:, :, kt * CK:(kt + 1) * CK],
                            in_=xn_t[:, :].rearrange("p (j t) -> p j t", j=16)).then_inc(lst.h, 16)
                lst.n += 32
                g.wait_ge(lst.h, lst.n)
                g.engine_nop().then_inc(fc.h, 1)
                fc.n += 1

        lst_a0 = lst.n
        fc_a0 = fc.n
        a0.close()
        if upto == 'a0':
            @blk.gpsimd
            def _(g):
                g.wait_ge(lst.h, lst.n)
            return nc

        a1 = ExitStack()
        xq = a1.enter_context(nc.sbuf_tensor("xq", [128, 16 * NT], BF16))
        wqh = a1.enter_context(nc.sbuf_tensor("wqh", [128, 16 * 1024], BF16))

        @blk.gpsimd
        def _(g):
            g.wait_ge(lst.h, lst_a0)
            g.wait_ge(ld.h, 48)
            rreg = g.alloc_register()
            g.reg_load(rreg, rk_s[0:1, 0:1])
            for r in range(4):
                with g.If_cmp(rreg, r, "IS_EQ"):
                    g.dma_start(out=xq[:, :].rearrange("p (j t) -> p j t", j=16),
                                in_=xnd3[:, :, r * NT:(r + 1) * NT]).then_inc(xqd.h, 16)
                    g.wait_ge(xqd.h, 16)
                    g.sem_inc(xs.h, 1)
                with g.Else():
                    g.sem_inc(xs.h, 1)
            xs.n += 4

        for half in range(2):
            @blk.gpsimd
            def _(g, half=half):
                if half > 0:
                    g.wait_ge(pe.h, pe.n)
                g.dma_start(out=wqh[:, :].rearrange("p (j f) -> p j f", j=16),
                            in_=wqT3[:, :, half * 1024:(half + 1) * 1024]).then_inc(lw.h, 16)
                lw.n += 16

            ld_wq = lw.n

            for hh in range(8):
                head = half * 8 + hh
                for tc in range(2):
                    @blk.tensor
                    def _(t, hh=hh, tc=tc, ld_wq=ld_wq):
                        t.wait_ge(lw.h, ld_wq)
                        t.wait_ge(xs.h, 4)
                        t.wait_ge(ve.h, ve.n)
                        mm = None
                        for j in range(16):
                            mm = t.matmul(psA[:, :],
                                     wqh[:, j * 1024 + hh * 128: j * 1024 + hh * 128 + 128],
                                     xq[:, j * NT + tc * 512: j * NT + tc * 512 + 512],
                                     start=(j == 0), stop=(j == 15))
                        mm.then_inc(pe.h, 1)
                        pe.n += 1

                    pw = pe.n

                    @blk.vector
                    def _(v, pw=pw):
                        v.wait_ge(pe.h, pw)
                        v.tensor_scalar(out=qg_t[:, :], in0=psA[:, :],
                                        scalar1=gam_s[:, 0:1], scalar2=bet_s[:, 0:1],
                                        op0=ALU.mult, op1=ALU.add).then_inc(ve.h, 1)
                        ve.n += 1
                        v.wait_ge(ve.h, ve.n)
                        v.tensor_mul(out=sq2[:, 0:CK], in0=qg_t[:, :], in1=qg_t[:, :]).then_inc(ve.h, 1)
                        ve.n += 1

                    qgd = ve.n

                    @blk.tensor
                    def _(t, qgd=qgd):
                        t.wait_ge(ve.h, qgd)
                        t.matmul(psS1[:, :], ones_bf[:, :], qg_t[:, :], start=True, stop=True)
                        t.matmul(psS2[:, :], ones_bf[:, :], sq2[:, 0:CK], start=True, stop=True).then_inc(pe.h, 1)
                        pe.n += 1

                    stats_ladder(blk, CK, 1.0 / DH, D_SC)

                    @blk.vector
                    def _(v, head=head, tc=tc):
                        v.wait_ge(ve.h, ve.n)
                        dst = q_all[:, head * NT + tc * 512: head * NT + tc * 512 + 512]
                        v.tensor_mul(out=dst, in0=qg_t[:, :], in1=ab_bc[:, 0:CK]).then_inc(ve.h, 1)
                        ve.n += 1
                        v.wait_ge(ve.h, ve.n)
                        v.tensor_add(out=dst, in0=dst, in1=ab_bc[:, CK:2 * CK]).then_inc(ve.h, 1)
                        ve.n += 1

        a1.close()
        if upto == 'a1':
            @blk.gpsimd
            def _(g):
                g.wait_ge(lst.h, lst.n)
                g.wait_ge(xs.h, 4)
            @blk.vector
            def _(v):
                v.wait_ge(ve.h, ve.n)
            return nc

        a2 = ExitStack()
        k_all = a2.enter_context(nc.sbuf_tensor("k_all", [128, 4 * CK], BF16))
        v_tm = a2.enter_context(nc.sbuf_tensor("v_tm", [128, 4 * CK], BF16))
        wk_slc = a2.enter_context(nc.sbuf_tensor("wk_slc", [128, 16 * 512], BF16))
        wv_slc = a2.enter_context(nc.sbuf_tensor("wv_slc", [128, 16 * 512], BF16))
        mck = a2.enter_context(nc.sbuf_tensor("mck", [128, 8 * 512], BF16))
        o_hg = a2.enter_context(nc.sbuf_tensor("o_hg", [128, 4 * NT], F32))
        z_hg = a2.enter_context(nc.sbuf_tensor("z_hg", [1, 4 * NT], F32))

        it = 0
        for hg in range(4):
            @blk.gpsimd
            def _(g, hg=hg):
                g.wait_ge(pe.h, pe.n)
                g.wait_ge(ve.h, ve.n)
                g.dma_start(out=wk_slc[:, :].rearrange("p (j f) -> p j f", j=16),
                            in_=wkT3[:, :, hg * 512:(hg + 1) * 512]).then_inc(lw.h, 16)
                g.dma_start(out=wv_slc[:, :].rearrange("p (j f) -> p j f", j=16),
                            in_=wvT3[:, :, hg * 512:(hg + 1) * 512]).then_inc(lw.h, 16)
                lw.n += 32

            ld_w = lw.n

            @blk.vector
            def _(v, hg=hg):
                if hg == 0:
                    v.memset(o_hg[:, :], 0.0)
                    v.memset(z_hg[:, :], 0.0).then_inc(ve.h, 1)
                    ve.n += 1

            for kt in range(NCK):
                bi = it % 2

                @blk.gpsimd
                def _(g, kt=kt, bi=bi, it=it):
                    if it >= 2:
                        g.wait_ge(fc.h, fc_a0 + it - 1)
                    else:
                        g.wait_ge(fc.h, fc_a0)
                    g.dma_start(out=xck[bi][:, :].rearrange("p (j t) -> p j t", j=16),
                                in_=xnd3[:, :, kt * CK:(kt + 1) * CK]).then_inc(lxp[bi].h, 16)
                    lxp[bi].n += 16
                    if it >= 1:
                        g.wait_ge(ve.h, ve.n)
                    g.dma_start(out=mck[:, :].rearrange("p (s q t) -> p s q t", s=4, q=2),
                                in_=msk3[:, kt, :, :, :]).then_inc(lm.h, 16)
                    lm.n += 16

                ldc = (lxp[bi].n, lm.n)

                for hc in range(4):
                    @blk.tensor
                    def _(t, hc=hc, bi=bi, ldc=ldc, ld_w=ld_w):
                        t.wait_ge(lw.h, ld_w)
                        t.wait_ge(lxp[bi].h, ldc[0])
                        t.wait_ge(ve.h, ve.n)
                        mm = None
                        for j in range(16):
                            mm = t.matmul(psA[:, :],
                                     wk_slc[:, j * 512 + hc * 128: j * 512 + hc * 128 + 128],
                                     xck[bi][:, j * CK:(j + 1) * CK],
                                     start=(j == 0), stop=(j == 15))
                        mm.then_inc(pe.h, 1)
                        pe.n += 1

                    pw = pe.n

                    @blk.vector
                    def _(v, pw=pw):
                        v.wait_ge(pe.h, pw)
                        v.tensor_scalar(out=kg_t[:, :], in0=psA[:, :],
                                        scalar1=gam_s[:, 0:1], scalar2=bet_s[:, 0:1],
                                        op0=ALU.mult, op1=ALU.add).then_inc(ve.h, 1)
                        ve.n += 1
                        v.wait_ge(ve.h, ve.n)
                        v.tensor_mul(out=sq2[:, 0:CK], in0=kg_t[:, :], in1=kg_t[:, :]).then_inc(ve.h, 1)
                        ve.n += 1

                    kgd = ve.n

                    @blk.tensor
                    def _(t, kgd=kgd):
                        t.wait_ge(ve.h, kgd)
                        t.matmul(psS1[:, :], ones_bf[:, :], kg_t[:, :], start=True, stop=True)
                        t.matmul(psS2[:, :], ones_bf[:, :], sq2[:, 0:CK], start=True, stop=True).then_inc(pe.h, 1)
                        pe.n += 1

                    stats_ladder(blk, CK, 1.0 / DH, 1.0)

                    @blk.vector
                    def _(v, hc=hc):
                        v.wait_ge(ve.h, ve.n)
                        dst = k_all[:, hc * CK:(hc + 1) * CK]
                        v.tensor_mul(out=dst, in0=kg_t[:, :], in1=ab_bc[:, 0:CK]).then_inc(ve.h, 1)
                        ve.n += 1
                        v.wait_ge(ve.h, ve.n)
                        v.tensor_add(out=dst, in0=dst, in1=ab_bc[:, CK:2 * CK]).then_inc(ve.h, 1)
                        ve.n += 1

                for ts_i in range(4):
                    @blk.tensor
                    def _(t, ts_i=ts_i, bi=bi, ldc=ldc, ld_w=ld_w):
                        t.wait_ge(lw.h, ld_w)
                        t.wait_ge(lxp[bi].h, ldc[0])
                        t.wait_ge(ve.h, ve.n)
                        mm = None
                        for j in range(16):
                            mm = t.matmul(psB[:, :],
                                     xck[bi][:, j * CK + ts_i * 128: j * CK + ts_i * 128 + 128],
                                     wv_slc[:, j * 512:(j + 1) * 512],
                                     start=(j == 0), stop=(j == 15))
                        mm.then_inc(pe.h, 1)
                        pe.n += 1

                    pw = pe.n

                    @blk.vector
                    def _(v, ts_i=ts_i, pw=pw):
                        v.wait_ge(pe.h, pw)
                        v.tensor_copy(out=v_tm[:, ts_i * 512:(ts_i + 1) * 512],
                                      in_=psB[:, :]).then_inc(ve.h, 1)
                        ve.n += 1

                kv_ready = ve.n

                for hc in range(4):
                    habs = hg * 4 + hc
                    for qt in range(2):
                        zbank = psS1 if qt == 0 else psS2
                        obank = psO0 if qt == 0 else psO1
                        for ktsub in range(4):
                            sbank = psA if ktsub % 2 == 0 else psB
                            pbuf = p_t[ktsub % 2]

                            @blk.tensor
                            def _(t, hc=hc, habs=habs, qt=qt, ktsub=ktsub,
                                  sbank=sbank, kv_ready=kv_ready):
                                t.wait_ge(ve.h, kv_ready)
                                t.wait_ge(se.h, se.n)
                                t.matmul(sbank[:, :],
                                         k_all[:, hc * CK + ktsub * 128: hc * CK + ktsub * 128 + 128],
                                         q_all[:, habs * NT + qt * 512: habs * NT + qt * 512 + 512],
                                         start=True, stop=True).then_inc(pe.h, 1)
                                pe.n += 1

                            pw = pe.n

                            @blk.scalar
                            def _(s, pw=pw, sbank=sbank, pbuf=pbuf):
                                s.wait_ge(pe.h, pw)
                                s.wait_ge(ve.h, ve.n)
                                s.activation(out=pbuf[:, :], in_=sbank[:, :],
                                             func=AF.Square).then_inc(se.h, 1)
                                se.n += 1

                            sw = se.n

                            @blk.vector
                            def _(v, ktsub=ktsub, qt=qt, sw=sw, pbuf=pbuf, ldc=ldc):
                                v.wait_ge(se.h, sw)
                                v.wait_ge(lm.h, ldc[1])
                                v.tensor_mul(out=pbuf[:, :], in0=pbuf[:, :],
                                             in1=mck[:, (ktsub * 2 + qt) * 512:(ktsub * 2 + qt + 1) * 512]).then_inc(ve.h, 1)
                                ve.n += 1

                            pd = ve.n

                            @blk.tensor
                            def _(t, hc=hc, ktsub=ktsub, pd=pd, pbuf=pbuf,
                                  obank=obank, zbank=zbank):
                                t.wait_ge(ve.h, pd)
                                t.matmul(obank[:, :],
                                         v_tm[:, ktsub * 512 + hc * 128: ktsub * 512 + hc * 128 + 128],
                                         pbuf[:, :],
                                         start=(ktsub == 0), stop=(ktsub == 3))
                                t.matmul(zbank[:, :], ones_bf[:, :], pbuf[:, :],
                                         start=(ktsub == 0), stop=(ktsub == 3)).then_inc(pe.h, 1)
                                pe.n += 1

                        pw = pe.n

                        @blk.vector
                        def _(v, hc=hc, qt=qt, pw=pw, obank=obank, zbank=zbank):
                            v.wait_ge(pe.h, pw)
                            dsto = o_hg[:, hc * NT + qt * 512: hc * NT + qt * 512 + 512]
                            v.tensor_add(out=dsto, in0=dsto, in1=obank[:, :])
                            dstz = z_hg[:, hc * NT + qt * 512: hc * NT + qt * 512 + 512]
                            v.tensor_add(out=dstz, in0=dstz, in1=zbank[:, :]).then_inc(ve.h, 1)
                            ve.n += 1

                @blk.vector
                def _(v):
                    v.engine_nop().then_inc(fc.h, 1)
                    fc.n += 1

                it += 1

            for hc in range(4):
                habs = hg * 4 + hc

                @blk.vector
                def _(v, hc=hc):
                    v.wait_ge(ve.h, ve.n)
                    v.tensor_scalar_add(out=rz[:, :], in0=z_hg[:, hc * NT:(hc + 1) * NT],
                                        scalar1=EPS).then_inc(ve.h, 1)
                    ve.n += 1
                    v.wait_ge(ve.h, ve.n)
                    v.reciprocal(out=rz[:, :], in_=rz[:, :]).then_inc(ve.h, 1)
                    ve.n += 1
                    v.wait_ge(ve.h, ve.n)
                    v.tensor_copy(out=rz_bf[:, :], in_=rz[:, :]).then_inc(ve.h, 1)
                    ve.n += 1

                rzd = ve.n

                for qt in range(2):
                    @blk.tensor
                    def _(t, qt=qt, rzd=rzd):
                        t.wait_ge(ve.h, max(rzd, ve.n))
                        t.matmul(psBC[:, :], ones_row[0:1, :], rz_bf[0:1, qt * 512:(qt + 1) * 512],
                                 start=True, stop=True).then_inc(pe.h, 1)
                        pe.n += 1

                    pw = pe.n

                    @blk.vector
                    def _(v, hc=hc, habs=habs, qt=qt, pw=pw):
                        v.wait_ge(pe.h, pw)
                        v.tensor_copy(out=ab_bc[:, 0:512], in_=psBC[:, :]).then_inc(ve.h, 1)
                        ve.n += 1
                        v.wait_ge(ve.h, ve.n)
                        sl_o = slice(hc * NT + qt * 512, hc * NT + qt * 512 + 512)
                        sl_n = slice(habs * NT + qt * 512, habs * NT + qt * 512 + 512)
                        v.tensor_mul(out=on_all[:, sl_n], in0=o_hg[:, sl_o],
                                     in1=ab_bc[:, 0:512]).then_inc(ve.h, 1)
                        ve.n += 1

            if hg < 3:
                @blk.vector
                def _(v):
                    v.wait_ge(ve.h, ve.n)
                    v.memset(o_hg[:, :], 0.0)
                    v.memset(z_hg[:, :], 0.0).then_inc(ve.h, 1)
                    ve.n += 1

        a2.close()
        if upto == 'a2':
            @blk.gpsimd
            def _(g):
                g.wait_ge(lst.h, lst.n)
            @blk.vector
            def _(v):
                v.wait_ge(ve.h, ve.n)
            return nc

        a3 = ExitStack()
        wo_f = a3.enter_context(nc.sbuf_tensor("wo_f", [128, 16 * 2048], BF16))
        s0_t = a3.enter_context(nc.sbuf_tensor("s0_t", [128, 512], F32))

        @blk.gpsimd
        def _(g):
            g.wait_ge(pe.h, pe.n)
            g.wait_ge(ve.h, ve.n)
            g.dma_start(out=wo_f[:, :].rearrange("p (j f) -> p j f", j=16),
                        in_=woT3).then_inc(lw.h, 16)
            lw.n += 16

        ld_wo = lw.n

        for dj in range(16):
            for tc in range(2):
                @blk.tensor
                def _(t, dj=dj, tc=tc):
                    t.wait_ge(lw.h, ld_wo)
                    t.wait_ge(ve.h, ve.n)
                    mm = None
                    for e in range(16):
                        mm = t.matmul(psO0[:, :],
                                 wo_f[:, e * 2048 + dj * 128: e * 2048 + dj * 128 + 128],
                                 on_all[:, e * NT + tc * 512: e * NT + tc * 512 + 512],
                                 start=(e == 0), stop=(e == 15))
                    mm.then_inc(pe.h, 1)
                    pe.n += 1

                pw = pe.n

                @blk.vector
                def _(v, pw=pw):
                    v.wait_ge(pe.h, pw)
                    v.wait_ge(ve.h, ve.n)
                    v.tensor_copy(out=s0_t[:, :], in_=psO0[:, :]).then_inc(ve.h, 1)
                    ve.n += 1

                sd = ve.n

                @blk.gpsimd
                def _(g, dj=dj, tc=tc, sd=sd):
                    g.wait_ge(ve.h, sd)
                    g.dma_start(out=s0o3[:, dj, tc * 512:(tc + 1) * 512],
                                in_=s0_t[:, :]).then_inc(so.h, 16)
                    so.n += 16
                    g.wait_ge(so.h, so.n)
                    g.engine_nop().then_inc(ve.h, 1)
                    ve.n += 1

        a3.close()

        @blk.gpsimd
        def _(g):
            g.wait_ge(so.h, so.n)
            g.wait_ge(lst.h, lst.n)

    return nc



def prep_launch_a(inputs):
    x = np.asarray(inputs["L0_attn_hidden_states"], np.float32)
    gam = np.ascontiguousarray(np.asarray(inputs["L0_attn_gamma"], np.float32).reshape(DH, 1))
    bet = np.ascontiguousarray(np.asarray(inputs["L0_attn_beta"], np.float32).reshape(DH, 1))
    wqT = np.ascontiguousarray(np.asarray(inputs["L0_attn_Wq"], np.float32).T)
    wkT = np.ascontiguousarray(np.asarray(inputs["L0_attn_Wk"], np.float32).T)
    wvT = np.ascontiguousarray(np.asarray(inputs["L0_attn_Wv"], np.float32).T)
    woT = np.ascontiguousarray(np.asarray(inputs["L0_attn_Wo"], np.float32).T)

    in_maps = []
    rows = np.arange(128)
    cols = np.arange(512)
    for c in range(NCORE):
        b, r = c // 4, c % 4
        xb = x[b].T.copy()
        xb[:, (r + 1) * NT:] = 0.0
        q0 = r * NT
        m = np.zeros((128, NCK, 4, 2, 512), np.float32)
        for k in range(NCK):
            for s_ in range(4):
                kt0 = k * CK + s_ * 128
                for qt in range(2):
                    qt0 = q0 + qt * 512
                    m[:, k, s_, qt, :] = ((kt0 + rows[:, None]) <= (qt0 + cols[None, :]))
        in_maps.append({
            "xT": np.ascontiguousarray(xb),
            "wqT": wqT, "wkT": wkT, "wvT": wvT, "woT": woT,
            "gam": gam, "bet": bet,
            "msk": np.ascontiguousarray(m.reshape(128, -1)),
            "rk": np.array([[r]], np.int32),
        })
    return in_maps


def kernel(**inputs):
    raise NotImplementedError("under construction")


if __name__ == "__main__":
    pass



OSL = INTER_SL = 688


def build_launch_c():
    nc = bass.Bass("TRN2", target_bir_lowering=False, debug=False)
    lut = nc.dram_tensor("lut", [1, 16], F32, kind="ExternalInput").ap()
    gidxT = nc.dram_tensor("gidxT", [DM, OSL], I32, kind="ExternalInput").ap()
    uidxT = nc.dram_tensor("uidxT", [DM, OSL], I32, kind="ExternalInput").ap()
    didxT = nc.dram_tensor("didxT", [OSL, DM], I32, kind="ExternalInput").ap()
    gws = nc.dram_tensor("gws", [DM, OSL], BF16, kind="ExternalOutput").ap()
    uws = nc.dram_tensor("uws", [DM, OSL], BF16, kind="ExternalOutput").ap()
    dws = nc.dram_tensor("dws", [OSL, DM], BF16, kind="ExternalOutput").ap()

    es = ExitStack()
    sb = lambda n, s, d: es.enter_context(nc.sbuf_tensor(n, s, d))
    pst = lambda n, s, d: es.enter_context(nc.psum_tensor(n, s, d))
    mksem = lambda n: Sem(es.enter_context(nc.semaphore(n)))

    ones_row = sb("ones_row", [1, 128], BF16)
    lut_s = sb("lut_s", [1, 16], F32)
    lut_bf = sb("lut_bf", [1, 16], BF16)
    lut_bc = sb("lut_bc", [128, 16], F32)
    idx_bf = [sb(f"idx_bf{i}", [128, DM], BF16) for i in range(2)]
    m_t = [sb(f"m_t{i}", [128, DM], BF16) for i in range(2)]
    acc = [sb(f"acc{i}", [128, DM], BF16) for i in range(2)]
    psL = pst("psL", [128, 16], F32)

    ld = mksem("ld")
    li = mksem("li")
    pe = mksem("pe")
    ve = mksem("ve")
    so = mksem("so")

    tiles = []
    for src, dst in ((gidxT, gws), (uidxT, uws)):
        for r0 in range(16):
            tiles.append((src[r0 * 128:(r0 + 1) * 128, :], dst[r0 * 128:(r0 + 1) * 128, :], 128, OSL))
    for r0 in range(6):
        nr = 128 if r0 < 5 else OSL - 5 * 128
        tiles.append((didxT[r0 * 128:r0 * 128 + nr, :], dws[r0 * 128:r0 * 128 + nr, :], nr, DM))

    with es, nc.Block() as blk:
        @blk.sync
        def _(s):
            s.dma_start(out=lut_s[:, :], in_=lut).then_inc(ld.h, 16)
            ld.n += 16

        @blk.vector
        def _(v):
            v.memset(ones_row[:, :], 1.0).then_inc(ve.h, 1)
            ve.n += 1
            v.wait_ge(ld.h, 16)
            v.tensor_copy(out=lut_bf[:, :], in_=lut_s[:, :]).then_inc(ve.h, 1)
            ve.n += 1

        @blk.tensor
        def _(t):
            t.wait_ge(ve.h, 2)
            t.matmul(psL[:, :], ones_row[0:1, :], lut_bf[0:1, :],
                     start=True, stop=True).then_inc(pe.h, 1)
            pe.n += 1

        @blk.vector
        def _(v):
            v.wait_ge(pe.h, 1)
            v.tensor_copy(out=lut_bc[:, :], in_=psL[:, :]).then_inc(ve.h, 1)
            ve.n += 1

        lutd = ve.n

        for ti, (src, dst, nr, ncol) in enumerate(tiles):
            bi = ti % 2

            @blk.gpsimd
            def _(g, src=src, bi=bi, nr=nr, ncol=ncol, ti=ti):
                if ti >= 2:
                    g.wait_ge(ve.h, ve.n)
                g.dma_start(out=idx_bf[bi][0:nr, 0:ncol], in_=src).then_inc(li.h, 16)
                li.n += 16

            lid = li.n

            @blk.vector
            def _(v, bi=bi, nr=nr, ncol=ncol, lid=lid, lutd=lutd):
                v.wait_ge(li.h, lid)
                v.wait_ge(ve.h, lutd)
                v.wait_ge(so.h, so.n)
                v.tensor_scalar(out=acc[bi][0:nr, 0:ncol], in0=idx_bf[bi][0:nr, 0:ncol],
                                scalar1=0.0, scalar2=lut_bc[0:nr, 0:1],
                                op0=ALU.is_equal, op1=ALU.mult).then_inc(ve.h, 1)
                ve.n += 1
                for i in range(1, 16):
                    v.tensor_scalar(out=m_t[bi][0:nr, 0:ncol], in0=idx_bf[bi][0:nr, 0:ncol],
                                    scalar1=float(i), scalar2=lut_bc[0:nr, i:i + 1],
                                    op0=ALU.is_equal, op1=ALU.mult).then_inc(ve.h, 1)
                    ve.n += 1
                    v.wait_ge(ve.h, ve.n)
                    v.tensor_add(out=acc[bi][0:nr, 0:ncol], in0=acc[bi][0:nr, 0:ncol],
                                 in1=m_t[bi][0:nr, 0:ncol]).then_inc(ve.h, 1)
                    ve.n += 1

            accd = ve.n

            @blk.gpsimd
            def _(g, dst=dst, bi=bi, nr=nr, ncol=ncol, accd=accd):
                g.wait_ge(ve.h, accd)
                g.dma_start(out=dst, in_=acc[bi][0:nr, 0:ncol]).then_inc(so.h, 16)
                so.n += 16

        @blk.gpsimd
        def _(g):
            g.wait_ge(so.h, so.n)

    return nc


def prep_launch_c(inputs):
    lut = np.ascontiguousarray(np.asarray(inputs["L1_ffn_lut"], np.float32).reshape(1, 16))
    gi = np.asarray(inputs["L1_ffn_gate_idx"], np.int32)
    ui = np.asarray(inputs["L1_ffn_up_idx"], np.int32)
    di = np.asarray(inputs["L1_ffn_down_idx"], np.int32)
    in_maps = []
    for c in range(NCORE):
        osl = slice(c * OSL, (c + 1) * OSL)
        in_maps.append({
            "lut": lut,
            "gidxT": np.ascontiguousarray(gi[osl, :].T),
            "uidxT": np.ascontiguousarray(ui[osl, :].T),
            "didxT": np.ascontiguousarray(di[:, osl].T),
        })
    return in_maps



H1N = 16
KV1 = 4
CH = 64
NCH = NT // CH

_L1_LAM = [float(2.0 ** (-8.0 * (h + 1) / 16.0)) for h in range(16)]
_L1_BLK = [float(np.exp(-lam * CH)) for lam in _L1_LAM]


def build_launch_b():
    nc = bass.Bass("TRN2", target_bir_lowering=False, debug=False)
    ap = lambda n, s, d, k: nc.dram_tensor(n, s, d, kind=k).ap()
    xT = ap("xT", [DM, NT], F32, "ExternalInput")
    xnT = ap("xnT", [DM, NT], BF16, "ExternalInput")
    s0T = ap("s0T", [DM, NT], F32, "ExternalInput")
    gwT = ap("gwT", [DM, 8], F32, "ExternalInput")
    gb1 = ap("gb1", [1, 8], F32, "ExternalInput")
    gb2 = ap("gb2", [1, 8], F32, "ExternalInput")
    w1T = ap("w1T", [8 * DM, 1408], F32, "ExternalInput")
    b1c = ap("b1c", [8 * 128, 11], F32, "ExternalInput")
    w2T = ap("w2T", [8 * 1408, DM], F32, "ExternalInput")
    b2r = ap("b2r", [8, DM], F32, "ExternalInput")
    wq1T = ap("wq1T", [DM, DM], F32, "ExternalInput")
    wk1T = ap("wk1T", [DM, 512], F32, "ExternalInput")
    wv1T = ap("wv1T", [DM, 512], F32, "ExternalInput")
    cosf = ap("cosf", [128, NT], F32, "ExternalInput")
    sinf = ap("sinf", [128, NT], F32, "ExternalInput")
    rotT = ap("rotT", [128, 128], F32, "ExternalInput")
    identw = ap("identw", [128, 128], F32, "ExternalInput")
    ddiag = ap("ddiag", [CH, 16 * CH], F32, "ExternalInput")
    qdecw = ap("qdecw", [128, 16 * CH], F32, "ExternalInput")
    kdecw = ap("kdecw", [CH, 16], F32, "ExternalInput")

    x1_out = ap("x1_out", [DM, NT], BF16, "ExternalOutput")
    xn1_out = ap("xn1_out", [DM, NT], BF16, "ExternalOutput")
    qr_out = ap("qr_out", [DM, NT], BF16, "ExternalOutput")
    oloc_out = ap("oloc_out", [DM, NT], BF16, "ExternalOutput")
    st_out = ap("st_out", [128, 16 * 128], F32, "ExternalOutput")

    r3 = lambda t: t.rearrange("(j p) n -> p j n", p=128)
    xT3, xnT3, s0T3 = r3(xT), r3(xnT), r3(s0T)
    gwT3 = r3(gwT)
    w1T3 = w1T.rearrange("(e j p) f -> e p j f", e=8, p=128)
    w2T3 = w2T.rearrange("(e j p) f -> e p j f", e=8, p=128)
    wq1T3, wk1T3, wv1T3 = r3(wq1T), r3(wk1T), r3(wv1T)
    x1o3, xn1o3, qro3, oloco3 = r3(x1_out), r3(xn1_out), r3(qr_out), r3(oloc_out)

    es = ExitStack()
    sb = lambda n, s, d: es.enter_context(nc.sbuf_tensor(n, s, d))
    pst = lambda n, s, d: es.enter_context(nc.psum_tensor(n, s, d))
    mksem = lambda n: Sem(es.enter_context(nc.semaphore(n)))

    ones_bf = sb("ones_bf", [128, 1], BF16)
    ones_row = sb("ones_row", [1, 128], BF16)
    eps_s = sb("eps_s", [128, 1], F32)
    zero_s = sb("zero_s", [128, 1], F32)
    ident = sb("ident", [128, 128], BF16)
    st = [sb(f"st{i}", [1, 512], F32) for i in range(7)]
    st_abf = sb("st_abf", [1, 512], BF16)
    st_bbf = sb("st_bbf", [1, 512], BF16)
    ab_bc = sb("ab_bc", [128, 1024], BF16)
    g_scr = sb("g_scr", [128, 8], F32)
    g_m1 = sb("g_m1", [128, 1], F32)
    g_nm1 = sb("g_nm1", [128, 1], F32)
    g_eq1 = sb("g_eq1", [128, 8], F32)
    g_s2 = sb("g_s2", [128, 8], F32)
    g_m2 = sb("g_m2", [128, 1], F32)
    g_ev = sb("g_ev", [128, 1], F32)
    g_g1 = sb("g_g1", [128, 1], F32)
    g_g2 = sb("g_g2", [128, 1], F32)
    g_eq2 = sb("g_eq2", [128, 8], F32)
    g_t0 = sb("g_t0", [128, 8], F32)
    comb_all = sb("comb_all", [128, 64], F32)
    combT = sb("combT", [8, 8 * 128], BF16)
    xn = sb("xn", [128, 16 * NT], BF16)
    acc = sb("acc", [128, 8 * DM], BF16)
    x1 = sb("x1", [128, 16 * NT], BF16)

    psA = pst("psA", [128, 512], F32)
    psB = pst("psB", [128, 512], F32)
    psS1 = pst("psS1", [1, 512], F32)
    psS2 = pst("psS2", [1, 512], F32)

    ld = mksem("ld")
    lw = mksem("lw")
    lx = mksem("lx")
    lst = mksem("lst")
    pe = mksem("pe")
    ve = mksem("ve")
    se = mksem("se")

    def ladder_b(blk, psBC, psBC2, W, inv_n, scale):
        pew = pe.n

        @blk.vector
        def _(v):
            v.wait_ge(pe.h, pew)
            v.tensor_copy(out=st[0][:, :W], in_=psS1[:, :W])
            v.tensor_copy(out=st[1][:, :W], in_=psS2[:, :W]).then_inc(ve.h, 1)
            ve.n += 1
            v.wait_ge(ve.h, ve.n)
            v.tensor_scalar_mul(out=st[2][:, :W], in0=st[0][:, :W], scalar1=float(inv_n))
            v.tensor_scalar_mul(out=st[3][:, :W], in0=st[1][:, :W], scalar1=float(inv_n)).then_inc(ve.h, 1)
            ve.n += 1
            v.wait_ge(ve.h, ve.n)
            v.tensor_mul(out=st[4][:, :W], in0=st[2][:, :W], in1=st[2][:, :W]).then_inc(ve.h, 1)
            ve.n += 1
            v.wait_ge(ve.h, ve.n)
            v.tensor_sub(out=st[3][:, :W], in0=st[3][:, :W], in1=st[4][:, :W]).then_inc(ve.h, 1)
            ve.n += 1

        sew0 = ve.n

        @blk.scalar
        def _(s):
            s.wait_ge(ve.h, sew0)
            s.activation(out=st[5][:, :W], in_=st[3][:, :W], func=AF.Sqrt,
                         bias=eps_s[0:1, 0:1], scale=1.0).then_inc(se.h, 1)
            se.n += 1

        sew1 = se.n

        @blk.vector
        def _(v):
            v.wait_ge(se.h, sew1)
            v.reciprocal(out=st[5][:, :W], in_=st[5][:, :W]).then_inc(ve.h, 1)
            ve.n += 1
            v.wait_ge(ve.h, ve.n)
            if scale != 1.0:
                v.tensor_scalar_mul(out=st[5][:, :W], in0=st[5][:, :W], scalar1=float(scale)).then_inc(ve.h, 1)
                ve.n += 1
                v.wait_ge(ve.h, ve.n)
            v.tensor_mul(out=st[6][:, :W], in0=st[2][:, :W], in1=st[5][:, :W]).then_inc(ve.h, 1)
            ve.n += 1
            v.wait_ge(ve.h, ve.n)
            v.tensor_scalar_mul(out=st[6][:, :W], in0=st[6][:, :W], scalar1=-1.0)
            v.tensor_copy(out=st_abf[:, :W], in_=st[5][:, :W]).then_inc(ve.h, 1)
            ve.n += 1
            v.wait_ge(ve.h, ve.n)
            v.tensor_copy(out=st_bbf[:, :W], in_=st[6][:, :W]).then_inc(ve.h, 1)
            ve.n += 1

        vew = ve.n

        @blk.tensor
        def _(t):
            t.wait_ge(ve.h, vew)
            t.matmul(psBC[:, :W], ones_row[0:1, :], st_abf[0:1, :W], start=True, stop=True)
            t.matmul(psBC2[:, :W], ones_row[0:1, :], st_bbf[0:1, :W], start=True, stop=True).then_inc(pe.h, 1)
            pe.n += 1

        pew2 = pe.n

        @blk.vector
        def _(v):
            v.wait_ge(pe.h, pew2)
            v.tensor_copy(out=ab_bc[:, 0:W], in_=psBC[:, :W])
            v.tensor_copy(out=ab_bc[:, W:2 * W], in_=psBC2[:, :W]).then_inc(ve.h, 1)
            ve.n += 1

    with es, nc.Block() as blk:
        b0 = ExitStack()
        gw_s = b0.enter_context(nc.sbuf_tensor("gw_s", [128, 16 * 8], BF16))
        gb_s = b0.enter_context(nc.sbuf_tensor("gb_s", [1, 16], BF16))

        @blk.vector
        def _(v):
            v.memset(ones_bf[:, :], 1.0)
            v.memset(ones_row[:, :], 1.0)
            v.memset(eps_s[:, :], EPS)
            v.memset(zero_s[:, :], 0.0).then_inc(ve.h, 1)
            ve.n += 1

        @blk.gpsimd
        def _(g):
            g.dma_start(out=xn[:, :].rearrange("p (j t) -> p j t", j=16), in_=xnT3).then_inc(lw.h, 16)
            g.dma_start(out=ident[:, :], in_=identw).then_inc(lw.h, 16)
            g.dma_start(out=gw_s[:, :].rearrange("p (j t) -> p j t", j=16), in_=gwT3).then_inc(lw.h, 16)
            g.dma_start(out=gb_s[:, 0:8], in_=gb1).then_inc(lw.h, 16)
            g.dma_start(out=gb_s[:, 8:16], in_=gb2).then_inc(lw.h, 16)
            lw.n += 80

        lw_b0 = lw.n

        b1 = ExitStack()
        psX1 = b1.enter_context(nc.psum_tensor("psX1", [128, 128], BF16))
        comb_bf = b1.enter_context(nc.sbuf_tensor("comb_bf", [128, 8], BF16))

        for tsub in range(8):
            @blk.tensor
            def _(t, tsub=tsub):
                t.wait_ge(lw.h, lw_b0)
                t.wait_ge(ve.h, 1 if tsub == 0 else ve.n)
                mm = None
                for j in range(16):
                    mm = t.matmul(psA[:, 0:8],
                                  xn[:, j * NT + tsub * 128: j * NT + tsub * 128 + 128],
                                  gw_s[:, j * 8:(j + 1) * 8],
                                  start=(j == 0), stop=False)
                t.matmul(psA[:, 0:8], ones_row[0:1, :], gb_s[0:1, 0:8], start=False, stop=False)
                t.matmul(psA[:, 0:8], ones_row[0:1, :], gb_s[0:1, 8:16], start=False, stop=True).then_inc(pe.h, 1)
                pe.n += 1

            pw = pe.n

            @blk.vector
            def _(v, pw=pw):
                v.wait_ge(pe.h, pw)
                v.tensor_copy(out=g_scr[:, :], in_=psA[:, 0:8]).then_inc(ve.h, 1)
                ve.n += 1
                v.wait_ge(ve.h, ve.n)
                v.reduce_max(out=g_m1[:, :], in_=g_scr[:, :], axis=AX)
                v.reduce_max(out=g_nm1[:, :], in_=g_scr[:, :], axis=AX, negate=True).then_inc(ve.h, 1)
                ve.n += 1
                v.wait_ge(ve.h, ve.n)
                v.tensor_single_scalar(out=g_eq1[:, :], in_=g_scr[:, :], scalar=g_m1[:, 0:1], op=ALU.is_equal).then_inc(ve.h, 1)
                ve.n += 1
                v.wait_ge(ve.h, ve.n)
                v.scalar_tensor_tensor(out=g_s2[:, :], in0=g_eq1[:, :], scalar=-1e30, in1=g_scr[:, :],
                                       op0=ALU.mult, op1=ALU.add).then_inc(ve.h, 1)
                ve.n += 1
                v.wait_ge(ve.h, ve.n)
                v.reduce_max(out=g_m2[:, :], in_=g_s2[:, :], axis=AX).then_inc(ve.h, 1)
                ve.n += 1
                v.wait_ge(ve.h, ve.n)
                v.tensor_single_scalar(out=g_eq2[:, :], in_=g_s2[:, :], scalar=g_m2[:, 0:1], op=ALU.is_equal).then_inc(ve.h, 1)
                ve.n += 1

            sew = ve.n

            @blk.scalar
            def _(s, sew=sew):
                s.wait_ge(ve.h, sew)
                s.activation(out=g_ev[:, :], in_=g_m2[:, :], func=AF.Exp,
                             bias=g_nm1[:, 0:1], scale=1.0).then_inc(se.h, 1)
                se.n += 1

            sw = se.n

            @blk.vector
            def _(v, tsub=tsub, sw=sw):
                v.wait_ge(se.h, sw)
                v.tensor_scalar_add(out=g_g1[:, :], in0=g_ev[:, :], scalar1=1.0).then_inc(ve.h, 1)
                ve.n += 1
                v.wait_ge(ve.h, ve.n)
                v.reciprocal(out=g_g1[:, :], in_=g_g1[:, :]).then_inc(ve.h, 1)
                ve.n += 1
                v.wait_ge(ve.h, ve.n)
                v.tensor_mul(out=g_g2[:, :], in0=g_ev[:, :], in1=g_g1[:, :])
                v.tensor_single_scalar(out=g_t0[:, :], in_=g_eq1[:, :], scalar=g_g1[:, 0:1], op=ALU.mult).then_inc(ve.h, 1)
                ve.n += 1
                v.wait_ge(ve.h, ve.n)
                v.scalar_tensor_tensor(out=comb_all[:, tsub * 8:(tsub + 1) * 8], in0=g_eq2[:, :],
                                       scalar=g_g2[:, 0:1], in1=g_t0[:, :],
                                       op0=ALU.mult, op1=ALU.add).then_inc(ve.h, 1)
                ve.n += 1
                v.wait_ge(ve.h, ve.n)
                v.tensor_copy(out=comb_bf[:, :], in_=comb_all[:, tsub * 8:(tsub + 1) * 8]).then_inc(ve.h, 1)
                ve.n += 1

            cbd = ve.n

            @blk.tensor
            def _(t, cbd=cbd):
                t.wait_ge(ve.h, cbd)
                t.transpose(psX1[0:8, 0:128], comb_bf[:, :], ident[:, :]).then_inc(pe.h, 1)
                pe.n += 1

            pw2 = pe.n

            @blk.vector
            def _(v, tsub=tsub, pw2=pw2):
                v.wait_ge(pe.h, pw2)
                v.tensor_copy(out=combT[0:8, tsub * 128:(tsub + 1) * 128], in_=psX1[0:8, 0:128]).then_inc(ve.h, 1)
                ve.n += 1

        b1.close()
        b0.close()

        b2 = ExitStack()
        w1s = b2.enter_context(nc.sbuf_tensor("w1s", [128, 16 * 1408], BF16))
        w2s = b2.enter_context(nc.sbuf_tensor("w2s", [128, 11 * 512], BF16))
        h_fm = b2.enter_context(nc.sbuf_tensor("h_fm", [128, 11 * NT], BF16))
        b1s = b2.enter_context(nc.sbuf_tensor("b1s", [128, 11], F32))
        b2s = b2.enter_context(nc.sbuf_tensor("b2s", [8, DM], BF16))

        @blk.gpsimd
        def _(g):
            g.wait_ge(lw.h, lw.n)
            g.dma_start(out=b2s[:, :], in_=b2r).then_inc(lw.h, 16)
            lw.n += 16

        lw_b2 = lw.n

        for tsub in range(8):
            for n4 in range(4):
                @blk.tensor
                def _(t, tsub=tsub, n4=n4):
                    t.wait_ge(lw.h, lw_b2)
                    t.wait_ge(ve.h, ve.n)
                    t.matmul(psB[:, :],
                             combT[0:8, tsub * 128:(tsub + 1) * 128],
                             b2s[0:8, n4 * 512:(n4 + 1) * 512],
                             start=True, stop=True).then_inc(pe.h, 1)
                    pe.n += 1

                pw = pe.n

                @blk.vector
                def _(v, tsub=tsub, n4=n4, pw=pw):
                    v.wait_ge(pe.h, pw)
                    v.tensor_copy(out=acc[:, tsub * DM + n4 * 512: tsub * DM + n4 * 512 + 512],
                                  in_=psB[:, :]).then_inc(ve.h, 1)
                    ve.n += 1

        for e in range(8):
            @blk.gpsimd
            def _(g, e=e):
                g.wait_ge(pe.h, pe.n)
                g.wait_ge(lw.h, lw.n)
                g.dma_start(out=w1s[:, :].rearrange("p (j f) -> p j f", j=16),
                            in_=w1T3[e]).then_inc(lw.h, 16)
                lw.n += 16

            lw_w1 = lw.n

            @blk.sync
            def _(s, e=e):
                s.wait_ge(se.h, se.n)
                s.dma_start(out=b1s[:, :], in_=b1c[e * 128:(e + 1) * 128, :]).then_inc(ld.h, 16)
                ld.n += 16

            ld_b1 = ld.n

            for fc in range(11):
                for tc in range(2):
                    @blk.tensor
                    def _(t, fc=fc, tc=tc, lw_w1=lw_w1):
                        t.wait_ge(lw.h, lw_w1)
                        t.wait_ge(se.h, se.n)
                        mm = None
                        for j in range(16):
                            mm = t.matmul(psA[:, :],
                                          w1s[:, j * 1408 + fc * 128: j * 1408 + fc * 128 + 128],
                                          xn[:, j * NT + tc * 512: j * NT + tc * 512 + 512],
                                          start=(j == 0), stop=(j == 15))
                        mm.then_inc(pe.h, 1)
                        pe.n += 1

                    pw = pe.n

                    @blk.scalar
                    def _(s, fc=fc, tc=tc, pw=pw, ld_b1=ld_b1):
                        s.wait_ge(pe.h, pw)
                        s.wait_ge(ld.h, ld_b1)
                        s.activation(out=h_fm[:, fc * NT + tc * 512: fc * NT + tc * 512 + 512],
                                     in_=psA[:, :], func=AF.Silu,
                                     bias=b1s[:, fc:fc + 1], scale=1.0).then_inc(se.h, 1)
                        se.n += 1

            hd_se = se.n

            for n4 in range(4):
                @blk.gpsimd
                def _(g, e=e, n4=n4):
                    g.wait_ge(pe.h, pe.n)
                    g.wait_ge(lw.h, lw.n)
                    g.dma_start(out=w2s[:, :].rearrange("p (j f) -> p j f", j=11),
                                in_=w2T3[e][:, :, n4 * 512:(n4 + 1) * 512]).then_inc(lw.h, 16)
                    lw.n += 16

                lw_w2 = lw.n

                for tsub in range(8):
                    @blk.tensor
                    def _(t, tsub=tsub, n4=n4, lw_w2=lw_w2, hd_se=hd_se):
                        t.wait_ge(lw.h, lw_w2)
                        t.wait_ge(se.h, hd_se)
                        t.wait_ge(ve.h, ve.n)
                        mm = None
                        for k in range(11):
                            mm = t.matmul(psB[:, :],
                                          h_fm[:, k * NT + tsub * 128: k * NT + tsub * 128 + 128],
                                          w2s[:, k * 512:(k + 1) * 512],
                                          start=(k == 0), stop=(k == 10))
                        mm.then_inc(pe.h, 1)
                        pe.n += 1

                    pw = pe.n

                    @blk.vector
                    def _(v, e=e, tsub=tsub, n4=n4, pw=pw):
                        v.wait_ge(pe.h, pw)
                        sl = slice(tsub * DM + n4 * 512, tsub * DM + n4 * 512 + 512)
                        v.scalar_tensor_tensor(out=acc[:, sl], in0=psB[:, :],
                                               scalar=comb_all[:, tsub * 8 + e: tsub * 8 + e + 1],
                                               in1=acc[:, sl], op0=ALU.mult, op1=ALU.add).then_inc(ve.h, 1)
                        ve.n += 1

        b2.close()

        b3 = ExitStack()
        psTR = b3.enter_context(nc.psum_tensor("psTR", [128, 128], BF16))
        xj = b3.enter_context(nc.sbuf_tensor("xj", [128, NT], BF16))
        s0j = b3.enter_context(nc.sbuf_tensor("s0j", [128, NT], F32))
        gfm = b3.enter_context(nc.sbuf_tensor("gfm", [128, NT], BF16))

        ve_accd = ve.n

        for t32 in range(32):
            @blk.scalar
            def _(s, t32=t32, ve_accd=ve_accd):
                if t32 == 0:
                    s.wait_ge(ve.h, ve_accd)
                s.activation(out=acc[:, t32 * 512:(t32 + 1) * 512],
                             in_=acc[:, t32 * 512:(t32 + 1) * 512],
                             func=AF.Silu, bias=zero_s[0:128, 0:1], scale=1.0).then_inc(se.h, 1)
                se.n += 1

        silu_done = se.n

        for fj in range(16):
            @blk.gpsimd
            def _(g, fj=fj):
                if fj > 0:
                    g.wait_ge(ve.h, ve.n)
                else:
                    g.wait_ge(pe.h, pe.n)
                g.dma_start(out=xj[:, :], in_=xT3[:, fj, :]).then_inc(lx.h, 16)
                lx.n += 16

            @blk.sync
            def _(s, fj=fj):
                if fj > 0:
                    s.wait_ge(ve.h, ve.n)
                else:
                    s.wait_ge(pe.h, pe.n)
                s.dma_start(out=s0j[:, :], in_=s0T3[:, fj, :]).then_inc(ld.h, 16)
                ld.n += 16

            lds = (lx.n, ld.n)

            for tsub in range(8):
                @blk.tensor
                def _(t, fj=fj, tsub=tsub, silu_done=silu_done):
                    t.wait_ge(se.h, silu_done)
                    if not (fj == 0 and tsub == 0):
                        t.wait_ge(ve.h, ve.n)
                    t.transpose(psTR[:, :], acc[:, tsub * DM + fj * 128: tsub * DM + fj * 128 + 128],
                                ident[:, :]).then_inc(pe.h, 1)
                    pe.n += 1

                pw = pe.n

                @blk.vector
                def _(v, fj=fj, tsub=tsub, pw=pw):
                    v.wait_ge(pe.h, pw)
                    v.tensor_copy(out=gfm[:, tsub * 128:(tsub + 1) * 128], in_=psTR[:, :]).then_inc(ve.h, 1)
                    ve.n += 1

            gfmd = ve.n

            @blk.vector
            def _(v, fj=fj, lds=lds, gfmd=gfmd):
                v.wait_ge(lx.h, lds[0])
                v.wait_ge(ld.h, lds[1])
                v.wait_ge(ve.h, gfmd)
                v.tensor_mul(out=gfm[:, :], in0=gfm[:, :], in1=s0j[:, :]).then_inc(ve.h, 1)
                ve.n += 1
                v.wait_ge(ve.h, ve.n)
                v.tensor_add(out=x1[:, fj * NT:(fj + 1) * NT], in0=gfm[:, :], in1=xj[:, :]).then_inc(ve.h, 1)
                ve.n += 1

            x1d = ve.n

            @blk.gpsimd
            def _(g, fj=fj, x1d=x1d):
                g.wait_ge(ve.h, x1d)
                g.dma_start(out=x1o3[:, fj, :], in_=x1[:, fj * NT:(fj + 1) * NT]).then_inc(lst.h, 16)
                lst.n += 16

        b3.close()

        b4 = ExitStack()
        psBC = b4.enter_context(nc.psum_tensor("psBC", [128, 512], F32))
        psBC2 = b4.enter_context(nc.psum_tensor("psBC2", [128, 512], F32))
        sqb = b4.enter_context(nc.sbuf_tensor("sqb", [128, 1024], BF16))
        xn1 = xn

        pe_sq_prev = 0
        for tc in range(2):
            for j in range(16):
                sqw = pe.n if j >= 2 else pe_sq_prev

                @blk.vector
                def _(v, tc=tc, j=j, sqw=sqw):
                    v.wait_ge(pe.h, sqw)
                    if j == 0:
                        v.wait_ge(ve.h, ve.n)
                    v.tensor_mul(out=sqb[:, (j % 2) * 512:(j % 2 + 1) * 512],
                                 in0=x1[:, j * NT + tc * 512: j * NT + tc * 512 + 512],
                                 in1=x1[:, j * NT + tc * 512: j * NT + tc * 512 + 512]).then_inc(ve.h, 1)
                    ve.n += 1

                vw = ve.n

                @blk.tensor
                def _(t, tc=tc, j=j, vw=vw):
                    t.wait_ge(ve.h, vw)
                    t.matmul(psS1[:, :], ones_bf[:, :],
                             x1[:, j * NT + tc * 512: j * NT + tc * 512 + 512],
                             start=(j == 0), stop=(j == 15))
                    t.matmul(psS2[:, :], ones_bf[:, :], sqb[:, (j % 2) * 512:(j % 2 + 1) * 512],
                             start=(j == 0), stop=(j == 15)).then_inc(pe.h, 1)
                    pe.n += 1

            pe_sq_prev = pe.n

            ladder_b(blk, psBC, psBC2, 512, 1.0 / DM, 1.0)
            abd = ve.n

            @blk.vector
            def _(v, tc=tc, abd=abd):
                v.wait_ge(ve.h, abd)
                for j in range(16):
                    if j > 0:
                        v.wait_ge(ve.h, ve.n)
                    v.tensor_mul(out=sqb[:, 0:512],
                                 in0=x1[:, j * NT + tc * 512: j * NT + tc * 512 + 512],
                                 in1=ab_bc[:, 0:512]).then_inc(ve.h, 1)
                    ve.n += 1
                    v.wait_ge(ve.h, ve.n)
                    v.tensor_add(out=xn1[:, j * NT + tc * 512: j * NT + tc * 512 + 512],
                                 in0=sqb[:, 0:512], in1=ab_bc[:, 512:1024]).then_inc(ve.h, 1)
                    ve.n += 1

            xn1d = ve.n

            @blk.gpsimd
            def _(g, tc=tc, xn1d=xn1d):
                g.wait_ge(ve.h, xn1d)
                g.dma_start(out=xn1o3[:, :, tc * 512:(tc + 1) * 512],
                            in_=xn1[:, :].rearrange("p (j t) -> p j t", j=16)[:, :, tc * 512:(tc + 1) * 512]).then_inc(lst.h, 16)
                lst.n += 16

        b4.close()

        b5 = ExitStack()
        qr = x1
        kr = b5.enter_context(nc.sbuf_tensor("kr", [128, 4 * NT], BF16))
        vck = b5.enter_context(nc.sbuf_tensor("vck", [64, 16 * 512], BF16))
        rot_s = b5.enter_context(nc.sbuf_tensor("rot_s", [128, 128], BF16))
        cos_s = b5.enter_context(nc.sbuf_tensor("cos_s", [128, NT], BF16))
        sin_s = b5.enter_context(nc.sbuf_tensor("sin_s", [128, NT], BF16))
        rbuf = b5.enter_context(nc.sbuf_tensor("rbuf", [128, NT], BF16))
        b5w = ExitStack()
        wqs = b5w.enter_context(nc.sbuf_tensor("wqs", [128, 16 * 1024], BF16))

        @blk.gpsimd
        def _(g):
            g.wait_ge(lw.h, lw.n)
            g.dma_start(out=rot_s[:, :], in_=rotT).then_inc(lw.h, 16)
            g.dma_start(out=cos_s[:, :], in_=cosf).then_inc(lw.h, 16)
            g.dma_start(out=sin_s[:, :], in_=sinf).then_inc(lw.h, 16)
            lw.n += 48

        lw_rope = lw.n

        lst_b4 = lst.n
        for half in range(2):
            @blk.gpsimd
            def _(g, half=half):
                g.wait_ge(pe.h, pe.n)
                g.wait_ge(lw.h, lw.n)
                g.dma_start(out=wqs[:, :].rearrange("p (j f) -> p j f", j=16),
                            in_=wq1T3[:, :, half * 1024:(half + 1) * 1024]).then_inc(lw.h, 16)
                lw.n += 16

            lw_wq = lw.n

            for hh in range(8):
                head = half * 8 + hh
                for tc in range(2):
                    @blk.tensor
                    def _(t, hh=hh, head=head, tc=tc, lw_wq=lw_wq):
                        t.wait_ge(lw.h, lw_wq)
                        t.wait_ge(ve.h, ve.n)
                        mm = None
                        for j in range(16):
                            mm = t.matmul(psA[:, :],
                                          wqs[:, j * 1024 + hh * 128: j * 1024 + hh * 128 + 128],
                                          xn1[:, j * NT + tc * 512: j * NT + tc * 512 + 512],
                                          start=(j == 0), stop=(j == 15))
                        mm.then_inc(pe.h, 1)
                        pe.n += 1

                    pw = pe.n

                    @blk.vector
                    def _(v, head=head, tc=tc, pw=pw):
                        v.wait_ge(pe.h, pw)
                        v.wait_ge(lst.h, lst_b4)
                        v.tensor_copy(out=qr[:, head * NT + tc * 512: head * NT + tc * 512 + 512],
                                      in_=psA[:, :]).then_inc(ve.h, 1)
                        ve.n += 1

        @blk.gpsimd
        def _(g):
            g.wait_ge(pe.h, pe.n)
            g.wait_ge(lw.h, lw.n)
            g.dma_start(out=wqs[:, 0:16 * 512].rearrange("p (j f) -> p j f", j=16),
                        in_=wk1T3).then_inc(lw.h, 16)
            g.dma_start(out=wqs[:, 16 * 512:16 * 1024].rearrange("p (j f) -> p j f", j=16),
                        in_=wv1T3).then_inc(lw.h, 16)
            lw.n += 32

        lw_wkv = lw.n

        for g4 in range(4):
            for tc in range(2):
                @blk.tensor
                def _(t, g4=g4, tc=tc):
                    t.wait_ge(lw.h, lw_wkv)
                    t.wait_ge(ve.h, ve.n)
                    mm = None
                    for j in range(16):
                        mm = t.matmul(psA[:, :],
                                      wqs[:, j * 512 + g4 * 128: j * 512 + g4 * 128 + 128],
                                      xn1[:, j * NT + tc * 512: j * NT + tc * 512 + 512],
                                      start=(j == 0), stop=(j == 15))
                    mm.then_inc(pe.h, 1)
                    pe.n += 1

                pw = pe.n

                @blk.vector
                def _(v, g4=g4, tc=tc, pw=pw):
                    v.wait_ge(pe.h, pw)
                    v.tensor_copy(out=kr[:, g4 * NT + tc * 512: g4 * NT + tc * 512 + 512],
                                  in_=psA[:, :]).then_inc(ve.h, 1)
                    ve.n += 1

        for tsub in range(16):
            @blk.tensor
            def _(t, tsub=tsub):
                t.wait_ge(lw.h, lw_wkv)
                t.wait_ge(ve.h, ve.n)
                mm = None
                for j in range(16):
                    mm = t.matmul(psB[0:64, :],
                                  xn1[:, j * NT + tsub * 64: j * NT + tsub * 64 + 64],
                                  wqs[:, 16 * 512 + j * 512: 16 * 512 + j * 512 + 512],
                                  start=(j == 0), stop=(j == 15))
                mm.then_inc(pe.h, 1)
                pe.n += 1

            pw = pe.n

            @blk.vector
            def _(v, tsub=tsub, pw=pw):
                v.wait_ge(pe.h, pw)
                v.tensor_copy(out=vck[0:64, tsub * 512:(tsub + 1) * 512], in_=psB[0:64, :]).then_inc(ve.h, 1)
                ve.n += 1

        vprj = pe.n

        @blk.gpsimd
        def _(g):
            g.wait_ge(pe.h, vprj)
            g.wait_ge(ve.h, ve.n)

        b5w.close()

        def rope_one(blk, buf, hidx, dst_dram):
            @blk.tensor
            def _(t, hidx=hidx):
                t.wait_ge(lw.h, lw_rope)
                t.wait_ge(ve.h, ve.n)
                t.matmul(psA[:, :], rot_s[:, :], buf[:, hidx * NT: hidx * NT + 512],
                         start=True, stop=True)
                t.matmul(psB[:, :], rot_s[:, :], buf[:, hidx * NT + 512: hidx * NT + 1024],
                         start=True, stop=True).then_inc(pe.h, 1)
                pe.n += 1

            pw = pe.n

            @blk.vector
            def _(v, hidx=hidx, pw=pw):
                v.wait_ge(pe.h, pw)
                v.tensor_copy(out=rbuf[:, 0:512], in_=psA[:, :])
                v.tensor_copy(out=rbuf[:, 512:1024], in_=psB[:, :]).then_inc(ve.h, 1)
                ve.n += 1
                v.wait_ge(ve.h, ve.n)
                v.tensor_mul(out=rbuf[:, :], in0=rbuf[:, :], in1=sin_s[:, :]).then_inc(ve.h, 1)
                ve.n += 1
                v.wait_ge(ve.h, ve.n)
                v.tensor_mul(out=buf[:, hidx * NT:(hidx + 1) * NT],
                             in0=buf[:, hidx * NT:(hidx + 1) * NT], in1=cos_s[:, :]).then_inc(ve.h, 1)
                ve.n += 1
                v.wait_ge(ve.h, ve.n)
                v.tensor_add(out=buf[:, hidx * NT:(hidx + 1) * NT],
                             in0=buf[:, hidx * NT:(hidx + 1) * NT], in1=rbuf[:, :]).then_inc(ve.h, 1)
                ve.n += 1

            rd = ve.n

            if dst_dram is not None:
                @blk.gpsimd
                def _(g, hidx=hidx, rd=rd):
                    g.wait_ge(ve.h, rd)
                    g.dma_start(out=dst_dram[:, hidx, :],
                                in_=buf[:, hidx * NT:(hidx + 1) * NT]).then_inc(lst.h, 16)
                    lst.n += 16

        for head in range(16):
            rope_one(blk, qr, head, qro3.rearrange("p j t -> p j t") if False else qro3)
        for g4 in range(4):
            rope_one(blk, kr, g4, None)

        b7 = ExitStack()
        psO0 = b7.enter_context(nc.psum_tensor("psO0", [128, 512], F32))
        psO1 = b7.enter_context(nc.psum_tensor("psO1", [128, 512], F32))
        psTR2 = b7.enter_context(nc.psum_tensor("psTR2", [128, 128], BF16))
        ktm = acc
        oloc = xn
        kd_bf = b7.enter_context(nc.sbuf_tensor("kd_bf", [64, 128], BF16))
        qd_bf = b7.enter_context(nc.sbuf_tensor("qd_bf", [128, CH], BF16))
        p_bf = b7.enter_context(nc.sbuf_tensor("p_bf", [64, CH], BF16))
        state_f = b7.enter_context(nc.sbuf_tensor("state_f", [128, 128], F32))
        state_b = b7.enter_context(nc.sbuf_tensor("state_b", [128, 128], BF16))
        kdec_s = b7.enter_context(nc.sbuf_tensor("kdec_s", [CH, 16], F32))
        QD0, DD0 = 8192, 9216

        @blk.gpsimd
        def _(g):
            g.wait_ge(lw.h, lw.n)
            g.wait_ge(ve.h, ve.n)
            g.dma_start(out=acc[0:64, DD0:DD0 + 16 * CH], in_=ddiag).then_inc(lw.h, 16)
            g.dma_start(out=acc[:, QD0:QD0 + 16 * CH], in_=qdecw).then_inc(lw.h, 16)
            lw.n += 32

        @blk.sync
        def _(s):
            s.wait_ge(ld.h, ld.n)
            s.wait_ge(pe.h, pe.n)
            s.dma_start(out=kdec_s[:, :], in_=kdecw).then_inc(ld.h, 16)
            ld.n += 16

        lw_dec = lw.n
        ld_dec = ld.n

        for g4 in range(4):
            for c in range(NCH):
                @blk.tensor
                def _(t, g4=g4, c=c):
                    if not (g4 == 0 and c == 0):
                        t.wait_ge(ve.h, ve.n)
                    else:
                        t.wait_ge(ve.h, ve.n)
                    t.transpose(psTR2[0:64, 0:128],
                                kr[:, g4 * NT + c * CH: g4 * NT + (c + 1) * CH],
                                ident[:, :]).then_inc(pe.h, 1)
                    pe.n += 1

                pw = pe.n

                @blk.vector
                def _(v, g4=g4, c=c, pw=pw):
                    v.wait_ge(pe.h, pw)
                    v.tensor_copy(out=ktm[0:64, (g4 * NCH + c) * 128:(g4 * NCH + c) * 128 + 128],
                                  in_=psTR2[0:64, 0:128]).then_inc(ve.h, 1)
                    ve.n += 1

        for h in range(16):
            g4 = h // 4
            blkf = _L1_BLK[h]

            @blk.vector
            def _(v):
                v.wait_ge(ve.h, ve.n)
                v.memset(state_f[:, :], 0.0)
                v.memset(state_b[:, :], 0.0).then_inc(ve.h, 1)
                ve.n += 1

            for c in range(NCH):
                po = psO0 if (c // 8) % 2 == 0 else psO1
                vcol = c * 512 + g4 * 128

                @blk.vector
                def _(v, h=h, g4=g4, c=c):
                    v.wait_ge(lw.h, lw_dec)
                    v.wait_ge(ld.h, ld_dec)
                    v.wait_ge(ve.h, ve.n)
                    v.tensor_single_scalar(out=kd_bf[0:64, :],
                                           in_=ktm[0:64, (g4 * NCH + c) * 128:(g4 * NCH + c) * 128 + 128],
                                           scalar=kdec_s[0:64, h:h + 1], op=ALU.mult)
                    v.tensor_mul(out=qd_bf[:, :],
                                 in0=qr[:, h * NT + c * CH: h * NT + (c + 1) * CH],
                                 in1=acc[:, QD0 + h * CH: QD0 + (h + 1) * CH]).then_inc(ve.h, 1)
                    ve.n += 1

                kqd = ve.n

                @blk.tensor
                def _(t, h=h, g4=g4, c=c):
                    t.wait_ge(ve.h, ve.n)
                    t.matmul(psA[0:64, 0:64],
                             kr[:, g4 * NT + c * CH: g4 * NT + (c + 1) * CH],
                             qr[:, h * NT + c * CH: h * NT + (c + 1) * CH],
                             start=True, stop=True).then_inc(pe.h, 1)
                    pe.n += 1

                pw = pe.n

                @blk.vector
                def _(v, h=h, pw=pw):
                    v.wait_ge(pe.h, pw)
                    v.wait_ge(ve.h, ve.n)
                    v.tensor_mul(out=p_bf[0:64, :], in0=psA[0:64, 0:64],
                                 in1=acc[0:64, DD0 + h * CH: DD0 + (h + 1) * CH]).then_inc(ve.h, 1)
                    ve.n += 1

                pd = ve.n

                @blk.tensor
                def _(t, c=c, po=po, vcol=vcol, pd=pd, kqd=kqd):
                    t.wait_ge(ve.h, pd)
                    t.matmul(po[:, (c % 8) * CH:(c % 8 + 1) * CH],
                             vck[0:64, vcol:vcol + 128],
                             p_bf[0:64, :], start=(c % 8 == 0), stop=False)
                    t.matmul(po[:, (c % 8) * CH:(c % 8 + 1) * CH],
                             state_b[:, :], qd_bf[:, :], start=False, stop=(c % 8 == 7))
                    t.matmul(psB[:, 0:128], kd_bf[0:64, :], vck[0:64, vcol:vcol + 128],
                             start=True, stop=True).then_inc(pe.h, 1)
                    pe.n += 1

                pw2 = pe.n

                @blk.vector
                def _(v, pw2=pw2, blkf=blkf):
                    v.wait_ge(pe.h, pw2)
                    v.scalar_tensor_tensor(out=state_f[:, :], in0=state_f[:, :], scalar=float(blkf),
                                           in1=psB[:, 0:128], op0=ALU.mult, op1=ALU.add).then_inc(ve.h, 1)
                    ve.n += 1
                    v.wait_ge(ve.h, ve.n)
                    v.tensor_copy(out=state_b[:, :], in_=state_f[:, :]).then_inc(ve.h, 1)
                    ve.n += 1

                if c % 8 == 7:
                    pw3 = pe.n

                    @blk.vector
                    def _(v, h=h, c=c, po=po, pw3=pw3):
                        v.wait_ge(pe.h, pw3)
                        v.tensor_copy(out=oloc[:, h * NT + (c // 8) * 512: h * NT + (c // 8) * 512 + 512],
                                      in_=po[:, :]).then_inc(ve.h, 1)
                        ve.n += 1

            std = ve.n

            @blk.gpsimd
            def _(g, h=h, std=std):
                g.wait_ge(ve.h, std)
                g.wait_ge(lst.h, lst.n)
                g.dma_start(out=st_out[:, h * 128:(h + 1) * 128], in_=state_f[:, :]).then_inc(lst.h, 16)
                lst.n += 16

            stf = lst.n

            @blk.vector
            def _(v, stf=stf):
                v.wait_ge(lst.h, stf)

        old = ve.n

        @blk.gpsimd
        def _(g, old=old):
            g.wait_ge(ve.h, old)
            g.wait_ge(lst.h, lst.n)
            g.dma_start(out=oloco3, in_=oloc[:, :].rearrange("p (j t) -> p j t", j=16)).then_inc(lst.h, 16)
            lst.n += 16

        b7.close()
        b5.close()

        @blk.gpsimd
        def _(g):
            g.wait_ge(lst.h, lst.n)
            g.wait_ge(ld.h, ld.n)

    return nc


def prep_launch_b(inputs, xn_full, s0_full):
    x = np.asarray(inputs["L0_attn_hidden_states"], np.float32)
    gwT = np.ascontiguousarray(np.asarray(inputs["L0_ffn_gate_w"], np.float32).T)
    gb1 = np.ascontiguousarray(np.asarray(inputs["L0_ffn_gate_b"], np.float32).reshape(1, 8))
    gb2 = np.ascontiguousarray(np.asarray(inputs["L0_ffn_bias"], np.float32).reshape(1, 8))
    w1 = np.asarray(inputs["L0_ffn_w1"], np.float32)
    b1 = np.asarray(inputs["L0_ffn_b1"], np.float32)
    w2 = np.asarray(inputs["L0_ffn_w2"], np.float32)
    b2 = np.asarray(inputs["L0_ffn_b2"], np.float32)
    w1T = np.ascontiguousarray(w1.transpose(0, 2, 1).reshape(8 * DM, 1408))
    w2T = np.ascontiguousarray(w2.transpose(0, 2, 1).reshape(8 * 1408, DM))
    b1c = np.ascontiguousarray(b1.reshape(8, 11, 128).transpose(0, 2, 1).reshape(8 * 128, 11))
    wq1T = np.ascontiguousarray(np.asarray(inputs["L1_attn_wq"], np.float32).T)
    wk1T = np.ascontiguousarray(np.asarray(inputs["L1_attn_wk"], np.float32).T)
    wv1T = np.ascontiguousarray(np.asarray(inputs["L1_attn_wv"], np.float32).T)
    identw = np.eye(128, dtype=np.float32)
    rotT = np.zeros((128, 128), np.float32)
    for i in range(64):
        rotT[i + 64, i] = -1.0
        rotT[i, i + 64] = 1.0
    lam = np.exp2(-8.0 * np.arange(1, 17) / 16.0).astype(np.float64)
    r = np.arange(CH, dtype=np.float64)
    ddiag = np.zeros((CH, 16 * CH), np.float32)
    qdec = np.zeros((128, 16 * CH), np.float32)
    kdec = np.zeros((CH, 16), np.float32)
    for h in range(16):
        diff = r[None, :] - r[:, None]
        d = np.where(diff >= 0, np.exp(-lam[h] * diff), 0.0)
        ddiag[:, h * CH:(h + 1) * CH] = d
        qdec[:, h * CH:(h + 1) * CH] = np.exp(-lam[h] * (r + 1.0))[None, :]
        kdec[:, h] = np.exp(-lam[h] * (CH - 1.0 - r))
    inv_freq = 1.0 / 10000.0 ** (np.arange(0, 128, 2, dtype=np.float64) / 128.0)

    in_maps = []
    for c in range(NCORE):
        b, rk_ = c // 4, c % 4
        tok = slice(rk_ * NT, (rk_ + 1) * NT)
        pos = np.arange(rk_ * NT, (rk_ + 1) * NT, dtype=np.float64)
        freqs = pos[None, :] * np.concatenate([inv_freq, inv_freq])[:, None]
        in_maps.append({
            "xT": np.ascontiguousarray(x[b].T[:, tok]),
            "xnT": np.ascontiguousarray(xn_full[c][:, tok]),
            "s0T": np.ascontiguousarray(s0_full[c]),
            "gwT": gwT, "gb1": gb1, "gb2": gb2,
            "w1T": w1T, "b1c": b1c, "w2T": w2T, "b2r": b2,
            "wq1T": wq1T, "wk1T": wk1T, "wv1T": wv1T,
            "cosf": np.ascontiguousarray(np.cos(freqs).astype(np.float32)),
            "sinf": np.ascontiguousarray(np.sin(freqs).astype(np.float32)),
            "rotT": rotT, "identw": identw,
            "ddiag": ddiag, "qdecw": qdec, "kdecw": kdec,
        })
    return in_maps



def build_launch_d():
    nc = bass.Bass("TRN2", target_bir_lowering=False, debug=False)
    ap = lambda n, s, d, k: nc.dram_tensor(n, s, d, kind=k).ap()
    x1T = ap("x1T", [DM, NT], BF16, "ExternalInput")
    xn1T = ap("xn1T", [DM, NT], BF16, "ExternalInput")
    qrT = ap("qrT", [DM, NT], BF16, "ExternalInput")
    olocT = ap("olocT", [DM, NT], BF16, "ExternalInput")
    stp = ap("stp", [128, 3 * 2048], F32, "ExternalInput")
    dfac = ap("dfac", [128, 3 * 16], F32, "ExternalInput")
    qdecg = ap("qdecg", [128, 16 * NT], F32, "ExternalInput")
    gwTf = ap("gwTf", [DM, INTER], BF16, "ExternalInput")
    uwTf = ap("uwTf", [DM, INTER], BF16, "ExternalInput")
    dwTf = ap("dwTf", [INTER, DM], BF16, "ExternalInput")
    wo1T = ap("wo1T", [DM, DM], F32, "ExternalInput")
    x2o = ap("x2o", [DM, NT], F32, "ExternalOutput")

    r3 = lambda t: t.rearrange("(j p) n -> p j n", p=128)
    x1T3, xn1T3, qrT3, olocT3, wo1T3 = r3(x1T), r3(xn1T), r3(qrT), r3(olocT), r3(wo1T)
    qdecg3 = qdecg
    gwTf3 = r3(gwTf)
    uwTf3 = r3(uwTf)
    dwTf3 = r3(dwTf)
    x2o3 = r3(x2o)

    es = ExitStack()
    sb = lambda n, s, d: es.enter_context(nc.sbuf_tensor(n, s, d))
    pst = lambda n, s, d: es.enter_context(nc.psum_tensor(n, s, d))
    mksem = lambda n: Sem(es.enter_context(nc.semaphore(n)))

    zero_s = sb("zero_s", [128, 1], F32)
    s1 = sb("s1", [128, 16 * NT], BF16)

    psA = pst("psA", [128, 512], F32)
    psB = pst("psB", [128, 512], F32)

    ld = mksem("ld")
    lw = mksem("lw")
    pe = mksem("pe")
    ve = mksem("ve")
    se = mksem("se")
    so = mksem("so")

    with es, nc.Block() as blk:
        d12 = ExitStack()
        on = d12.enter_context(nc.sbuf_tensor("on", [128, 16 * NT], BF16))
        d1 = ExitStack()
        qr = d1.enter_context(nc.sbuf_tensor("qr", [128, 16 * NT], BF16))
        oloc = d1.enter_context(nc.sbuf_tensor("oloc", [128, 16 * NT], BF16))
        qdg = d1.enter_context(nc.sbuf_tensor("qdg", [128, 16 * NT], BF16))
        sts = d1.enter_context(nc.sbuf_tensor("sts", [128, 3 * 2048], F32))
        dfc = d1.enter_context(nc.sbuf_tensor("dfc", [128, 3 * 16], F32))
        s_t = d1.enter_context(nc.sbuf_tensor("s_t", [128, 3 * 128], F32))
        s_bf = d1.enter_context(nc.sbuf_tensor("s_bf", [128, 128], BF16))
        qd_bf = d1.enter_context(nc.sbuf_tensor("qd_bf", [128, NT], BF16))

        @blk.vector
        def _(v):
            v.memset(zero_s[:, :], 0.0).then_inc(ve.h, 1)
            ve.n += 1

        @blk.sync
        def _(s):
            s.dma_start(out=qr[:, :].rearrange("p (j t) -> p j t", j=16), in_=qrT3).then_inc(ld.h, 16)
            s.dma_start(out=oloc[:, :].rearrange("p (j t) -> p j t", j=16), in_=olocT3).then_inc(ld.h, 16)
            s.dma_start(out=sts[:, :], in_=stp).then_inc(ld.h, 16)
            s.dma_start(out=dfc[:, :], in_=dfac).then_inc(ld.h, 16)
            ld.n += 64

        @blk.gpsimd
        def _(g):
            g.dma_start(out=qdg[:, :], in_=qdecg3).then_inc(lw.h, 16)
            lw.n += 16

        ld_d1 = ld.n
        lw_d1 = lw.n

        for h in range(16):
            @blk.vector
            def _(v, h=h):
                v.wait_ge(ld.h, ld_d1)
                v.wait_ge(lw.h, lw_d1)
                v.wait_ge(pe.h, pe.n)
                for j in range(3):
                    v.tensor_single_scalar(out=s_t[:, j * 128:(j + 1) * 128],
                                           in_=sts[:, (j * 16 + h) * 128:(j * 16 + h) * 128 + 128],
                                           scalar=dfc[:, j * 16 + h: j * 16 + h + 1],
                                           op=ALU.mult)
                v.engine_nop().then_inc(ve.h, 1)
                ve.n += 1
                v.wait_ge(ve.h, ve.n)
                v.tensor_add(out=s_t[:, 0:128], in0=s_t[:, 0:128], in1=s_t[:, 128:256]).then_inc(ve.h, 1)
                ve.n += 1
                v.wait_ge(ve.h, ve.n)
                v.tensor_add(out=s_t[:, 0:128], in0=s_t[:, 0:128], in1=s_t[:, 256:384]).then_inc(ve.h, 1)
                ve.n += 1
                v.wait_ge(ve.h, ve.n)
                v.tensor_copy(out=s_bf[:, :], in_=s_t[:, 0:128])
                v.tensor_mul(out=qd_bf[:, :], in0=qr[:, h * NT:(h + 1) * NT],
                             in1=qdg[:, h * NT:(h + 1) * NT]).then_inc(ve.h, 1)
                ve.n += 1

            kqd = ve.n

            for tc in range(2):
                @blk.tensor
                def _(t, h=h, tc=tc, kqd=kqd):
                    t.wait_ge(ve.h, kqd if tc == 0 else ve.n)
                    t.matmul(psA[:, :] if tc == 0 else psB[:, :],
                             s_bf[:, :], qd_bf[:, tc * 512:(tc + 1) * 512],
                             start=True, stop=True).then_inc(pe.h, 1)
                    pe.n += 1

                pw = pe.n

                @blk.vector
                def _(v, h=h, tc=tc, pw=pw):
                    v.wait_ge(pe.h, pw)
                    sl = slice(h * NT + tc * 512, h * NT + tc * 512 + 512)
                    v.tensor_add(out=on[:, sl], in0=oloc[:, sl],
                                 in1=(psA if tc == 0 else psB)[:, :]).then_inc(ve.h, 1)
                    ve.n += 1

        d1.close()

        d2 = ExitStack()
        wo_f = d2.enter_context(nc.sbuf_tensor("wo_f", [128, 16 * 2048], BF16))

        @blk.gpsimd
        def _(g):
            g.wait_ge(pe.h, pe.n)
            g.wait_ge(ve.h, ve.n)
            g.wait_ge(lw.h, lw.n)
            g.dma_start(out=wo_f[:, :].rearrange("p (j f) -> p j f", j=16),
                        in_=wo1T3).then_inc(lw.h, 16)
            lw.n += 16

        lw_wo = lw.n

        for dj in range(16):
            for tc in range(2):
                @blk.tensor
                def _(t, dj=dj, tc=tc):
                    t.wait_ge(lw.h, lw_wo)
                    t.wait_ge(ve.h, ve.n)
                    mm = None
                    for e in range(16):
                        mm = t.matmul(psA[:, :],
                                      wo_f[:, e * 2048 + dj * 128: e * 2048 + dj * 128 + 128],
                                      on[:, e * NT + tc * 512: e * NT + tc * 512 + 512],
                                      start=(e == 0), stop=(e == 15))
                    mm.then_inc(pe.h, 1)
                    pe.n += 1

                pw = pe.n

                @blk.vector
                def _(v, dj=dj, tc=tc, pw=pw):
                    v.wait_ge(pe.h, pw)
                    v.tensor_copy(out=s1[:, dj * NT + tc * 512: dj * NT + tc * 512 + 512],
                                  in_=psA[:, :]).then_inc(ve.h, 1)
                    ve.n += 1

        d2.close()
        d12.close()

        d3 = ExitStack()
        x1 = d3.enter_context(nc.sbuf_tensor("x1", [128, 16 * NT], BF16))
        xn1 = d3.enter_context(nc.sbuf_tensor("xn1", [128, 16 * NT], BF16))
        h1 = d3.enter_context(nc.sbuf_tensor("h1", [128, 43 * 512], BF16))
        wg = [d3.enter_context(nc.sbuf_tensor(f"wg{i}", [128, 16 * 128], BF16)) for i in range(2)]
        wu = [d3.enter_context(nc.sbuf_tensor(f"wu{i}", [128, 16 * 128], BF16)) for i in range(2)]
        wd = [d3.enter_context(nc.sbuf_tensor(f"wd{i}", [128, 43 * 128], BF16)) for i in range(2)]
        t1b = d3.enter_context(nc.sbuf_tensor("t1b", [128, 512], BF16))
        gfb = d3.enter_context(nc.sbuf_tensor("gfb", [128, 512], BF16))
        x2f = d3.enter_context(nc.sbuf_tensor("x2f", [128, 512], F32))

        @blk.sync
        def _(s):
            s.wait_ge(pe.h, pe.n)
            s.dma_start(out=x1[:, :].rearrange("p (j t) -> p j t", j=16), in_=x1T3).then_inc(ld.h, 16)
            s.dma_start(out=xn1[:, :].rearrange("p (j t) -> p j t", j=16), in_=xn1T3).then_inc(ld.h, 16)
            ld.n += 32

        ld_d3 = ld.n

        for tg in range(2):
            for oc in range(43):
                bi = oc % 2

                @blk.sync
                def _(s, tg=tg, oc=oc, bi=bi):
                    s.wait_ge(ld.h, ld.n)
                    if tg > 0 or oc >= 2:
                        s.wait_ge(ve.h, ve.n)
                    if tg == 0 and oc == 0:
                        s.wait_ge(pe.h, pe.n)
                    s.dma_start(out=wg[bi][:, :].rearrange("p (j f) -> p j f", j=16),
                                in_=gwTf3[:, :, oc * 128:(oc + 1) * 128]).then_inc(ld.h, 16)
                    s.dma_start(out=wu[bi][:, :].rearrange("p (j f) -> p j f", j=16),
                                in_=uwTf3[:, :, oc * 128:(oc + 1) * 128]).then_inc(ld.h, 16)
                    ld.n += 32

                ldw = ld.n

                @blk.tensor
                def _(t, tg=tg, oc=oc, bi=bi, ldw=ldw):
                    t.wait_ge(ld.h, ldw)
                    t.wait_ge(se.h, se.n)
                    if tg == 0 and oc == 0:
                        t.wait_ge(ve.h, ve.n)
                    mm = None
                    for j in range(16):
                        mm = t.matmul(psA[:, :],
                                      wg[bi][:, j * 128:(j + 1) * 128],
                                      xn1[:, j * NT + tg * 512: j * NT + tg * 512 + 512],
                                      start=(j == 0), stop=(j == 15))
                    mm.then_inc(pe.h, 1)
                    pe.n += 1

                pwg = pe.n

                @blk.scalar
                def _(s, pwg=pwg):
                    s.wait_ge(pe.h, pwg)
                    s.wait_ge(ve.h, ve.n)
                    s.activation(out=t1b[:, :], in_=psA[:, :], func=AF.Silu,
                                 bias=zero_s[:, 0:1], scale=1.0).then_inc(se.h, 1)
                    se.n += 1

                swg = se.n

                @blk.tensor
                def _(t, tg=tg, oc=oc, bi=bi, ldw=ldw):
                    t.wait_ge(ld.h, ldw)
                    t.wait_ge(ve.h, ve.n)
                    mm = None
                    for j in range(16):
                        mm = t.matmul(psB[:, :],
                                      wu[bi][:, j * 128:(j + 1) * 128],
                                      xn1[:, j * NT + tg * 512: j * NT + tg * 512 + 512],
                                      start=(j == 0), stop=(j == 15))
                    mm.then_inc(pe.h, 1)
                    pe.n += 1

                pwu = pe.n

                @blk.vector
                def _(v, oc=oc, pwu=pwu, swg=swg):
                    v.wait_ge(pe.h, pwu)
                    v.wait_ge(se.h, swg)
                    v.tensor_mul(out=h1[:, oc * 512:(oc + 1) * 512],
                                 in0=t1b[:, :], in1=psB[:, :]).then_inc(ve.h, 1)
                    ve.n += 1

            for fo in range(16):
                bi = fo % 2

                @blk.gpsimd
                def _(g, tg=tg, fo=fo, bi=bi):
                    g.wait_ge(lw.h, lw.n)
                    if tg > 0 or fo >= 2:
                        g.wait_ge(pe.h, pe.n)
                    g.dma_start(out=wd[bi][:, :].rearrange("p (j f) -> p j f", j=43),
                                in_=dwTf3[:, :, fo * 128:(fo + 1) * 128]).then_inc(lw.h, 16)
                    lw.n += 16

                ldd = lw.n

                @blk.tensor
                def _(t, tg=tg, fo=fo, bi=bi, ldd=ldd):
                    t.wait_ge(lw.h, ldd)
                    t.wait_ge(se.h, se.n)
                    t.wait_ge(ve.h, ve.n)
                    mm = None
                    for kc in range(43):
                        mm = t.matmul(psA[:, :],
                                      wd[bi][:, kc * 128:(kc + 1) * 128],
                                      h1[:, kc * 512:(kc + 1) * 512],
                                      start=(kc == 0), stop=(kc == 42))
                    mm.then_inc(pe.h, 1)
                    pe.n += 1

                pwd = pe.n

                @blk.scalar
                def _(s, pwd=pwd):
                    s.wait_ge(pe.h, pwd)
                    s.wait_ge(ve.h, ve.n)
                    s.activation(out=gfb[:, :], in_=psA[:, :], func=AF.Silu,
                                 bias=zero_s[:, 0:1], scale=1.0).then_inc(se.h, 1)
                    se.n += 1

                swd = se.n

                @blk.vector
                def _(v, tg=tg, fo=fo, swd=swd):
                    v.wait_ge(se.h, swd)
                    v.wait_ge(so.h, so.n)
                    sl = slice(fo * NT + tg * 512, fo * NT + tg * 512 + 512)
                    v.tensor_mul(out=gfb[:, :], in0=gfb[:, :], in1=s1[:, sl]).then_inc(ve.h, 1)
                    ve.n += 1
                    v.wait_ge(ve.h, ve.n)
                    v.tensor_add(out=x2f[:, :], in0=gfb[:, :], in1=x1[:, sl]).then_inc(ve.h, 1)
                    ve.n += 1

                x2d = ve.n

                @blk.gpsimd
                def _(g, tg=tg, fo=fo, x2d=x2d):
                    g.wait_ge(ve.h, x2d)
                    g.dma_start(out=x2o3[:, fo, tg * 512:(tg + 1) * 512],
                                in_=x2f[:, :]).then_inc(so.h, 16)
                    so.n += 16

        d3.close()

        @blk.gpsimd
        def _(g):
            g.wait_ge(so.h, so.n)

    return nc


def prep_launch_d(inputs, b_outs, ghost):
    import ml_dtypes
    wo1T = np.ascontiguousarray(np.asarray(inputs["L1_attn_wo"], np.float32).T)
    lam = np.exp2(-8.0 * np.arange(1, 17) / 16.0).astype(np.float64)
    qdecg = np.zeros((128, 16 * NT), np.float32)
    p = np.arange(NT, dtype=np.float64)
    for h in range(16):
        qdecg[:, h * NT:(h + 1) * NT] = np.exp(-lam[h] * (p + 1.0))[None, :]
    gwT_full, uwT_full, dwT_full = ghost
    in_maps = []
    for c in range(NCORE):
        b, r = c // 4, c % 4
        stp = np.zeros((128, 3 * 2048), np.float32)
        dfac = np.zeros((128, 3 * 16), np.float32)
        for j in range(r):
            src = b_outs[b * 4 + j]["st_out"]
            stp[:, j * 2048:(j + 1) * 2048] = src
            for h in range(16):
                dfac[:, j * 16 + h] = np.exp(-lam[h] * 1024.0 * (r - 1 - j))
        in_maps.append({
            "x1T": b_outs[c]["x1_out"], "xn1T": b_outs[c]["xn1_out"],
            "qrT": b_outs[c]["qr_out"], "olocT": b_outs[c]["oloc_out"],
            "stp": stp, "dfac": dfac, "qdecg": qdecg,
            "gwTf": gwT_full, "uwTf": uwT_full, "dwTf": dwT_full,
            "wo1T": wo1T,
        })
    return in_maps



_NC_CACHE = {}


def kernel(**inputs):
    if "A" not in _NC_CACHE:
        _NC_CACHE["A"] = build_launch_a()
        _NC_CACHE["C"] = build_launch_c()
        _NC_CACHE["B"] = build_launch_b()
        _NC_CACHE["D"] = build_launch_d()

    res_a = run_bass_kernel_spmd(_NC_CACHE["A"], prep_launch_a(inputs), core_ids=list(range(8)))
    res_c = run_bass_kernel_spmd(_NC_CACHE["C"], prep_launch_c(inputs), core_ids=list(range(8)))

    xn_full = [res_a.results[c]["xn_out"] for c in range(8)]
    s0_full = [res_a.results[c]["s0_out"] for c in range(8)]
    res_b = run_bass_kernel_spmd(_NC_CACHE["B"], prep_launch_b(inputs, xn_full, s0_full),
                                 core_ids=list(range(8)))

    gwT_full = np.concatenate([res_c.results[c]["gws"] for c in range(8)], axis=1)
    uwT_full = np.concatenate([res_c.results[c]["uws"] for c in range(8)], axis=1)
    dwT_full = np.concatenate([res_c.results[c]["dws"] for c in range(8)], axis=0)
    b_outs = [res_b.results[c] for c in range(8)]
    res_d = run_bass_kernel_spmd(_NC_CACHE["D"],
                                 prep_launch_d(inputs, b_outs, (gwT_full, uwT_full, dwT_full)),
                                 core_ids=list(range(8)))

    out = np.empty((B, S, DM), np.float32)
    for c in range(NCORE):
        b, r = c // 4, c % 4
        out[b, r * NT:(r + 1) * NT, :] = res_d.results[c]["x2o"].T
    return out


# version 03
# speedup vs baseline: 1.0052x; 1.0052x over previous
import math
import numpy as np
from contextlib import ExitStack

import concourse.bass as bass
import concourse.mybir as mybir
from concourse.bass_utils import run_bass_kernel_spmd

F32 = mybir.dt.float32
BF16 = mybir.dt.bfloat16
I32 = mybir.dt.int32
AF = mybir.ActivationFunctionType
ALU = mybir.AluOpType
AX = mybir.AxisListType.X

NCORE = 8
B = 2
S = 4096
DM = 2048
NT = 1024
H0 = 16
DH = 128
CK = 512
NCK = S // CK
EPS = 1e-5
D_SC = DH ** -0.5


INTER = 5504

class Sem:
    def __init__(self, h):
        self.h = h
        self.n = 0



def build_launch_a(upto='a3'):
    nc = bass.Bass("TRN2", target_bir_lowering=False, debug=False)
    xT = nc.dram_tensor("xT", [DM, S], F32, kind="ExternalInput").ap()
    wqT = nc.dram_tensor("wqT", [DM, DM], F32, kind="ExternalInput").ap()
    wkT = nc.dram_tensor("wkT", [DM, DM], F32, kind="ExternalInput").ap()
    wvT = nc.dram_tensor("wvT", [DM, DM], F32, kind="ExternalInput").ap()
    woT = nc.dram_tensor("woT", [DM, DM], F32, kind="ExternalInput").ap()
    gam = nc.dram_tensor("gam", [DH, 1], F32, kind="ExternalInput").ap()
    bet = nc.dram_tensor("bet", [DH, 1], F32, kind="ExternalInput").ap()
    msk = nc.dram_tensor("msk", [128, NCK * 4 * 2 * 512], BF16, kind="ExternalInput").ap()
    rk_in = nc.dram_tensor("rk", [1, 1], I32, kind="ExternalInput").ap()

    xn_out = nc.dram_tensor("xn_out", [DM, S], BF16, kind="ExternalOutput").ap()
    s0_out = nc.dram_tensor("s0_out", [DM, NT], F32, kind="ExternalOutput").ap()
    xn_dram = nc.dram_tensor("xn_dram", [DM, S], BF16)

    xT3 = xT.rearrange("(j p) t -> p j t", p=128)
    wqT3 = wqT.rearrange("(j p) f -> p j f", p=128)
    wkT3 = wkT.rearrange("(j p) f -> p j f", p=128)
    wvT3 = wvT.rearrange("(j p) f -> p j f", p=128)
    woT3 = woT.rearrange("(j p) f -> p j f", p=128)
    xnd3 = xn_dram.ap().rearrange("(j p) t -> p j t", p=128)
    xno3 = xn_out.rearrange("(j p) t -> p j t", p=128)
    s0o3 = s0_out.rearrange("(j p) t -> p j t", p=128)
    msk3 = msk.rearrange("p (k s q t) -> p k s q t", k=NCK, s=4, q=2)

    es = ExitStack()
    sb = lambda n, s, d: es.enter_context(nc.sbuf_tensor(n, s, d))
    pst = lambda n, s, d: es.enter_context(nc.psum_tensor(n, s, d))
    mksem = lambda n: Sem(es.enter_context(nc.semaphore(n)))

    ones_bf = sb("ones_bf", [128, 1], BF16)
    eps_s = sb("eps_s", [128, 1], F32)
    ones_row = sb("ones_row", [1, 128], BF16)
    gam_s = sb("gam_s", [DH, 1], F32)
    bet_s = sb("bet_s", [DH, 1], F32)
    rk_s = sb("rk_s", [1, 1], I32)
    st = [sb(f"st{i}", [1, CK], F32) for i in range(7)]
    st_abf = sb("st_abf", [1, CK], BF16)
    st_bbf = sb("st_bbf", [1, CK], BF16)
    ab_bc = sb("ab_bc", [128, 2 * CK], BF16)
    p_t = [sb(f"p_t{i}", [128, 512], BF16) for i in range(2)]
    qg_t = sb("qg_t", [128, CK], BF16)
    kg_t = sb("kg_t", [128, CK], BF16)
    sq2 = sb("sq2", [128, 2 * CK], BF16)
    rz = sb("rz", [1, NT], F32)
    rz_bf = sb("rz_bf", [1, NT], BF16)
    q_all = sb("q_all", [128, H0 * NT], BF16)
    on_all = q_all
    xck = [sb(f"xck{i}", [128, 16 * CK], BF16) for i in range(2)]

    psA = pst("psA", [128, 512], F32)
    psB = pst("psB", [128, 512], F32)
    psO0 = pst("psO0", [128, 512], F32)
    psO1 = pst("psO1", [128, 512], F32)
    psS1 = pst("psS1", [1, 512], F32)
    psS2 = pst("psS2", [1, 512], F32)
    psBC = pst("psBC", [128, 512], F32)
    psBC2 = pst("psBC2", [128, 512], F32)

    ld = mksem("ld")
    lw = mksem("lw")
    lxp = [mksem("lxa"), mksem("lxb")]
    lm = mksem("lm")
    lst = mksem("lst")
    pe = mksem("pe")
    ve = mksem("ve")
    se = mksem("se")
    fc = mksem("fc")
    so = mksem("so")
    xs = mksem("xs")
    xqd = mksem("xqd")

    def stats_ladder(blk, W, inv_n, scale):
        pew = pe.n

        @blk.vector
        def _(v):
            v.wait_ge(pe.h, pew)
            v.tensor_copy(out=st[0][:, :W], in_=psS1[:, :W])
            v.tensor_copy(out=st[1][:, :W], in_=psS2[:, :W]).then_inc(ve.h, 1)
            ve.n += 1
            v.wait_ge(ve.h, ve.n)
            v.tensor_scalar_mul(out=st[2][:, :W], in0=st[0][:, :W], scalar1=float(inv_n))
            v.tensor_scalar_mul(out=st[3][:, :W], in0=st[1][:, :W], scalar1=float(inv_n)).then_inc(ve.h, 1)
            ve.n += 1
            v.wait_ge(ve.h, ve.n)
            v.tensor_mul(out=st[4][:, :W], in0=st[2][:, :W], in1=st[2][:, :W]).then_inc(ve.h, 1)
            ve.n += 1
            v.wait_ge(ve.h, ve.n)
            v.tensor_sub(out=st[3][:, :W], in0=st[3][:, :W], in1=st[4][:, :W]).then_inc(ve.h, 1)
            ve.n += 1

        sew0 = ve.n

        @blk.scalar
        def _(s):
            s.wait_ge(ve.h, sew0)
            s.activation(out=st[5][:, :W], in_=st[3][:, :W], func=AF.Sqrt,
                         bias=eps_s[0:1, 0:1], scale=1.0).then_inc(se.h, 1)
            se.n += 1

        sew1 = se.n

        @blk.vector
        def _(v):
            v.wait_ge(se.h, sew1)
            v.reciprocal(out=st[5][:, :W], in_=st[5][:, :W]).then_inc(ve.h, 1)
            ve.n += 1
            v.wait_ge(ve.h, ve.n)
            if scale != 1.0:
                v.tensor_scalar_mul(out=st[5][:, :W], in0=st[5][:, :W], scalar1=float(scale)).then_inc(ve.h, 1)
                ve.n += 1
                v.wait_ge(ve.h, ve.n)
            v.tensor_mul(out=st[6][:, :W], in0=st[2][:, :W], in1=st[5][:, :W]).then_inc(ve.h, 1)
            ve.n += 1
            v.wait_ge(ve.h, ve.n)
            v.tensor_scalar_mul(out=st[6][:, :W], in0=st[6][:, :W], scalar1=-1.0)
            v.tensor_copy(out=st_abf[:, :W], in_=st[5][:, :W]).then_inc(ve.h, 1)
            ve.n += 1
            v.wait_ge(ve.h, ve.n)
            v.tensor_copy(out=st_bbf[:, :W], in_=st[6][:, :W]).then_inc(ve.h, 1)
            ve.n += 1

        vew = ve.n

        @blk.tensor
        def _(t):
            t.wait_ge(ve.h, vew)
            t.matmul(psBC[:, :W], ones_row[0:1, :], st_abf[0:1, :W], start=True, stop=True)
            t.matmul(psBC2[:, :W], ones_row[0:1, :], st_bbf[0:1, :W], start=True, stop=True).then_inc(pe.h, 1)
            pe.n += 1

        pew2 = pe.n

        @blk.vector
        def _(v):
            v.wait_ge(pe.h, pew2)
            v.tensor_copy(out=ab_bc[:, 0:W], in_=psBC[:, :W])
            v.tensor_copy(out=ab_bc[:, W:2 * W], in_=psBC2[:, :W]).then_inc(ve.h, 1)
            ve.n += 1

    with es, nc.Block() as blk:
        a0 = ExitStack()
        xn_t = a0.enter_context(nc.sbuf_tensor("xn_t", [128, 16 * CK], BF16))

        @blk.sync
        def _(s):
            s.dma_start(out=gam_s[:, :], in_=gam).then_inc(ld.h, 16)
            s.dma_start(out=bet_s[:, :], in_=bet).then_inc(ld.h, 16)
            s.dma_start(out=rk_s[:, :], in_=rk_in).then_inc(ld.h, 16)
            ld.n += 48

        @blk.vector
        def _(v):
            v.memset(ones_bf[:, :], 1.0)
            v.memset(ones_row[:, :], 1.0)
            v.memset(eps_s[:, :], EPS).then_inc(ve.h, 1)
            ve.n += 1

        pe_stats_prev = 0
        for kt in range(NCK):
            bi = kt % 2

            @blk.gpsimd
            def _(g, kt=kt, bi=bi):
                if kt >= 2:
                    g.wait_ge(fc.h, 2 * (kt - 1))
                g.dma_start(out=xck[bi][:, :].rearrange("p (j t) -> p j t", j=16),
                            in_=xT3[:, :, kt * CK:(kt + 1) * CK]).then_inc(lxp[bi].h, 16)
                lxp[bi].n += 16

            ldx = lxp[bi].n

            for j in range(16):
                sqw = pe.n if j >= 2 else pe_stats_prev

                @blk.vector
                def _(v, bi=bi, j=j, ldx=ldx, sqw=sqw):
                    if j == 0:
                        v.wait_ge(lxp[bi].h, ldx)
                    v.wait_ge(pe.h, sqw)
                    v.tensor_mul(out=sq2[:, (j % 2) * CK:(j % 2 + 1) * CK],
                                 in0=xck[bi][:, j * CK:(j + 1) * CK],
                                 in1=xck[bi][:, j * CK:(j + 1) * CK]).then_inc(ve.h, 1)
                    ve.n += 1

                vw = ve.n

                @blk.tensor
                def _(t, bi=bi, j=j, ldx=ldx, vw=vw):
                    if j == 0:
                        t.wait_ge(lxp[bi].h, ldx)
                    t.wait_ge(ve.h, vw)
                    t.matmul(psS1[:, :], ones_bf[:, :], xck[bi][:, j * CK:(j + 1) * CK],
                             start=(j == 0), stop=(j == 15))
                    t.matmul(psS2[:, :], ones_bf[:, :], sq2[:, (j % 2) * CK:(j % 2 + 1) * CK],
                             start=(j == 0), stop=(j == 15)).then_inc(pe.h, 1)
                    pe.n += 1

            pe_stats_prev = pe.n

            stats_ladder(blk, CK, 1.0 / DM, 1.0)
            abd = ve.n

            @blk.vector
            def _(v, bi=bi, abd=abd, kt=kt):
                v.wait_ge(ve.h, abd)
                if kt >= 1:
                    v.wait_ge(fc.h, 2 * kt)
                last = None
                for j in range(16):
                    last = v.tensor_mul(out=xn_t[:, j * CK:(j + 1) * CK],
                                 in0=xck[bi][:, j * CK:(j + 1) * CK],
                                 in1=ab_bc[:, 0:CK])
                last.then_inc(ve.h, 1)
                ve.n += 1
                v.wait_ge(ve.h, ve.n)
                last = None
                for j in range(16):
                    last = v.tensor_add(out=xn_t[:, j * CK:(j + 1) * CK],
                                 in0=xn_t[:, j * CK:(j + 1) * CK],
                                 in1=ab_bc[:, CK:2 * CK])
                last.then_inc(ve.h, 1)
                ve.n += 1
                v.engine_nop().then_inc(fc.h, 1)
                fc.n += 1

            xnd = ve.n

            @blk.gpsimd
            def _(g, kt=kt, xnd=xnd):
                g.wait_ge(ve.h, xnd)
                g.dma_start(out=xnd3[:, :, kt * CK:(kt + 1) * CK],
                            in_=xn_t[:, :].rearrange("p (j t) -> p j t", j=16)).then_inc(lst.h, 16)
                g.dma_start(out=xno3[:, :, kt * CK:(kt + 1) * CK],
                            in_=xn_t[:, :].rearrange("p (j t) -> p j t", j=16)).then_inc(lst.h, 16)
                lst.n += 32
                g.wait_ge(lst.h, lst.n)
                g.engine_nop().then_inc(fc.h, 1)
                fc.n += 1

        lst_a0 = lst.n
        fc_a0 = fc.n
        a0.close()
        if upto == 'a0':
            @blk.gpsimd
            def _(g):
                g.wait_ge(lst.h, lst.n)
            return nc

        a1 = ExitStack()
        xq = a1.enter_context(nc.sbuf_tensor("xq", [128, 16 * NT], BF16))
        wqh = a1.enter_context(nc.sbuf_tensor("wqh", [128, 16 * 1024], BF16))

        @blk.gpsimd
        def _(g):
            g.wait_ge(lst.h, lst_a0)
            g.wait_ge(ld.h, 48)
            rreg = g.alloc_register()
            g.reg_load(rreg, rk_s[0:1, 0:1])
            for r in range(4):
                with g.If_cmp(rreg, r, "IS_EQ"):
                    g.dma_start(out=xq[:, :].rearrange("p (j t) -> p j t", j=16),
                                in_=xnd3[:, :, r * NT:(r + 1) * NT]).then_inc(xqd.h, 16)
                    g.wait_ge(xqd.h, 16)
                    g.sem_inc(xs.h, 1)
                with g.Else():
                    g.sem_inc(xs.h, 1)
            xs.n += 4

        for half in range(2):
            @blk.gpsimd
            def _(g, half=half):
                if half > 0:
                    g.wait_ge(pe.h, pe.n)
                g.dma_start(out=wqh[:, :].rearrange("p (j f) -> p j f", j=16),
                            in_=wqT3[:, :, half * 1024:(half + 1) * 1024]).then_inc(lw.h, 16)
                lw.n += 16

            ld_wq = lw.n

            for hh in range(8):
                head = half * 8 + hh
                for tc in range(2):
                    @blk.tensor
                    def _(t, hh=hh, tc=tc, ld_wq=ld_wq):
                        t.wait_ge(lw.h, ld_wq)
                        t.wait_ge(xs.h, 4)
                        t.wait_ge(ve.h, ve.n)
                        mm = None
                        for j in range(16):
                            mm = t.matmul(psA[:, :],
                                     wqh[:, j * 1024 + hh * 128: j * 1024 + hh * 128 + 128],
                                     xq[:, j * NT + tc * 512: j * NT + tc * 512 + 512],
                                     start=(j == 0), stop=(j == 15))
                        mm.then_inc(pe.h, 1)
                        pe.n += 1

                    pw = pe.n

                    @blk.vector
                    def _(v, pw=pw):
                        v.wait_ge(pe.h, pw)
                        v.tensor_scalar(out=qg_t[:, :], in0=psA[:, :],
                                        scalar1=gam_s[:, 0:1], scalar2=bet_s[:, 0:1],
                                        op0=ALU.mult, op1=ALU.add).then_inc(ve.h, 1)
                        ve.n += 1
                        v.wait_ge(ve.h, ve.n)
                        v.tensor_mul(out=sq2[:, 0:CK], in0=qg_t[:, :], in1=qg_t[:, :]).then_inc(ve.h, 1)
                        ve.n += 1

                    qgd = ve.n

                    @blk.tensor
                    def _(t, qgd=qgd):
                        t.wait_ge(ve.h, qgd)
                        t.matmul(psS1[:, :], ones_bf[:, :], qg_t[:, :], start=True, stop=True)
                        t.matmul(psS2[:, :], ones_bf[:, :], sq2[:, 0:CK], start=True, stop=True).then_inc(pe.h, 1)
                        pe.n += 1

                    stats_ladder(blk, CK, 1.0 / DH, D_SC)

                    @blk.vector
                    def _(v, head=head, tc=tc):
                        v.wait_ge(ve.h, ve.n)
                        dst = q_all[:, head * NT + tc * 512: head * NT + tc * 512 + 512]
                        v.tensor_mul(out=dst, in0=qg_t[:, :], in1=ab_bc[:, 0:CK]).then_inc(ve.h, 1)
                        ve.n += 1
                        v.wait_ge(ve.h, ve.n)
                        v.tensor_add(out=dst, in0=dst, in1=ab_bc[:, CK:2 * CK]).then_inc(ve.h, 1)
                        ve.n += 1

        a1.close()
        if upto == 'a1':
            @blk.gpsimd
            def _(g):
                g.wait_ge(lst.h, lst.n)
                g.wait_ge(xs.h, 4)
            @blk.vector
            def _(v):
                v.wait_ge(ve.h, ve.n)
            return nc

        a2 = ExitStack()
        k_all = a2.enter_context(nc.sbuf_tensor("k_all", [128, 4 * CK], BF16))
        v_tm = a2.enter_context(nc.sbuf_tensor("v_tm", [128, 4 * CK], BF16))
        wk_slc = a2.enter_context(nc.sbuf_tensor("wk_slc", [128, 16 * 512], BF16))
        wv_slc = a2.enter_context(nc.sbuf_tensor("wv_slc", [128, 16 * 512], BF16))
        mck = a2.enter_context(nc.sbuf_tensor("mck", [128, 8 * 512], BF16))
        o_hg = a2.enter_context(nc.sbuf_tensor("o_hg", [128, 4 * NT], F32))
        z_hg = a2.enter_context(nc.sbuf_tensor("z_hg", [1, 4 * NT], F32))

        it = 0
        for hg in range(4):
            @blk.gpsimd
            def _(g, hg=hg):
                g.wait_ge(pe.h, pe.n)
                g.wait_ge(ve.h, ve.n)
                g.dma_start(out=wk_slc[:, :].rearrange("p (j f) -> p j f", j=16),
                            in_=wkT3[:, :, hg * 512:(hg + 1) * 512]).then_inc(lw.h, 16)
                g.dma_start(out=wv_slc[:, :].rearrange("p (j f) -> p j f", j=16),
                            in_=wvT3[:, :, hg * 512:(hg + 1) * 512]).then_inc(lw.h, 16)
                lw.n += 32

            ld_w = lw.n

            @blk.vector
            def _(v, hg=hg):
                if hg == 0:
                    v.memset(o_hg[:, :], 0.0)
                    v.memset(z_hg[:, :], 0.0).then_inc(ve.h, 1)
                    ve.n += 1

            for kt in range(NCK):
                bi = it % 2

                @blk.gpsimd
                def _(g, kt=kt, bi=bi, it=it):
                    if it >= 2:
                        g.wait_ge(fc.h, fc_a0 + it - 1)
                    else:
                        g.wait_ge(fc.h, fc_a0)
                    g.dma_start(out=xck[bi][:, :].rearrange("p (j t) -> p j t", j=16),
                                in_=xnd3[:, :, kt * CK:(kt + 1) * CK]).then_inc(lxp[bi].h, 16)
                    lxp[bi].n += 16
                    if it >= 1:
                        g.wait_ge(ve.h, ve.n)
                    g.dma_start(out=mck[:, :].rearrange("p (s q t) -> p s q t", s=4, q=2),
                                in_=msk3[:, kt, :, :, :]).then_inc(lm.h, 16)
                    lm.n += 16

                ldc = (lxp[bi].n, lm.n)

                for hc in range(4):
                    @blk.tensor
                    def _(t, hc=hc, bi=bi, ldc=ldc, ld_w=ld_w):
                        t.wait_ge(lw.h, ld_w)
                        t.wait_ge(lxp[bi].h, ldc[0])
                        t.wait_ge(ve.h, ve.n)
                        mm = None
                        for j in range(16):
                            mm = t.matmul(psA[:, :],
                                     wk_slc[:, j * 512 + hc * 128: j * 512 + hc * 128 + 128],
                                     xck[bi][:, j * CK:(j + 1) * CK],
                                     start=(j == 0), stop=(j == 15))
                        mm.then_inc(pe.h, 1)
                        pe.n += 1

                    pw = pe.n

                    @blk.vector
                    def _(v, pw=pw):
                        v.wait_ge(pe.h, pw)
                        v.tensor_scalar(out=kg_t[:, :], in0=psA[:, :],
                                        scalar1=gam_s[:, 0:1], scalar2=bet_s[:, 0:1],
                                        op0=ALU.mult, op1=ALU.add).then_inc(ve.h, 1)
                        ve.n += 1
                        v.wait_ge(ve.h, ve.n)
                        v.tensor_mul(out=sq2[:, 0:CK], in0=kg_t[:, :], in1=kg_t[:, :]).then_inc(ve.h, 1)
                        ve.n += 1

                    kgd = ve.n

                    @blk.tensor
                    def _(t, kgd=kgd):
                        t.wait_ge(ve.h, kgd)
                        t.matmul(psS1[:, :], ones_bf[:, :], kg_t[:, :], start=True, stop=True)
                        t.matmul(psS2[:, :], ones_bf[:, :], sq2[:, 0:CK], start=True, stop=True).then_inc(pe.h, 1)
                        pe.n += 1

                    stats_ladder(blk, CK, 1.0 / DH, 1.0)

                    @blk.vector
                    def _(v, hc=hc):
                        v.wait_ge(ve.h, ve.n)
                        dst = k_all[:, hc * CK:(hc + 1) * CK]
                        v.tensor_mul(out=dst, in0=kg_t[:, :], in1=ab_bc[:, 0:CK]).then_inc(ve.h, 1)
                        ve.n += 1
                        v.wait_ge(ve.h, ve.n)
                        v.tensor_add(out=dst, in0=dst, in1=ab_bc[:, CK:2 * CK]).then_inc(ve.h, 1)
                        ve.n += 1

                for ts_i in range(4):
                    @blk.tensor
                    def _(t, ts_i=ts_i, bi=bi, ldc=ldc, ld_w=ld_w):
                        t.wait_ge(lw.h, ld_w)
                        t.wait_ge(lxp[bi].h, ldc[0])
                        t.wait_ge(ve.h, ve.n)
                        mm = None
                        for j in range(16):
                            mm = t.matmul(psB[:, :],
                                     xck[bi][:, j * CK + ts_i * 128: j * CK + ts_i * 128 + 128],
                                     wv_slc[:, j * 512:(j + 1) * 512],
                                     start=(j == 0), stop=(j == 15))
                        mm.then_inc(pe.h, 1)
                        pe.n += 1

                    pw = pe.n

                    @blk.vector
                    def _(v, ts_i=ts_i, pw=pw):
                        v.wait_ge(pe.h, pw)
                        v.tensor_copy(out=v_tm[:, ts_i * 512:(ts_i + 1) * 512],
                                      in_=psB[:, :]).then_inc(ve.h, 1)
                        ve.n += 1

                kv_ready = ve.n

                sq_bank = {0: 0, 1: 0}
                tt_buf = {0: 0, 1: 0}
                for hc in range(4):
                    habs = hg * 4 + hc
                    for qt in range(2):
                        zbank = psS1 if qt == 0 else psS2
                        obank = psO0 if qt == 0 else psO1
                        for ktsub in range(4):
                            sbank = psA if ktsub % 2 == 0 else psB
                            pbuf = p_t[ktsub % 2]

                            sqw = sq_bank[ktsub % 2]

                            @blk.tensor
                            def _(t, hc=hc, habs=habs, qt=qt, ktsub=ktsub,
                                  sbank=sbank, kv_ready=kv_ready, sqw=sqw):
                                t.wait_ge(ve.h, kv_ready)
                                t.wait_ge(se.h, sqw)
                                t.matmul(sbank[:, :],
                                         k_all[:, hc * CK + ktsub * 128: hc * CK + ktsub * 128 + 128],
                                         q_all[:, habs * NT + qt * 512: habs * NT + qt * 512 + 512],
                                         start=True, stop=True).then_inc(pe.h, 1)
                                pe.n += 1

                            pw = pe.n
                            ttw = tt_buf[ktsub % 2]

                            @blk.scalar
                            def _(s, pw=pw, sbank=sbank, pbuf=pbuf, ttw=ttw):
                                s.wait_ge(pe.h, pw)
                                s.wait_ge(ve.h, ttw)
                                s.activation(out=pbuf[:, :], in_=sbank[:, :],
                                             func=AF.Square).then_inc(se.h, 1)
                                se.n += 1

                            sq_bank[ktsub % 2] = se.n
                            sw = se.n

                            @blk.vector
                            def _(v, ktsub=ktsub, qt=qt, sw=sw, pbuf=pbuf, ldc=ldc):
                                v.wait_ge(se.h, sw)
                                v.wait_ge(lm.h, ldc[1])
                                v.tensor_mul(out=pbuf[:, :], in0=pbuf[:, :],
                                             in1=mck[:, (ktsub * 2 + qt) * 512:(ktsub * 2 + qt + 1) * 512]).then_inc(ve.h, 1)
                                ve.n += 1

                            tt_buf[ktsub % 2] = ve.n
                            pd = ve.n

                            @blk.tensor
                            def _(t, hc=hc, ktsub=ktsub, pd=pd, pbuf=pbuf,
                                  obank=obank, zbank=zbank):
                                t.wait_ge(ve.h, pd)
                                t.matmul(obank[:, :],
                                         v_tm[:, ktsub * 512 + hc * 128: ktsub * 512 + hc * 128 + 128],
                                         pbuf[:, :],
                                         start=(ktsub == 0), stop=(ktsub == 3))
                                t.matmul(zbank[:, :], ones_bf[:, :], pbuf[:, :],
                                         start=(ktsub == 0), stop=(ktsub == 3)).then_inc(pe.h, 1)
                                pe.n += 1

                        pw = pe.n

                        @blk.vector
                        def _(v, hc=hc, qt=qt, pw=pw, obank=obank, zbank=zbank):
                            v.wait_ge(pe.h, pw)
                            dsto = o_hg[:, hc * NT + qt * 512: hc * NT + qt * 512 + 512]
                            v.tensor_add(out=dsto, in0=dsto, in1=obank[:, :])
                            dstz = z_hg[:, hc * NT + qt * 512: hc * NT + qt * 512 + 512]
                            v.tensor_add(out=dstz, in0=dstz, in1=zbank[:, :]).then_inc(ve.h, 1)
                            ve.n += 1

                @blk.vector
                def _(v):
                    v.engine_nop().then_inc(fc.h, 1)
                    fc.n += 1

                it += 1

            for hc in range(4):
                habs = hg * 4 + hc

                @blk.vector
                def _(v, hc=hc):
                    v.wait_ge(ve.h, ve.n)
                    v.tensor_scalar_add(out=rz[:, :], in0=z_hg[:, hc * NT:(hc + 1) * NT],
                                        scalar1=EPS).then_inc(ve.h, 1)
                    ve.n += 1
                    v.wait_ge(ve.h, ve.n)
                    v.reciprocal(out=rz[:, :], in_=rz[:, :]).then_inc(ve.h, 1)
                    ve.n += 1
                    v.wait_ge(ve.h, ve.n)
                    v.tensor_copy(out=rz_bf[:, :], in_=rz[:, :]).then_inc(ve.h, 1)
                    ve.n += 1

                rzd = ve.n

                for qt in range(2):
                    @blk.tensor
                    def _(t, qt=qt, rzd=rzd):
                        t.wait_ge(ve.h, max(rzd, ve.n))
                        t.matmul(psBC[:, :], ones_row[0:1, :], rz_bf[0:1, qt * 512:(qt + 1) * 512],
                                 start=True, stop=True).then_inc(pe.h, 1)
                        pe.n += 1

                    pw = pe.n

                    @blk.vector
                    def _(v, hc=hc, habs=habs, qt=qt, pw=pw):
                        v.wait_ge(pe.h, pw)
                        v.tensor_copy(out=ab_bc[:, 0:512], in_=psBC[:, :]).then_inc(ve.h, 1)
                        ve.n += 1
                        v.wait_ge(ve.h, ve.n)
                        sl_o = slice(hc * NT + qt * 512, hc * NT + qt * 512 + 512)
                        sl_n = slice(habs * NT + qt * 512, habs * NT + qt * 512 + 512)
                        v.tensor_mul(out=on_all[:, sl_n], in0=o_hg[:, sl_o],
                                     in1=ab_bc[:, 0:512]).then_inc(ve.h, 1)
                        ve.n += 1

            if hg < 3:
                @blk.vector
                def _(v):
                    v.wait_ge(ve.h, ve.n)
                    v.memset(o_hg[:, :], 0.0)
                    v.memset(z_hg[:, :], 0.0).then_inc(ve.h, 1)
                    ve.n += 1

        a2.close()
        if upto == 'a2':
            @blk.gpsimd
            def _(g):
                g.wait_ge(lst.h, lst.n)
            @blk.vector
            def _(v):
                v.wait_ge(ve.h, ve.n)
            return nc

        a3 = ExitStack()
        wo_f = a3.enter_context(nc.sbuf_tensor("wo_f", [128, 16 * 2048], BF16))
        s0_t = a3.enter_context(nc.sbuf_tensor("s0_t", [128, 512], F32))

        @blk.gpsimd
        def _(g):
            g.wait_ge(pe.h, pe.n)
            g.wait_ge(ve.h, ve.n)
            g.dma_start(out=wo_f[:, :].rearrange("p (j f) -> p j f", j=16),
                        in_=woT3).then_inc(lw.h, 16)
            lw.n += 16

        ld_wo = lw.n

        for dj in range(16):
            for tc in range(2):
                @blk.tensor
                def _(t, dj=dj, tc=tc):
                    t.wait_ge(lw.h, ld_wo)
                    t.wait_ge(ve.h, ve.n)
                    mm = None
                    for e in range(16):
                        mm = t.matmul(psO0[:, :],
                                 wo_f[:, e * 2048 + dj * 128: e * 2048 + dj * 128 + 128],
                                 on_all[:, e * NT + tc * 512: e * NT + tc * 512 + 512],
                                 start=(e == 0), stop=(e == 15))
                    mm.then_inc(pe.h, 1)
                    pe.n += 1

                pw = pe.n

                @blk.vector
                def _(v, pw=pw):
                    v.wait_ge(pe.h, pw)
                    v.wait_ge(ve.h, ve.n)
                    v.tensor_copy(out=s0_t[:, :], in_=psO0[:, :]).then_inc(ve.h, 1)
                    ve.n += 1

                sd = ve.n

                @blk.gpsimd
                def _(g, dj=dj, tc=tc, sd=sd):
                    g.wait_ge(ve.h, sd)
                    g.dma_start(out=s0o3[:, dj, tc * 512:(tc + 1) * 512],
                                in_=s0_t[:, :]).then_inc(so.h, 16)
                    so.n += 16
                    g.wait_ge(so.h, so.n)
                    g.engine_nop().then_inc(ve.h, 1)
                    ve.n += 1

        a3.close()

        @blk.gpsimd
        def _(g):
            g.wait_ge(so.h, so.n)
            g.wait_ge(lst.h, lst.n)

    return nc



def prep_launch_a(inputs):
    x = np.asarray(inputs["L0_attn_hidden_states"], np.float32)
    gam = np.ascontiguousarray(np.asarray(inputs["L0_attn_gamma"], np.float32).reshape(DH, 1))
    bet = np.ascontiguousarray(np.asarray(inputs["L0_attn_beta"], np.float32).reshape(DH, 1))
    wqT = np.ascontiguousarray(np.asarray(inputs["L0_attn_Wq"], np.float32).T)
    wkT = np.ascontiguousarray(np.asarray(inputs["L0_attn_Wk"], np.float32).T)
    wvT = np.ascontiguousarray(np.asarray(inputs["L0_attn_Wv"], np.float32).T)
    woT = np.ascontiguousarray(np.asarray(inputs["L0_attn_Wo"], np.float32).T)

    in_maps = []
    rows = np.arange(128)
    cols = np.arange(512)
    for c in range(NCORE):
        b, r = c // 4, c % 4
        xb = x[b].T.copy()
        xb[:, (r + 1) * NT:] = 0.0
        q0 = r * NT
        m = np.zeros((128, NCK, 4, 2, 512), np.float32)
        for k in range(NCK):
            for s_ in range(4):
                kt0 = k * CK + s_ * 128
                for qt in range(2):
                    qt0 = q0 + qt * 512
                    m[:, k, s_, qt, :] = ((kt0 + rows[:, None]) <= (qt0 + cols[None, :]))
        import ml_dtypes
        in_maps.append({
            "xT": np.ascontiguousarray(xb),
            "wqT": wqT, "wkT": wkT, "wvT": wvT, "woT": woT,
            "gam": gam, "bet": bet,
            "msk": np.ascontiguousarray(m.reshape(128, -1).astype(ml_dtypes.bfloat16)),
            "rk": np.array([[r]], np.int32),
        })
    return in_maps


def kernel(**inputs):
    raise NotImplementedError("under construction")


if __name__ == "__main__":
    pass



OSL = INTER_SL = 688


def build_launch_c():
    nc = bass.Bass("TRN2", target_bir_lowering=False, debug=False)
    lut = nc.dram_tensor("lut", [1, 16], F32, kind="ExternalInput").ap()
    gidxT = nc.dram_tensor("gidxT", [DM, OSL], I32, kind="ExternalInput").ap()
    uidxT = nc.dram_tensor("uidxT", [DM, OSL], I32, kind="ExternalInput").ap()
    didxT = nc.dram_tensor("didxT", [OSL, DM], I32, kind="ExternalInput").ap()
    gws = nc.dram_tensor("gws", [DM, OSL], BF16, kind="ExternalOutput").ap()
    uws = nc.dram_tensor("uws", [DM, OSL], BF16, kind="ExternalOutput").ap()
    dws = nc.dram_tensor("dws", [OSL, DM], BF16, kind="ExternalOutput").ap()

    es = ExitStack()
    sb = lambda n, s, d: es.enter_context(nc.sbuf_tensor(n, s, d))
    pst = lambda n, s, d: es.enter_context(nc.psum_tensor(n, s, d))
    mksem = lambda n: Sem(es.enter_context(nc.semaphore(n)))

    ones_row = sb("ones_row", [1, 128], BF16)
    lut_s = sb("lut_s", [1, 16], F32)
    lut_bf = sb("lut_bf", [1, 16], BF16)
    lut_bc = sb("lut_bc", [128, 16], F32)
    idx_bf = [sb(f"idx_bf{i}", [128, DM], BF16) for i in range(2)]
    m_t = [sb(f"m_t{i}", [128, DM], BF16) for i in range(2)]
    acc = [sb(f"acc{i}", [128, DM], BF16) for i in range(2)]
    psL = pst("psL", [128, 16], F32)

    ld = mksem("ld")
    li = mksem("li")
    pe = mksem("pe")
    ve = mksem("ve")
    so = mksem("so")

    tiles = []
    for src, dst in ((gidxT, gws), (uidxT, uws)):
        for r0 in range(16):
            tiles.append((src[r0 * 128:(r0 + 1) * 128, :], dst[r0 * 128:(r0 + 1) * 128, :], 128, OSL))
    for r0 in range(6):
        nr = 128 if r0 < 5 else OSL - 5 * 128
        tiles.append((didxT[r0 * 128:r0 * 128 + nr, :], dws[r0 * 128:r0 * 128 + nr, :], nr, DM))

    with es, nc.Block() as blk:
        @blk.sync
        def _(s):
            s.dma_start(out=lut_s[:, :], in_=lut).then_inc(ld.h, 16)
            ld.n += 16

        @blk.vector
        def _(v):
            v.memset(ones_row[:, :], 1.0).then_inc(ve.h, 1)
            ve.n += 1
            v.wait_ge(ld.h, 16)
            v.tensor_copy(out=lut_bf[:, :], in_=lut_s[:, :]).then_inc(ve.h, 1)
            ve.n += 1

        @blk.tensor
        def _(t):
            t.wait_ge(ve.h, 2)
            t.matmul(psL[:, :], ones_row[0:1, :], lut_bf[0:1, :],
                     start=True, stop=True).then_inc(pe.h, 1)
            pe.n += 1

        @blk.vector
        def _(v):
            v.wait_ge(pe.h, 1)
            v.tensor_copy(out=lut_bc[:, :], in_=psL[:, :]).then_inc(ve.h, 1)
            ve.n += 1

        lutd = ve.n

        for ti, (src, dst, nr, ncol) in enumerate(tiles):
            bi = ti % 2

            @blk.gpsimd
            def _(g, src=src, bi=bi, nr=nr, ncol=ncol, ti=ti):
                if ti >= 2:
                    g.wait_ge(ve.h, ve.n)
                g.dma_start(out=idx_bf[bi][0:nr, 0:ncol], in_=src).then_inc(li.h, 16)
                li.n += 16

            lid = li.n

            @blk.vector
            def _(v, bi=bi, nr=nr, ncol=ncol, lid=lid, lutd=lutd):
                v.wait_ge(li.h, lid)
                v.wait_ge(ve.h, lutd)
                v.wait_ge(so.h, so.n)
                v.tensor_scalar(out=acc[bi][0:nr, 0:ncol], in0=idx_bf[bi][0:nr, 0:ncol],
                                scalar1=0.0, scalar2=lut_bc[0:nr, 0:1],
                                op0=ALU.is_equal, op1=ALU.mult).then_inc(ve.h, 1)
                ve.n += 1
                for i in range(1, 16):
                    v.tensor_scalar(out=m_t[bi][0:nr, 0:ncol], in0=idx_bf[bi][0:nr, 0:ncol],
                                    scalar1=float(i), scalar2=lut_bc[0:nr, i:i + 1],
                                    op0=ALU.is_equal, op1=ALU.mult).then_inc(ve.h, 1)
                    ve.n += 1
                    v.wait_ge(ve.h, ve.n)
                    v.tensor_add(out=acc[bi][0:nr, 0:ncol], in0=acc[bi][0:nr, 0:ncol],
                                 in1=m_t[bi][0:nr, 0:ncol]).then_inc(ve.h, 1)
                    ve.n += 1

            accd = ve.n

            @blk.gpsimd
            def _(g, dst=dst, bi=bi, nr=nr, ncol=ncol, accd=accd):
                g.wait_ge(ve.h, accd)
                g.dma_start(out=dst, in_=acc[bi][0:nr, 0:ncol]).then_inc(so.h, 16)
                so.n += 16

        @blk.gpsimd
        def _(g):
            g.wait_ge(so.h, so.n)

    return nc


def prep_launch_c(inputs):
    lut = np.ascontiguousarray(np.asarray(inputs["L1_ffn_lut"], np.float32).reshape(1, 16))
    gi = np.asarray(inputs["L1_ffn_gate_idx"], np.int32)
    ui = np.asarray(inputs["L1_ffn_up_idx"], np.int32)
    di = np.asarray(inputs["L1_ffn_down_idx"], np.int32)
    in_maps = []
    for c in range(NCORE):
        osl = slice(c * OSL, (c + 1) * OSL)
        in_maps.append({
            "lut": lut,
            "gidxT": np.ascontiguousarray(gi[osl, :].T),
            "uidxT": np.ascontiguousarray(ui[osl, :].T),
            "didxT": np.ascontiguousarray(di[:, osl].T),
        })
    return in_maps



H1N = 16
KV1 = 4
CH = 64
NCH = NT // CH

_L1_LAM = [float(2.0 ** (-8.0 * (h + 1) / 16.0)) for h in range(16)]
_L1_BLK = [float(np.exp(-lam * CH)) for lam in _L1_LAM]


def build_launch_b():
    nc = bass.Bass("TRN2", target_bir_lowering=False, debug=False)
    ap = lambda n, s, d, k: nc.dram_tensor(n, s, d, kind=k).ap()
    xT = ap("xT", [DM, NT], F32, "ExternalInput")
    xnT = ap("xnT", [DM, NT], BF16, "ExternalInput")
    s0T = ap("s0T", [DM, NT], F32, "ExternalInput")
    gwT = ap("gwT", [DM, 8], F32, "ExternalInput")
    gb1 = ap("gb1", [1, 8], F32, "ExternalInput")
    gb2 = ap("gb2", [1, 8], F32, "ExternalInput")
    w1T = ap("w1T", [8 * DM, 1408], F32, "ExternalInput")
    b1c = ap("b1c", [8 * 128, 11], F32, "ExternalInput")
    w2T = ap("w2T", [8 * 1408, DM], F32, "ExternalInput")
    b2r = ap("b2r", [8, DM], F32, "ExternalInput")
    wq1T = ap("wq1T", [DM, DM], F32, "ExternalInput")
    wk1T = ap("wk1T", [DM, 512], F32, "ExternalInput")
    wv1T = ap("wv1T", [DM, 512], F32, "ExternalInput")
    cosf = ap("cosf", [128, NT], F32, "ExternalInput")
    sinf = ap("sinf", [128, NT], F32, "ExternalInput")
    rotT = ap("rotT", [128, 128], F32, "ExternalInput")
    identw = ap("identw", [128, 128], F32, "ExternalInput")
    ddiag = ap("ddiag", [CH, 16 * CH], F32, "ExternalInput")
    qdecw = ap("qdecw", [128, 16 * CH], F32, "ExternalInput")
    kdecw = ap("kdecw", [CH, 16], F32, "ExternalInput")

    x1_out = ap("x1_out", [DM, NT], BF16, "ExternalOutput")
    xn1_out = ap("xn1_out", [DM, NT], BF16, "ExternalOutput")
    qr_out = ap("qr_out", [DM, NT], BF16, "ExternalOutput")
    oloc_out = ap("oloc_out", [DM, NT], BF16, "ExternalOutput")
    st_out = ap("st_out", [128, 16 * 128], F32, "ExternalOutput")

    r3 = lambda t: t.rearrange("(j p) n -> p j n", p=128)
    xT3, xnT3, s0T3 = r3(xT), r3(xnT), r3(s0T)
    gwT3 = r3(gwT)
    w1T3 = w1T.rearrange("(e j p) f -> e p j f", e=8, p=128)
    w2T3 = w2T.rearrange("(e j p) f -> e p j f", e=8, p=128)
    wq1T3, wk1T3, wv1T3 = r3(wq1T), r3(wk1T), r3(wv1T)
    x1o3, xn1o3, qro3, oloco3 = r3(x1_out), r3(xn1_out), r3(qr_out), r3(oloc_out)

    es = ExitStack()
    sb = lambda n, s, d: es.enter_context(nc.sbuf_tensor(n, s, d))
    pst = lambda n, s, d: es.enter_context(nc.psum_tensor(n, s, d))
    mksem = lambda n: Sem(es.enter_context(nc.semaphore(n)))

    ones_bf = sb("ones_bf", [128, 1], BF16)
    ones_row = sb("ones_row", [1, 128], BF16)
    eps_s = sb("eps_s", [128, 1], F32)
    zero_s = sb("zero_s", [128, 1], F32)
    ident = sb("ident", [128, 128], BF16)
    st = [sb(f"st{i}", [1, 512], F32) for i in range(7)]
    st_abf = sb("st_abf", [1, 512], BF16)
    st_bbf = sb("st_bbf", [1, 512], BF16)
    ab_bc = sb("ab_bc", [128, 1024], BF16)
    g_scr = sb("g_scr", [128, 8], F32)
    g_m1 = sb("g_m1", [128, 1], F32)
    g_nm1 = sb("g_nm1", [128, 1], F32)
    g_eq1 = sb("g_eq1", [128, 8], F32)
    g_s2 = sb("g_s2", [128, 8], F32)
    g_m2 = sb("g_m2", [128, 1], F32)
    g_ev = sb("g_ev", [128, 1], F32)
    g_g1 = sb("g_g1", [128, 1], F32)
    g_g2 = sb("g_g2", [128, 1], F32)
    g_eq2 = sb("g_eq2", [128, 8], F32)
    g_t0 = sb("g_t0", [128, 8], F32)
    comb_all = sb("comb_all", [128, 64], F32)
    combT = sb("combT", [8, 8 * 128], BF16)
    xn = sb("xn", [128, 16 * NT], BF16)
    acc = sb("acc", [128, 8 * DM], BF16)
    x1 = sb("x1", [128, 16 * NT], BF16)

    psA = pst("psA", [128, 512], F32)
    psB = pst("psB", [128, 512], F32)
    psS1 = pst("psS1", [1, 512], F32)
    psS2 = pst("psS2", [1, 512], F32)

    ld = mksem("ld")
    lw = mksem("lw")
    lx = mksem("lx")
    lst = mksem("lst")
    pe = mksem("pe")
    ve = mksem("ve")
    se = mksem("se")

    def ladder_b(blk, psBC, psBC2, W, inv_n, scale):
        pew = pe.n

        @blk.vector
        def _(v):
            v.wait_ge(pe.h, pew)
            v.tensor_copy(out=st[0][:, :W], in_=psS1[:, :W])
            v.tensor_copy(out=st[1][:, :W], in_=psS2[:, :W]).then_inc(ve.h, 1)
            ve.n += 1
            v.wait_ge(ve.h, ve.n)
            v.tensor_scalar_mul(out=st[2][:, :W], in0=st[0][:, :W], scalar1=float(inv_n))
            v.tensor_scalar_mul(out=st[3][:, :W], in0=st[1][:, :W], scalar1=float(inv_n)).then_inc(ve.h, 1)
            ve.n += 1
            v.wait_ge(ve.h, ve.n)
            v.tensor_mul(out=st[4][:, :W], in0=st[2][:, :W], in1=st[2][:, :W]).then_inc(ve.h, 1)
            ve.n += 1
            v.wait_ge(ve.h, ve.n)
            v.tensor_sub(out=st[3][:, :W], in0=st[3][:, :W], in1=st[4][:, :W]).then_inc(ve.h, 1)
            ve.n += 1

        sew0 = ve.n

        @blk.scalar
        def _(s):
            s.wait_ge(ve.h, sew0)
            s.activation(out=st[5][:, :W], in_=st[3][:, :W], func=AF.Sqrt,
                         bias=eps_s[0:1, 0:1], scale=1.0).then_inc(se.h, 1)
            se.n += 1

        sew1 = se.n

        @blk.vector
        def _(v):
            v.wait_ge(se.h, sew1)
            v.reciprocal(out=st[5][:, :W], in_=st[5][:, :W]).then_inc(ve.h, 1)
            ve.n += 1
            v.wait_ge(ve.h, ve.n)
            if scale != 1.0:
                v.tensor_scalar_mul(out=st[5][:, :W], in0=st[5][:, :W], scalar1=float(scale)).then_inc(ve.h, 1)
                ve.n += 1
                v.wait_ge(ve.h, ve.n)
            v.tensor_mul(out=st[6][:, :W], in0=st[2][:, :W], in1=st[5][:, :W]).then_inc(ve.h, 1)
            ve.n += 1
            v.wait_ge(ve.h, ve.n)
            v.tensor_scalar_mul(out=st[6][:, :W], in0=st[6][:, :W], scalar1=-1.0)
            v.tensor_copy(out=st_abf[:, :W], in_=st[5][:, :W]).then_inc(ve.h, 1)
            ve.n += 1
            v.wait_ge(ve.h, ve.n)
            v.tensor_copy(out=st_bbf[:, :W], in_=st[6][:, :W]).then_inc(ve.h, 1)
            ve.n += 1

        vew = ve.n

        @blk.tensor
        def _(t):
            t.wait_ge(ve.h, vew)
            t.matmul(psBC[:, :W], ones_row[0:1, :], st_abf[0:1, :W], start=True, stop=True)
            t.matmul(psBC2[:, :W], ones_row[0:1, :], st_bbf[0:1, :W], start=True, stop=True).then_inc(pe.h, 1)
            pe.n += 1

        pew2 = pe.n

        @blk.vector
        def _(v):
            v.wait_ge(pe.h, pew2)
            v.tensor_copy(out=ab_bc[:, 0:W], in_=psBC[:, :W])
            v.tensor_copy(out=ab_bc[:, W:2 * W], in_=psBC2[:, :W]).then_inc(ve.h, 1)
            ve.n += 1

    with es, nc.Block() as blk:
        b0 = ExitStack()
        gw_s = b0.enter_context(nc.sbuf_tensor("gw_s", [128, 16 * 8], BF16))
        gb_s = b0.enter_context(nc.sbuf_tensor("gb_s", [1, 16], BF16))

        @blk.vector
        def _(v):
            v.memset(ones_bf[:, :], 1.0)
            v.memset(ones_row[:, :], 1.0)
            v.memset(eps_s[:, :], EPS)
            v.memset(zero_s[:, :], 0.0).then_inc(ve.h, 1)
            ve.n += 1

        @blk.gpsimd
        def _(g):
            g.dma_start(out=xn[:, :].rearrange("p (j t) -> p j t", j=16), in_=xnT3).then_inc(lw.h, 16)
            g.dma_start(out=ident[:, :], in_=identw).then_inc(lw.h, 16)
            g.dma_start(out=gw_s[:, :].rearrange("p (j t) -> p j t", j=16), in_=gwT3).then_inc(lw.h, 16)
            g.dma_start(out=gb_s[:, 0:8], in_=gb1).then_inc(lw.h, 16)
            g.dma_start(out=gb_s[:, 8:16], in_=gb2).then_inc(lw.h, 16)
            lw.n += 80

        lw_b0 = lw.n

        b1 = ExitStack()
        psX1 = b1.enter_context(nc.psum_tensor("psX1", [128, 128], BF16))
        comb_bf = b1.enter_context(nc.sbuf_tensor("comb_bf", [128, 8], BF16))

        for tsub in range(8):
            @blk.tensor
            def _(t, tsub=tsub):
                t.wait_ge(lw.h, lw_b0)
                t.wait_ge(ve.h, 1 if tsub == 0 else ve.n)
                mm = None
                for j in range(16):
                    mm = t.matmul(psA[:, 0:8],
                                  xn[:, j * NT + tsub * 128: j * NT + tsub * 128 + 128],
                                  gw_s[:, j * 8:(j + 1) * 8],
                                  start=(j == 0), stop=False)
                t.matmul(psA[:, 0:8], ones_row[0:1, :], gb_s[0:1, 0:8], start=False, stop=False)
                t.matmul(psA[:, 0:8], ones_row[0:1, :], gb_s[0:1, 8:16], start=False, stop=True).then_inc(pe.h, 1)
                pe.n += 1

            pw = pe.n

            @blk.vector
            def _(v, pw=pw):
                v.wait_ge(pe.h, pw)
                v.tensor_copy(out=g_scr[:, :], in_=psA[:, 0:8]).then_inc(ve.h, 1)
                ve.n += 1
                v.wait_ge(ve.h, ve.n)
                v.reduce_max(out=g_m1[:, :], in_=g_scr[:, :], axis=AX)
                v.reduce_max(out=g_nm1[:, :], in_=g_scr[:, :], axis=AX, negate=True).then_inc(ve.h, 1)
                ve.n += 1
                v.wait_ge(ve.h, ve.n)
                v.tensor_single_scalar(out=g_eq1[:, :], in_=g_scr[:, :], scalar=g_m1[:, 0:1], op=ALU.is_equal).then_inc(ve.h, 1)
                ve.n += 1
                v.wait_ge(ve.h, ve.n)
                v.scalar_tensor_tensor(out=g_s2[:, :], in0=g_eq1[:, :], scalar=-1e30, in1=g_scr[:, :],
                                       op0=ALU.mult, op1=ALU.add).then_inc(ve.h, 1)
                ve.n += 1
                v.wait_ge(ve.h, ve.n)
                v.reduce_max(out=g_m2[:, :], in_=g_s2[:, :], axis=AX).then_inc(ve.h, 1)
                ve.n += 1
                v.wait_ge(ve.h, ve.n)
                v.tensor_single_scalar(out=g_eq2[:, :], in_=g_s2[:, :], scalar=g_m2[:, 0:1], op=ALU.is_equal).then_inc(ve.h, 1)
                ve.n += 1

            sew = ve.n

            @blk.scalar
            def _(s, sew=sew):
                s.wait_ge(ve.h, sew)
                s.activation(out=g_ev[:, :], in_=g_m2[:, :], func=AF.Exp,
                             bias=g_nm1[:, 0:1], scale=1.0).then_inc(se.h, 1)
                se.n += 1

            sw = se.n

            @blk.vector
            def _(v, tsub=tsub, sw=sw):
                v.wait_ge(se.h, sw)
                v.tensor_scalar_add(out=g_g1[:, :], in0=g_ev[:, :], scalar1=1.0).then_inc(ve.h, 1)
                ve.n += 1
                v.wait_ge(ve.h, ve.n)
                v.reciprocal(out=g_g1[:, :], in_=g_g1[:, :]).then_inc(ve.h, 1)
                ve.n += 1
                v.wait_ge(ve.h, ve.n)
                v.tensor_mul(out=g_g2[:, :], in0=g_ev[:, :], in1=g_g1[:, :])
                v.tensor_single_scalar(out=g_t0[:, :], in_=g_eq1[:, :], scalar=g_g1[:, 0:1], op=ALU.mult).then_inc(ve.h, 1)
                ve.n += 1
                v.wait_ge(ve.h, ve.n)
                v.scalar_tensor_tensor(out=comb_all[:, tsub * 8:(tsub + 1) * 8], in0=g_eq2[:, :],
                                       scalar=g_g2[:, 0:1], in1=g_t0[:, :],
                                       op0=ALU.mult, op1=ALU.add).then_inc(ve.h, 1)
                ve.n += 1
                v.wait_ge(ve.h, ve.n)
                v.tensor_copy(out=comb_bf[:, :], in_=comb_all[:, tsub * 8:(tsub + 1) * 8]).then_inc(ve.h, 1)
                ve.n += 1

            cbd = ve.n

            @blk.tensor
            def _(t, cbd=cbd):
                t.wait_ge(ve.h, cbd)
                t.transpose(psX1[0:8, 0:128], comb_bf[:, :], ident[:, :]).then_inc(pe.h, 1)
                pe.n += 1

            pw2 = pe.n

            @blk.vector
            def _(v, tsub=tsub, pw2=pw2):
                v.wait_ge(pe.h, pw2)
                v.tensor_copy(out=combT[0:8, tsub * 128:(tsub + 1) * 128], in_=psX1[0:8, 0:128]).then_inc(ve.h, 1)
                ve.n += 1

        b1.close()
        b0.close()

        b2 = ExitStack()
        w1s = b2.enter_context(nc.sbuf_tensor("w1s", [128, 16 * 1408], BF16))
        w2s = b2.enter_context(nc.sbuf_tensor("w2s", [128, 11 * 512], BF16))
        h_fm = b2.enter_context(nc.sbuf_tensor("h_fm", [128, 11 * NT], BF16))
        b1s = b2.enter_context(nc.sbuf_tensor("b1s", [128, 11], F32))
        b2s = b2.enter_context(nc.sbuf_tensor("b2s", [8, DM], BF16))

        @blk.gpsimd
        def _(g):
            g.wait_ge(lw.h, lw.n)
            g.dma_start(out=b2s[:, :], in_=b2r).then_inc(lw.h, 16)
            lw.n += 16

        lw_b2 = lw.n

        for tsub in range(8):
            for n4 in range(4):
                @blk.tensor
                def _(t, tsub=tsub, n4=n4):
                    t.wait_ge(lw.h, lw_b2)
                    t.wait_ge(ve.h, ve.n)
                    t.matmul(psB[:, :],
                             combT[0:8, tsub * 128:(tsub + 1) * 128],
                             b2s[0:8, n4 * 512:(n4 + 1) * 512],
                             start=True, stop=True).then_inc(pe.h, 1)
                    pe.n += 1

                pw = pe.n

                @blk.vector
                def _(v, tsub=tsub, n4=n4, pw=pw):
                    v.wait_ge(pe.h, pw)
                    v.tensor_copy(out=acc[:, tsub * DM + n4 * 512: tsub * DM + n4 * 512 + 512],
                                  in_=psB[:, :]).then_inc(ve.h, 1)
                    ve.n += 1

        for e in range(8):
            @blk.gpsimd
            def _(g, e=e):
                g.wait_ge(pe.h, pe.n)
                g.wait_ge(lw.h, lw.n)
                g.dma_start(out=w1s[:, :].rearrange("p (j f) -> p j f", j=16),
                            in_=w1T3[e]).then_inc(lw.h, 16)
                lw.n += 16

            lw_w1 = lw.n

            @blk.sync
            def _(s, e=e):
                s.wait_ge(se.h, se.n)
                s.dma_start(out=b1s[:, :], in_=b1c[e * 128:(e + 1) * 128, :]).then_inc(ld.h, 16)
                ld.n += 16

            ld_b1 = ld.n

            for fc in range(11):
                for tc in range(2):
                    @blk.tensor
                    def _(t, fc=fc, tc=tc, lw_w1=lw_w1):
                        t.wait_ge(lw.h, lw_w1)
                        t.wait_ge(se.h, se.n)
                        mm = None
                        for j in range(16):
                            mm = t.matmul(psA[:, :],
                                          w1s[:, j * 1408 + fc * 128: j * 1408 + fc * 128 + 128],
                                          xn[:, j * NT + tc * 512: j * NT + tc * 512 + 512],
                                          start=(j == 0), stop=(j == 15))
                        mm.then_inc(pe.h, 1)
                        pe.n += 1

                    pw = pe.n

                    @blk.scalar
                    def _(s, fc=fc, tc=tc, pw=pw, ld_b1=ld_b1):
                        s.wait_ge(pe.h, pw)
                        s.wait_ge(ld.h, ld_b1)
                        s.activation(out=h_fm[:, fc * NT + tc * 512: fc * NT + tc * 512 + 512],
                                     in_=psA[:, :], func=AF.Silu,
                                     bias=b1s[:, fc:fc + 1], scale=1.0).then_inc(se.h, 1)
                        se.n += 1

            hd_se = se.n

            for n4 in range(4):
                @blk.gpsimd
                def _(g, e=e, n4=n4):
                    g.wait_ge(pe.h, pe.n)
                    g.wait_ge(lw.h, lw.n)
                    g.dma_start(out=w2s[:, :].rearrange("p (j f) -> p j f", j=11),
                                in_=w2T3[e][:, :, n4 * 512:(n4 + 1) * 512]).then_inc(lw.h, 16)
                    lw.n += 16

                lw_w2 = lw.n

                for tsub in range(8):
                    @blk.tensor
                    def _(t, tsub=tsub, n4=n4, lw_w2=lw_w2, hd_se=hd_se):
                        t.wait_ge(lw.h, lw_w2)
                        t.wait_ge(se.h, hd_se)
                        t.wait_ge(ve.h, ve.n)
                        mm = None
                        for k in range(11):
                            mm = t.matmul(psB[:, :],
                                          h_fm[:, k * NT + tsub * 128: k * NT + tsub * 128 + 128],
                                          w2s[:, k * 512:(k + 1) * 512],
                                          start=(k == 0), stop=(k == 10))
                        mm.then_inc(pe.h, 1)
                        pe.n += 1

                    pw = pe.n

                    @blk.vector
                    def _(v, e=e, tsub=tsub, n4=n4, pw=pw):
                        v.wait_ge(pe.h, pw)
                        sl = slice(tsub * DM + n4 * 512, tsub * DM + n4 * 512 + 512)
                        v.scalar_tensor_tensor(out=acc[:, sl], in0=psB[:, :],
                                               scalar=comb_all[:, tsub * 8 + e: tsub * 8 + e + 1],
                                               in1=acc[:, sl], op0=ALU.mult, op1=ALU.add).then_inc(ve.h, 1)
                        ve.n += 1

        b2.close()

        b3 = ExitStack()
        psTR = b3.enter_context(nc.psum_tensor("psTR", [128, 128], BF16))
        xj = b3.enter_context(nc.sbuf_tensor("xj", [128, NT], BF16))
        s0j = b3.enter_context(nc.sbuf_tensor("s0j", [128, NT], F32))
        gfm = b3.enter_context(nc.sbuf_tensor("gfm", [128, NT], BF16))

        ve_accd = ve.n

        for t32 in range(32):
            @blk.scalar
            def _(s, t32=t32, ve_accd=ve_accd):
                if t32 == 0:
                    s.wait_ge(ve.h, ve_accd)
                s.activation(out=acc[:, t32 * 512:(t32 + 1) * 512],
                             in_=acc[:, t32 * 512:(t32 + 1) * 512],
                             func=AF.Silu, bias=zero_s[0:128, 0:1], scale=1.0).then_inc(se.h, 1)
                se.n += 1

        silu_done = se.n

        for fj in range(16):
            @blk.gpsimd
            def _(g, fj=fj):
                if fj > 0:
                    g.wait_ge(ve.h, ve.n)
                else:
                    g.wait_ge(pe.h, pe.n)
                g.dma_start(out=xj[:, :], in_=xT3[:, fj, :]).then_inc(lx.h, 16)
                lx.n += 16

            @blk.sync
            def _(s, fj=fj):
                if fj > 0:
                    s.wait_ge(ve.h, ve.n)
                else:
                    s.wait_ge(pe.h, pe.n)
                s.dma_start(out=s0j[:, :], in_=s0T3[:, fj, :]).then_inc(ld.h, 16)
                ld.n += 16

            lds = (lx.n, ld.n)

            for tsub in range(8):
                @blk.tensor
                def _(t, fj=fj, tsub=tsub, silu_done=silu_done):
                    t.wait_ge(se.h, silu_done)
                    if not (fj == 0 and tsub == 0):
                        t.wait_ge(ve.h, ve.n)
                    t.transpose(psTR[:, :], acc[:, tsub * DM + fj * 128: tsub * DM + fj * 128 + 128],
                                ident[:, :]).then_inc(pe.h, 1)
                    pe.n += 1

                pw = pe.n

                @blk.vector
                def _(v, fj=fj, tsub=tsub, pw=pw):
                    v.wait_ge(pe.h, pw)
                    v.tensor_copy(out=gfm[:, tsub * 128:(tsub + 1) * 128], in_=psTR[:, :]).then_inc(ve.h, 1)
                    ve.n += 1

            gfmd = ve.n

            @blk.vector
            def _(v, fj=fj, lds=lds, gfmd=gfmd):
                v.wait_ge(lx.h, lds[0])
                v.wait_ge(ld.h, lds[1])
                v.wait_ge(ve.h, gfmd)
                v.tensor_mul(out=gfm[:, :], in0=gfm[:, :], in1=s0j[:, :]).then_inc(ve.h, 1)
                ve.n += 1
                v.wait_ge(ve.h, ve.n)
                v.tensor_add(out=x1[:, fj * NT:(fj + 1) * NT], in0=gfm[:, :], in1=xj[:, :]).then_inc(ve.h, 1)
                ve.n += 1

            x1d = ve.n

            @blk.gpsimd
            def _(g, fj=fj, x1d=x1d):
                g.wait_ge(ve.h, x1d)
                g.dma_start(out=x1o3[:, fj, :], in_=x1[:, fj * NT:(fj + 1) * NT]).then_inc(lst.h, 16)
                lst.n += 16

        b3.close()

        b4 = ExitStack()
        psBC = b4.enter_context(nc.psum_tensor("psBC", [128, 512], F32))
        psBC2 = b4.enter_context(nc.psum_tensor("psBC2", [128, 512], F32))
        sqb = b4.enter_context(nc.sbuf_tensor("sqb", [128, 1024], BF16))
        xn1 = xn

        pe_sq_prev = 0
        for tc in range(2):
            for j in range(16):
                sqw = pe.n if j >= 2 else pe_sq_prev

                @blk.vector
                def _(v, tc=tc, j=j, sqw=sqw):
                    v.wait_ge(pe.h, sqw)
                    if j == 0:
                        v.wait_ge(ve.h, ve.n)
                    v.tensor_mul(out=sqb[:, (j % 2) * 512:(j % 2 + 1) * 512],
                                 in0=x1[:, j * NT + tc * 512: j * NT + tc * 512 + 512],
                                 in1=x1[:, j * NT + tc * 512: j * NT + tc * 512 + 512]).then_inc(ve.h, 1)
                    ve.n += 1

                vw = ve.n

                @blk.tensor
                def _(t, tc=tc, j=j, vw=vw):
                    t.wait_ge(ve.h, vw)
                    t.matmul(psS1[:, :], ones_bf[:, :],
                             x1[:, j * NT + tc * 512: j * NT + tc * 512 + 512],
                             start=(j == 0), stop=(j == 15))
                    t.matmul(psS2[:, :], ones_bf[:, :], sqb[:, (j % 2) * 512:(j % 2 + 1) * 512],
                             start=(j == 0), stop=(j == 15)).then_inc(pe.h, 1)
                    pe.n += 1

            pe_sq_prev = pe.n

            ladder_b(blk, psBC, psBC2, 512, 1.0 / DM, 1.0)
            abd = ve.n

            @blk.vector
            def _(v, tc=tc, abd=abd):
                v.wait_ge(ve.h, abd)
                for j in range(16):
                    if j > 0:
                        v.wait_ge(ve.h, ve.n)
                    v.tensor_mul(out=sqb[:, 0:512],
                                 in0=x1[:, j * NT + tc * 512: j * NT + tc * 512 + 512],
                                 in1=ab_bc[:, 0:512]).then_inc(ve.h, 1)
                    ve.n += 1
                    v.wait_ge(ve.h, ve.n)
                    v.tensor_add(out=xn1[:, j * NT + tc * 512: j * NT + tc * 512 + 512],
                                 in0=sqb[:, 0:512], in1=ab_bc[:, 512:1024]).then_inc(ve.h, 1)
                    ve.n += 1

            xn1d = ve.n

            @blk.gpsimd
            def _(g, tc=tc, xn1d=xn1d):
                g.wait_ge(ve.h, xn1d)
                g.dma_start(out=xn1o3[:, :, tc * 512:(tc + 1) * 512],
                            in_=xn1[:, :].rearrange("p (j t) -> p j t", j=16)[:, :, tc * 512:(tc + 1) * 512]).then_inc(lst.h, 16)
                lst.n += 16

        b4.close()

        b5 = ExitStack()
        qr = x1
        kr = b5.enter_context(nc.sbuf_tensor("kr", [128, 4 * NT], BF16))
        vck = b5.enter_context(nc.sbuf_tensor("vck", [64, 16 * 512], BF16))
        rot_s = b5.enter_context(nc.sbuf_tensor("rot_s", [128, 128], BF16))
        cos_s = b5.enter_context(nc.sbuf_tensor("cos_s", [128, NT], BF16))
        sin_s = b5.enter_context(nc.sbuf_tensor("sin_s", [128, NT], BF16))
        rbuf = b5.enter_context(nc.sbuf_tensor("rbuf", [128, NT], BF16))
        b5w = ExitStack()
        wqs = b5w.enter_context(nc.sbuf_tensor("wqs", [128, 16 * 1024], BF16))

        @blk.gpsimd
        def _(g):
            g.wait_ge(lw.h, lw.n)
            g.dma_start(out=rot_s[:, :], in_=rotT).then_inc(lw.h, 16)
            g.dma_start(out=cos_s[:, :], in_=cosf).then_inc(lw.h, 16)
            g.dma_start(out=sin_s[:, :], in_=sinf).then_inc(lw.h, 16)
            lw.n += 48

        lw_rope = lw.n

        lst_b4 = lst.n
        for half in range(2):
            @blk.gpsimd
            def _(g, half=half):
                g.wait_ge(pe.h, pe.n)
                g.wait_ge(lw.h, lw.n)
                g.dma_start(out=wqs[:, :].rearrange("p (j f) -> p j f", j=16),
                            in_=wq1T3[:, :, half * 1024:(half + 1) * 1024]).then_inc(lw.h, 16)
                lw.n += 16

            lw_wq = lw.n

            for hh in range(8):
                head = half * 8 + hh
                for tc in range(2):
                    @blk.tensor
                    def _(t, hh=hh, head=head, tc=tc, lw_wq=lw_wq):
                        t.wait_ge(lw.h, lw_wq)
                        t.wait_ge(ve.h, ve.n)
                        mm = None
                        for j in range(16):
                            mm = t.matmul(psA[:, :],
                                          wqs[:, j * 1024 + hh * 128: j * 1024 + hh * 128 + 128],
                                          xn1[:, j * NT + tc * 512: j * NT + tc * 512 + 512],
                                          start=(j == 0), stop=(j == 15))
                        mm.then_inc(pe.h, 1)
                        pe.n += 1

                    pw = pe.n

                    @blk.vector
                    def _(v, head=head, tc=tc, pw=pw):
                        v.wait_ge(pe.h, pw)
                        v.wait_ge(lst.h, lst_b4)
                        v.tensor_copy(out=qr[:, head * NT + tc * 512: head * NT + tc * 512 + 512],
                                      in_=psA[:, :]).then_inc(ve.h, 1)
                        ve.n += 1

        @blk.gpsimd
        def _(g):
            g.wait_ge(pe.h, pe.n)
            g.wait_ge(lw.h, lw.n)
            g.dma_start(out=wqs[:, 0:16 * 512].rearrange("p (j f) -> p j f", j=16),
                        in_=wk1T3).then_inc(lw.h, 16)
            g.dma_start(out=wqs[:, 16 * 512:16 * 1024].rearrange("p (j f) -> p j f", j=16),
                        in_=wv1T3).then_inc(lw.h, 16)
            lw.n += 32

        lw_wkv = lw.n

        for g4 in range(4):
            for tc in range(2):
                @blk.tensor
                def _(t, g4=g4, tc=tc):
                    t.wait_ge(lw.h, lw_wkv)
                    t.wait_ge(ve.h, ve.n)
                    mm = None
                    for j in range(16):
                        mm = t.matmul(psA[:, :],
                                      wqs[:, j * 512 + g4 * 128: j * 512 + g4 * 128 + 128],
                                      xn1[:, j * NT + tc * 512: j * NT + tc * 512 + 512],
                                      start=(j == 0), stop=(j == 15))
                    mm.then_inc(pe.h, 1)
                    pe.n += 1

                pw = pe.n

                @blk.vector
                def _(v, g4=g4, tc=tc, pw=pw):
                    v.wait_ge(pe.h, pw)
                    v.tensor_copy(out=kr[:, g4 * NT + tc * 512: g4 * NT + tc * 512 + 512],
                                  in_=psA[:, :]).then_inc(ve.h, 1)
                    ve.n += 1

        for tsub in range(16):
            @blk.tensor
            def _(t, tsub=tsub):
                t.wait_ge(lw.h, lw_wkv)
                t.wait_ge(ve.h, ve.n)
                mm = None
                for j in range(16):
                    mm = t.matmul(psB[0:64, :],
                                  xn1[:, j * NT + tsub * 64: j * NT + tsub * 64 + 64],
                                  wqs[:, 16 * 512 + j * 512: 16 * 512 + j * 512 + 512],
                                  start=(j == 0), stop=(j == 15))
                mm.then_inc(pe.h, 1)
                pe.n += 1

            pw = pe.n

            @blk.vector
            def _(v, tsub=tsub, pw=pw):
                v.wait_ge(pe.h, pw)
                v.tensor_copy(out=vck[0:64, tsub * 512:(tsub + 1) * 512], in_=psB[0:64, :]).then_inc(ve.h, 1)
                ve.n += 1

        vprj = pe.n

        @blk.gpsimd
        def _(g):
            g.wait_ge(pe.h, vprj)
            g.wait_ge(ve.h, ve.n)

        b5w.close()

        def rope_one(blk, buf, hidx, dst_dram):
            @blk.tensor
            def _(t, hidx=hidx):
                t.wait_ge(lw.h, lw_rope)
                t.wait_ge(ve.h, ve.n)
                t.matmul(psA[:, :], rot_s[:, :], buf[:, hidx * NT: hidx * NT + 512],
                         start=True, stop=True)
                t.matmul(psB[:, :], rot_s[:, :], buf[:, hidx * NT + 512: hidx * NT + 1024],
                         start=True, stop=True).then_inc(pe.h, 1)
                pe.n += 1

            pw = pe.n

            @blk.vector
            def _(v, hidx=hidx, pw=pw):
                v.wait_ge(pe.h, pw)
                v.tensor_copy(out=rbuf[:, 0:512], in_=psA[:, :])
                v.tensor_copy(out=rbuf[:, 512:1024], in_=psB[:, :]).then_inc(ve.h, 1)
                ve.n += 1
                v.wait_ge(ve.h, ve.n)
                v.tensor_mul(out=rbuf[:, :], in0=rbuf[:, :], in1=sin_s[:, :]).then_inc(ve.h, 1)
                ve.n += 1
                v.wait_ge(ve.h, ve.n)
                v.tensor_mul(out=buf[:, hidx * NT:(hidx + 1) * NT],
                             in0=buf[:, hidx * NT:(hidx + 1) * NT], in1=cos_s[:, :]).then_inc(ve.h, 1)
                ve.n += 1
                v.wait_ge(ve.h, ve.n)
                v.tensor_add(out=buf[:, hidx * NT:(hidx + 1) * NT],
                             in0=buf[:, hidx * NT:(hidx + 1) * NT], in1=rbuf[:, :]).then_inc(ve.h, 1)
                ve.n += 1

            rd = ve.n

            if dst_dram is not None:
                @blk.gpsimd
                def _(g, hidx=hidx, rd=rd):
                    g.wait_ge(ve.h, rd)
                    g.dma_start(out=dst_dram[:, hidx, :],
                                in_=buf[:, hidx * NT:(hidx + 1) * NT]).then_inc(lst.h, 16)
                    lst.n += 16

        for head in range(16):
            rope_one(blk, qr, head, qro3.rearrange("p j t -> p j t") if False else qro3)
        for g4 in range(4):
            rope_one(blk, kr, g4, None)

        b7 = ExitStack()
        psO0 = b7.enter_context(nc.psum_tensor("psO0", [128, 512], F32))
        psO1 = b7.enter_context(nc.psum_tensor("psO1", [128, 512], F32))
        psTR2 = b7.enter_context(nc.psum_tensor("psTR2", [128, 128], BF16))
        ktm = acc
        oloc = xn
        kd_bf = b7.enter_context(nc.sbuf_tensor("kd_bf", [64, 128], BF16))
        qd_bf = b7.enter_context(nc.sbuf_tensor("qd_bf", [128, CH], BF16))
        p_bf = b7.enter_context(nc.sbuf_tensor("p_bf", [64, CH], BF16))
        state_f = b7.enter_context(nc.sbuf_tensor("state_f", [128, 128], F32))
        state_b = b7.enter_context(nc.sbuf_tensor("state_b", [128, 128], BF16))
        kdec_s = b7.enter_context(nc.sbuf_tensor("kdec_s", [CH, 16], F32))
        QD0, DD0 = 8192, 9216

        @blk.gpsimd
        def _(g):
            g.wait_ge(lw.h, lw.n)
            g.wait_ge(ve.h, ve.n)
            g.dma_start(out=acc[0:64, DD0:DD0 + 16 * CH], in_=ddiag).then_inc(lw.h, 16)
            g.dma_start(out=acc[:, QD0:QD0 + 16 * CH], in_=qdecw).then_inc(lw.h, 16)
            lw.n += 32

        @blk.sync
        def _(s):
            s.wait_ge(ld.h, ld.n)
            s.wait_ge(pe.h, pe.n)
            s.dma_start(out=kdec_s[:, :], in_=kdecw).then_inc(ld.h, 16)
            ld.n += 16

        lw_dec = lw.n
        ld_dec = ld.n

        for g4 in range(4):
            for c in range(NCH):
                @blk.tensor
                def _(t, g4=g4, c=c):
                    if not (g4 == 0 and c == 0):
                        t.wait_ge(ve.h, ve.n)
                    else:
                        t.wait_ge(ve.h, ve.n)
                    t.transpose(psTR2[0:64, 0:128],
                                kr[:, g4 * NT + c * CH: g4 * NT + (c + 1) * CH],
                                ident[:, :]).then_inc(pe.h, 1)
                    pe.n += 1

                pw = pe.n

                @blk.vector
                def _(v, g4=g4, c=c, pw=pw):
                    v.wait_ge(pe.h, pw)
                    v.tensor_copy(out=ktm[0:64, (g4 * NCH + c) * 128:(g4 * NCH + c) * 128 + 128],
                                  in_=psTR2[0:64, 0:128]).then_inc(ve.h, 1)
                    ve.n += 1

        for h in range(16):
            g4 = h // 4
            blkf = _L1_BLK[h]

            @blk.vector
            def _(v):
                v.wait_ge(ve.h, ve.n)
                v.memset(state_f[:, :], 0.0)
                v.memset(state_b[:, :], 0.0).then_inc(ve.h, 1)
                ve.n += 1

            for c in range(NCH):
                po = psO0 if (c // 8) % 2 == 0 else psO1
                vcol = c * 512 + g4 * 128

                @blk.vector
                def _(v, h=h, g4=g4, c=c):
                    v.wait_ge(lw.h, lw_dec)
                    v.wait_ge(ld.h, ld_dec)
                    v.wait_ge(ve.h, ve.n)
                    v.tensor_single_scalar(out=kd_bf[0:64, :],
                                           in_=ktm[0:64, (g4 * NCH + c) * 128:(g4 * NCH + c) * 128 + 128],
                                           scalar=kdec_s[0:64, h:h + 1], op=ALU.mult)
                    v.tensor_mul(out=qd_bf[:, :],
                                 in0=qr[:, h * NT + c * CH: h * NT + (c + 1) * CH],
                                 in1=acc[:, QD0 + h * CH: QD0 + (h + 1) * CH]).then_inc(ve.h, 1)
                    ve.n += 1

                kqd = ve.n

                @blk.tensor
                def _(t, h=h, g4=g4, c=c):
                    t.wait_ge(ve.h, ve.n)
                    t.matmul(psA[0:64, 0:64],
                             kr[:, g4 * NT + c * CH: g4 * NT + (c + 1) * CH],
                             qr[:, h * NT + c * CH: h * NT + (c + 1) * CH],
                             start=True, stop=True).then_inc(pe.h, 1)
                    pe.n += 1

                pw = pe.n

                @blk.vector
                def _(v, h=h, pw=pw):
                    v.wait_ge(pe.h, pw)
                    v.wait_ge(ve.h, ve.n)
                    v.tensor_mul(out=p_bf[0:64, :], in0=psA[0:64, 0:64],
                                 in1=acc[0:64, DD0 + h * CH: DD0 + (h + 1) * CH]).then_inc(ve.h, 1)
                    ve.n += 1

                pd = ve.n

                @blk.tensor
                def _(t, c=c, po=po, vcol=vcol, pd=pd, kqd=kqd):
                    t.wait_ge(ve.h, pd)
                    t.matmul(po[:, (c % 8) * CH:(c % 8 + 1) * CH],
                             vck[0:64, vcol:vcol + 128],
                             p_bf[0:64, :], start=(c % 8 == 0), stop=False)
                    t.matmul(po[:, (c % 8) * CH:(c % 8 + 1) * CH],
                             state_b[:, :], qd_bf[:, :], start=False, stop=(c % 8 == 7))
                    t.matmul(psB[:, 0:128], kd_bf[0:64, :], vck[0:64, vcol:vcol + 128],
                             start=True, stop=True).then_inc(pe.h, 1)
                    pe.n += 1

                pw2 = pe.n

                @blk.vector
                def _(v, pw2=pw2, blkf=blkf):
                    v.wait_ge(pe.h, pw2)
                    v.scalar_tensor_tensor(out=state_f[:, :], in0=state_f[:, :], scalar=float(blkf),
                                           in1=psB[:, 0:128], op0=ALU.mult, op1=ALU.add).then_inc(ve.h, 1)
                    ve.n += 1
                    v.wait_ge(ve.h, ve.n)
                    v.tensor_copy(out=state_b[:, :], in_=state_f[:, :]).then_inc(ve.h, 1)
                    ve.n += 1

                if c % 8 == 7:
                    pw3 = pe.n

                    @blk.vector
                    def _(v, h=h, c=c, po=po, pw3=pw3):
                        v.wait_ge(pe.h, pw3)
                        v.tensor_copy(out=oloc[:, h * NT + (c // 8) * 512: h * NT + (c // 8) * 512 + 512],
                                      in_=po[:, :]).then_inc(ve.h, 1)
                        ve.n += 1

            std = ve.n

            @blk.gpsimd
            def _(g, h=h, std=std):
                g.wait_ge(ve.h, std)
                g.wait_ge(lst.h, lst.n)
                g.dma_start(out=st_out[:, h * 128:(h + 1) * 128], in_=state_f[:, :]).then_inc(lst.h, 16)
                lst.n += 16

            stf = lst.n

            @blk.vector
            def _(v, stf=stf):
                v.wait_ge(lst.h, stf)

        old = ve.n

        @blk.gpsimd
        def _(g, old=old):
            g.wait_ge(ve.h, old)
            g.wait_ge(lst.h, lst.n)
            g.dma_start(out=oloco3, in_=oloc[:, :].rearrange("p (j t) -> p j t", j=16)).then_inc(lst.h, 16)
            lst.n += 16

        b7.close()
        b5.close()

        @blk.gpsimd
        def _(g):
            g.wait_ge(lst.h, lst.n)
            g.wait_ge(ld.h, ld.n)

    return nc


def prep_launch_b(inputs, xn_full, s0_full):
    x = np.asarray(inputs["L0_attn_hidden_states"], np.float32)
    gwT = np.ascontiguousarray(np.asarray(inputs["L0_ffn_gate_w"], np.float32).T)
    gb1 = np.ascontiguousarray(np.asarray(inputs["L0_ffn_gate_b"], np.float32).reshape(1, 8))
    gb2 = np.ascontiguousarray(np.asarray(inputs["L0_ffn_bias"], np.float32).reshape(1, 8))
    w1 = np.asarray(inputs["L0_ffn_w1"], np.float32)
    b1 = np.asarray(inputs["L0_ffn_b1"], np.float32)
    w2 = np.asarray(inputs["L0_ffn_w2"], np.float32)
    b2 = np.asarray(inputs["L0_ffn_b2"], np.float32)
    w1T = np.ascontiguousarray(w1.transpose(0, 2, 1).reshape(8 * DM, 1408))
    w2T = np.ascontiguousarray(w2.transpose(0, 2, 1).reshape(8 * 1408, DM))
    b1c = np.ascontiguousarray(b1.reshape(8, 11, 128).transpose(0, 2, 1).reshape(8 * 128, 11))
    wq1T = np.ascontiguousarray(np.asarray(inputs["L1_attn_wq"], np.float32).T)
    wk1T = np.ascontiguousarray(np.asarray(inputs["L1_attn_wk"], np.float32).T)
    wv1T = np.ascontiguousarray(np.asarray(inputs["L1_attn_wv"], np.float32).T)
    identw = np.eye(128, dtype=np.float32)
    rotT = np.zeros((128, 128), np.float32)
    for i in range(64):
        rotT[i + 64, i] = -1.0
        rotT[i, i + 64] = 1.0
    lam = np.exp2(-8.0 * np.arange(1, 17) / 16.0).astype(np.float64)
    r = np.arange(CH, dtype=np.float64)
    ddiag = np.zeros((CH, 16 * CH), np.float32)
    qdec = np.zeros((128, 16 * CH), np.float32)
    kdec = np.zeros((CH, 16), np.float32)
    for h in range(16):
        diff = r[None, :] - r[:, None]
        d = np.where(diff >= 0, np.exp(-lam[h] * diff), 0.0)
        ddiag[:, h * CH:(h + 1) * CH] = d
        qdec[:, h * CH:(h + 1) * CH] = np.exp(-lam[h] * (r + 1.0))[None, :]
        kdec[:, h] = np.exp(-lam[h] * (CH - 1.0 - r))
    inv_freq = 1.0 / 10000.0 ** (np.arange(0, 128, 2, dtype=np.float64) / 128.0)

    in_maps = []
    for c in range(NCORE):
        b, rk_ = c // 4, c % 4
        tok = slice(rk_ * NT, (rk_ + 1) * NT)
        pos = np.arange(rk_ * NT, (rk_ + 1) * NT, dtype=np.float64)
        freqs = pos[None, :] * np.concatenate([inv_freq, inv_freq])[:, None]
        in_maps.append({
            "xT": np.ascontiguousarray(x[b].T[:, tok]),
            "xnT": np.ascontiguousarray(xn_full[c][:, tok]),
            "s0T": np.ascontiguousarray(s0_full[c]),
            "gwT": gwT, "gb1": gb1, "gb2": gb2,
            "w1T": w1T, "b1c": b1c, "w2T": w2T, "b2r": b2,
            "wq1T": wq1T, "wk1T": wk1T, "wv1T": wv1T,
            "cosf": np.ascontiguousarray(np.cos(freqs).astype(np.float32)),
            "sinf": np.ascontiguousarray(np.sin(freqs).astype(np.float32)),
            "rotT": rotT, "identw": identw,
            "ddiag": ddiag, "qdecw": qdec, "kdecw": kdec,
        })
    return in_maps



def build_launch_d():
    nc = bass.Bass("TRN2", target_bir_lowering=False, debug=False)
    ap = lambda n, s, d, k: nc.dram_tensor(n, s, d, kind=k).ap()
    x1T = ap("x1T", [DM, NT], BF16, "ExternalInput")
    xn1T = ap("xn1T", [DM, NT], BF16, "ExternalInput")
    qrT = ap("qrT", [DM, NT], BF16, "ExternalInput")
    olocT = ap("olocT", [DM, NT], BF16, "ExternalInput")
    stp = ap("stp", [128, 3 * 2048], F32, "ExternalInput")
    dfac = ap("dfac", [128, 3 * 16], F32, "ExternalInput")
    qdecg = ap("qdecg", [128, 16 * NT], BF16, "ExternalInput")
    gwTf = ap("gwTf", [DM, INTER], BF16, "ExternalInput")
    uwTf = ap("uwTf", [DM, INTER], BF16, "ExternalInput")
    dwTf = ap("dwTf", [INTER, DM], BF16, "ExternalInput")
    wo1T = ap("wo1T", [DM, DM], F32, "ExternalInput")
    x2o = ap("x2o", [DM, NT], F32, "ExternalOutput")

    r3 = lambda t: t.rearrange("(j p) n -> p j n", p=128)
    x1T3, xn1T3, qrT3, olocT3, wo1T3 = r3(x1T), r3(xn1T), r3(qrT), r3(olocT), r3(wo1T)
    qdecg3 = qdecg
    gwTf3 = r3(gwTf)
    uwTf3 = r3(uwTf)
    dwTf3 = r3(dwTf)
    x2o3 = r3(x2o)

    es = ExitStack()
    sb = lambda n, s, d: es.enter_context(nc.sbuf_tensor(n, s, d))
    pst = lambda n, s, d: es.enter_context(nc.psum_tensor(n, s, d))
    mksem = lambda n: Sem(es.enter_context(nc.semaphore(n)))

    zero_s = sb("zero_s", [128, 1], F32)
    s1 = sb("s1", [128, 16 * NT], BF16)

    psA = pst("psA", [128, 512], F32)
    psB = pst("psB", [128, 512], F32)

    ld = mksem("ld")
    lw = mksem("lw")
    pe = mksem("pe")
    ve = mksem("ve")
    se = mksem("se")
    so = mksem("so")

    with es, nc.Block() as blk:
        d12 = ExitStack()
        on = d12.enter_context(nc.sbuf_tensor("on", [128, 16 * NT], BF16))
        d1 = ExitStack()
        qr = d1.enter_context(nc.sbuf_tensor("qr", [128, 16 * NT], BF16))
        oloc = d1.enter_context(nc.sbuf_tensor("oloc", [128, 16 * NT], BF16))
        qdg = d1.enter_context(nc.sbuf_tensor("qdg", [128, 16 * NT], BF16))
        sts = d1.enter_context(nc.sbuf_tensor("sts", [128, 3 * 2048], F32))
        dfc = d1.enter_context(nc.sbuf_tensor("dfc", [128, 3 * 16], F32))
        s_t = d1.enter_context(nc.sbuf_tensor("s_t", [128, 3 * 128], F32))
        s_bf = d1.enter_context(nc.sbuf_tensor("s_bf", [128, 128], BF16))
        qd_bf = d1.enter_context(nc.sbuf_tensor("qd_bf", [128, NT], BF16))

        @blk.vector
        def _(v):
            v.memset(zero_s[:, :], 0.0).then_inc(ve.h, 1)
            ve.n += 1

        @blk.sync
        def _(s):
            s.dma_start(out=qr[:, :].rearrange("p (j t) -> p j t", j=16), in_=qrT3).then_inc(ld.h, 16)
            s.dma_start(out=oloc[:, :].rearrange("p (j t) -> p j t", j=16), in_=olocT3).then_inc(ld.h, 16)
            s.dma_start(out=sts[:, :], in_=stp).then_inc(ld.h, 16)
            s.dma_start(out=dfc[:, :], in_=dfac).then_inc(ld.h, 16)
            ld.n += 64

        @blk.gpsimd
        def _(g):
            g.dma_start(out=qdg[:, :], in_=qdecg3).then_inc(lw.h, 16)
            lw.n += 16

        ld_d1 = ld.n
        lw_d1 = lw.n

        for h in range(16):
            @blk.vector
            def _(v, h=h):
                v.wait_ge(ld.h, ld_d1)
                v.wait_ge(lw.h, lw_d1)
                v.wait_ge(pe.h, pe.n)
                for j in range(3):
                    v.tensor_single_scalar(out=s_t[:, j * 128:(j + 1) * 128],
                                           in_=sts[:, (j * 16 + h) * 128:(j * 16 + h) * 128 + 128],
                                           scalar=dfc[:, j * 16 + h: j * 16 + h + 1],
                                           op=ALU.mult)
                v.engine_nop().then_inc(ve.h, 1)
                ve.n += 1
                v.wait_ge(ve.h, ve.n)
                v.tensor_add(out=s_t[:, 0:128], in0=s_t[:, 0:128], in1=s_t[:, 128:256]).then_inc(ve.h, 1)
                ve.n += 1
                v.wait_ge(ve.h, ve.n)
                v.tensor_add(out=s_t[:, 0:128], in0=s_t[:, 0:128], in1=s_t[:, 256:384]).then_inc(ve.h, 1)
                ve.n += 1
                v.wait_ge(ve.h, ve.n)
                v.tensor_copy(out=s_bf[:, :], in_=s_t[:, 0:128])
                v.tensor_mul(out=qd_bf[:, :], in0=qr[:, h * NT:(h + 1) * NT],
                             in1=qdg[:, h * NT:(h + 1) * NT]).then_inc(ve.h, 1)
                ve.n += 1

            kqd = ve.n

            for tc in range(2):
                @blk.tensor
                def _(t, h=h, tc=tc, kqd=kqd):
                    t.wait_ge(ve.h, kqd if tc == 0 else ve.n)
                    t.matmul(psA[:, :] if tc == 0 else psB[:, :],
                             s_bf[:, :], qd_bf[:, tc * 512:(tc + 1) * 512],
                             start=True, stop=True).then_inc(pe.h, 1)
                    pe.n += 1

                pw = pe.n

                @blk.vector
                def _(v, h=h, tc=tc, pw=pw):
                    v.wait_ge(pe.h, pw)
                    sl = slice(h * NT + tc * 512, h * NT + tc * 512 + 512)
                    v.tensor_add(out=on[:, sl], in0=oloc[:, sl],
                                 in1=(psA if tc == 0 else psB)[:, :]).then_inc(ve.h, 1)
                    ve.n += 1

        d1.close()

        d2 = ExitStack()
        wo_f = d2.enter_context(nc.sbuf_tensor("wo_f", [128, 16 * 2048], BF16))

        @blk.gpsimd
        def _(g):
            g.wait_ge(pe.h, pe.n)
            g.wait_ge(ve.h, ve.n)
            g.wait_ge(lw.h, lw.n)
            g.dma_start(out=wo_f[:, :].rearrange("p (j f) -> p j f", j=16),
                        in_=wo1T3).then_inc(lw.h, 16)
            lw.n += 16

        lw_wo = lw.n

        for dj in range(16):
            for tc in range(2):
                @blk.tensor
                def _(t, dj=dj, tc=tc):
                    t.wait_ge(lw.h, lw_wo)
                    t.wait_ge(ve.h, ve.n)
                    mm = None
                    for e in range(16):
                        mm = t.matmul(psA[:, :],
                                      wo_f[:, e * 2048 + dj * 128: e * 2048 + dj * 128 + 128],
                                      on[:, e * NT + tc * 512: e * NT + tc * 512 + 512],
                                      start=(e == 0), stop=(e == 15))
                    mm.then_inc(pe.h, 1)
                    pe.n += 1

                pw = pe.n

                @blk.vector
                def _(v, dj=dj, tc=tc, pw=pw):
                    v.wait_ge(pe.h, pw)
                    v.tensor_copy(out=s1[:, dj * NT + tc * 512: dj * NT + tc * 512 + 512],
                                  in_=psA[:, :]).then_inc(ve.h, 1)
                    ve.n += 1

        d2.close()
        d12.close()

        d3 = ExitStack()
        x1 = d3.enter_context(nc.sbuf_tensor("x1", [128, 16 * NT], BF16))
        xn1 = d3.enter_context(nc.sbuf_tensor("xn1", [128, 16 * NT], BF16))
        h1 = d3.enter_context(nc.sbuf_tensor("h1", [128, 43 * 512], BF16))
        wg = [d3.enter_context(nc.sbuf_tensor(f"wg{i}", [128, 16 * 128], BF16)) for i in range(2)]
        wu = [d3.enter_context(nc.sbuf_tensor(f"wu{i}", [128, 16 * 128], BF16)) for i in range(2)]
        wd = [d3.enter_context(nc.sbuf_tensor(f"wd{i}", [128, 43 * 128], BF16)) for i in range(2)]
        t1b = d3.enter_context(nc.sbuf_tensor("t1b", [128, 512], BF16))
        gfb = d3.enter_context(nc.sbuf_tensor("gfb", [128, 512], BF16))
        x2f = d3.enter_context(nc.sbuf_tensor("x2f", [128, 512], F32))

        @blk.sync
        def _(s):
            s.wait_ge(pe.h, pe.n)
            s.dma_start(out=x1[:, :].rearrange("p (j t) -> p j t", j=16), in_=x1T3).then_inc(ld.h, 16)
            s.dma_start(out=xn1[:, :].rearrange("p (j t) -> p j t", j=16), in_=xn1T3).then_inc(ld.h, 16)
            ld.n += 32

        ld_d3 = ld.n

        for tg in range(2):
            for oc in range(43):
                bi = oc % 2

                @blk.sync
                def _(s, tg=tg, oc=oc, bi=bi):
                    s.wait_ge(ld.h, ld.n)
                    if tg > 0 or oc >= 2:
                        s.wait_ge(ve.h, ve.n)
                    if tg == 0 and oc == 0:
                        s.wait_ge(pe.h, pe.n)
                    s.dma_start(out=wg[bi][:, :].rearrange("p (j f) -> p j f", j=16),
                                in_=gwTf3[:, :, oc * 128:(oc + 1) * 128]).then_inc(ld.h, 16)
                    s.dma_start(out=wu[bi][:, :].rearrange("p (j f) -> p j f", j=16),
                                in_=uwTf3[:, :, oc * 128:(oc + 1) * 128]).then_inc(ld.h, 16)
                    ld.n += 32

                ldw = ld.n

                @blk.tensor
                def _(t, tg=tg, oc=oc, bi=bi, ldw=ldw):
                    t.wait_ge(ld.h, ldw)
                    t.wait_ge(se.h, se.n)
                    if tg == 0 and oc == 0:
                        t.wait_ge(ve.h, ve.n)
                    mm = None
                    for j in range(16):
                        mm = t.matmul(psA[:, :],
                                      wg[bi][:, j * 128:(j + 1) * 128],
                                      xn1[:, j * NT + tg * 512: j * NT + tg * 512 + 512],
                                      start=(j == 0), stop=(j == 15))
                    mm.then_inc(pe.h, 1)
                    pe.n += 1

                pwg = pe.n

                @blk.scalar
                def _(s, pwg=pwg):
                    s.wait_ge(pe.h, pwg)
                    s.wait_ge(ve.h, ve.n)
                    s.activation(out=t1b[:, :], in_=psA[:, :], func=AF.Silu,
                                 bias=zero_s[:, 0:1], scale=1.0).then_inc(se.h, 1)
                    se.n += 1

                swg = se.n

                @blk.tensor
                def _(t, tg=tg, oc=oc, bi=bi, ldw=ldw):
                    t.wait_ge(ld.h, ldw)
                    t.wait_ge(ve.h, ve.n)
                    mm = None
                    for j in range(16):
                        mm = t.matmul(psB[:, :],
                                      wu[bi][:, j * 128:(j + 1) * 128],
                                      xn1[:, j * NT + tg * 512: j * NT + tg * 512 + 512],
                                      start=(j == 0), stop=(j == 15))
                    mm.then_inc(pe.h, 1)
                    pe.n += 1

                pwu = pe.n

                @blk.vector
                def _(v, oc=oc, pwu=pwu, swg=swg):
                    v.wait_ge(pe.h, pwu)
                    v.wait_ge(se.h, swg)
                    v.tensor_mul(out=h1[:, oc * 512:(oc + 1) * 512],
                                 in0=t1b[:, :], in1=psB[:, :]).then_inc(ve.h, 1)
                    ve.n += 1

            for fo in range(16):
                bi = fo % 2

                @blk.gpsimd
                def _(g, tg=tg, fo=fo, bi=bi):
                    g.wait_ge(lw.h, lw.n)
                    if tg > 0 or fo >= 2:
                        g.wait_ge(pe.h, pe.n)
                    g.dma_start(out=wd[bi][:, :].rearrange("p (j f) -> p j f", j=43),
                                in_=dwTf3[:, :, fo * 128:(fo + 1) * 128]).then_inc(lw.h, 16)
                    lw.n += 16

                ldd = lw.n

                @blk.tensor
                def _(t, tg=tg, fo=fo, bi=bi, ldd=ldd):
                    t.wait_ge(lw.h, ldd)
                    t.wait_ge(se.h, se.n)
                    t.wait_ge(ve.h, ve.n)
                    mm = None
                    for kc in range(43):
                        mm = t.matmul(psA[:, :],
                                      wd[bi][:, kc * 128:(kc + 1) * 128],
                                      h1[:, kc * 512:(kc + 1) * 512],
                                      start=(kc == 0), stop=(kc == 42))
                    mm.then_inc(pe.h, 1)
                    pe.n += 1

                pwd = pe.n

                @blk.scalar
                def _(s, pwd=pwd):
                    s.wait_ge(pe.h, pwd)
                    s.wait_ge(ve.h, ve.n)
                    s.activation(out=gfb[:, :], in_=psA[:, :], func=AF.Silu,
                                 bias=zero_s[:, 0:1], scale=1.0).then_inc(se.h, 1)
                    se.n += 1

                swd = se.n

                @blk.vector
                def _(v, tg=tg, fo=fo, swd=swd):
                    v.wait_ge(se.h, swd)
                    v.wait_ge(so.h, so.n)
                    sl = slice(fo * NT + tg * 512, fo * NT + tg * 512 + 512)
                    v.tensor_mul(out=gfb[:, :], in0=gfb[:, :], in1=s1[:, sl]).then_inc(ve.h, 1)
                    ve.n += 1
                    v.wait_ge(ve.h, ve.n)
                    v.tensor_add(out=x2f[:, :], in0=gfb[:, :], in1=x1[:, sl]).then_inc(ve.h, 1)
                    ve.n += 1

                x2d = ve.n

                @blk.gpsimd
                def _(g, tg=tg, fo=fo, x2d=x2d):
                    g.wait_ge(ve.h, x2d)
                    g.dma_start(out=x2o3[:, fo, tg * 512:(tg + 1) * 512],
                                in_=x2f[:, :]).then_inc(so.h, 16)
                    so.n += 16

        d3.close()

        @blk.gpsimd
        def _(g):
            g.wait_ge(so.h, so.n)

    return nc


def prep_launch_d(inputs, b_outs, ghost):
    import ml_dtypes
    wo1T = np.ascontiguousarray(np.asarray(inputs["L1_attn_wo"], np.float32).T)
    lam = np.exp2(-8.0 * np.arange(1, 17) / 16.0).astype(np.float64)
    qdecg = np.zeros((128, 16 * NT), np.float32)
    p = np.arange(NT, dtype=np.float64)
    for h in range(16):
        qdecg[:, h * NT:(h + 1) * NT] = np.exp(-lam[h] * (p + 1.0))[None, :]
    qdecg = qdecg.astype(ml_dtypes.bfloat16)
    gwT_full, uwT_full, dwT_full = ghost
    in_maps = []
    for c in range(NCORE):
        b, r = c // 4, c % 4
        stp = np.zeros((128, 3 * 2048), np.float32)
        dfac = np.zeros((128, 3 * 16), np.float32)
        for j in range(r):
            src = b_outs[b * 4 + j]["st_out"]
            stp[:, j * 2048:(j + 1) * 2048] = src
            for h in range(16):
                dfac[:, j * 16 + h] = np.exp(-lam[h] * 1024.0 * (r - 1 - j))
        in_maps.append({
            "x1T": b_outs[c]["x1_out"], "xn1T": b_outs[c]["xn1_out"],
            "qrT": b_outs[c]["qr_out"], "olocT": b_outs[c]["oloc_out"],
            "stp": stp, "dfac": dfac, "qdecg": qdecg,
            "gwTf": gwT_full, "uwTf": uwT_full, "dwTf": dwT_full,
            "wo1T": wo1T,
        })
    return in_maps



_NC_CACHE = {}


def kernel(**inputs):
    if "A" not in _NC_CACHE:
        _NC_CACHE["A"] = build_launch_a()
        _NC_CACHE["C"] = build_launch_c()
        _NC_CACHE["B"] = build_launch_b()
        _NC_CACHE["D"] = build_launch_d()

    res_a = run_bass_kernel_spmd(_NC_CACHE["A"], prep_launch_a(inputs), core_ids=list(range(8)))
    res_c = run_bass_kernel_spmd(_NC_CACHE["C"], prep_launch_c(inputs), core_ids=list(range(8)))

    xn_full = [res_a.results[c]["xn_out"] for c in range(8)]
    s0_full = [res_a.results[c]["s0_out"] for c in range(8)]
    res_b = run_bass_kernel_spmd(_NC_CACHE["B"], prep_launch_b(inputs, xn_full, s0_full),
                                 core_ids=list(range(8)))

    gwT_full = np.concatenate([res_c.results[c]["gws"] for c in range(8)], axis=1)
    uwT_full = np.concatenate([res_c.results[c]["uws"] for c in range(8)], axis=1)
    dwT_full = np.concatenate([res_c.results[c]["dws"] for c in range(8)], axis=0)
    b_outs = [res_b.results[c] for c in range(8)]
    res_d = run_bass_kernel_spmd(_NC_CACHE["D"],
                                 prep_launch_d(inputs, b_outs, (gwT_full, uwT_full, dwT_full)),
                                 core_ids=list(range(8)))

    out = np.empty((B, S, DM), np.float32)
    for c in range(NCORE):
        b, r = c // 4, c % 4
        out[b, r * NT:(r + 1) * NT, :] = res_d.results[c]["x2o"].T
    return out
